# version 04
# baseline (speedup 1.0000x reference)
import numpy as np
import ml_dtypes
from contextlib import ExitStack

import concourse.bass as bass
import concourse.mybir as mybir
from concourse.bass_utils import run_bass_kernel_spmd

F32 = mybir.dt.float32
BF16 = mybir.dt.bfloat16
AF = mybir.ActivationFunctionType
ALU = mybir.AluOpType
AX = mybir.AxisListType

ENGINES = ("sync", "scalar", "vector", "tensor", "gpsimd")

D = 2048
NCD = 16
T = 1024
W = T + 1
NH = 16
HDIM = 128
FFH = 8192
NFH = FFH // 128
NEG = -1.0e9
LN_EPS = 1e-5
RMS_EPS = 1.1920929e-7
ATT_SCALE = 0.08838834764831845
PAIRS = [[0, 1], [2, 3], [4, 5], [6, 7]]


class Sem:
    __slots__ = ("h", "n")

    def __init__(self, h):
        self.h = h
        self.n = 0

    def inc(self, k=1):
        self.n += k
        return self.n


class Em:

    def __init__(self):
        self.q = {e: [] for e in ENGINES}
        self.dirty = {e: set() for e in ENGINES}
        self.rdirty = {e: set() for e in ENGINES}

    @staticmethod
    def _keys(aps):
        out = set()
        for a in aps:
            if a is None:
                continue
            s = a if isinstance(a, str) else a.name
            name, _, tag = s.partition(":")
            out.add((name, tag or None))
        return out

    @staticmethod
    def _conflict(ks, dirty):
        for n, t in ks:
            for dn, dt in dirty:
                if n == dn and (t is None or dt is None or t == dt):
                    return True
        return False

    def on(self, e, f, r=(), w=()):
        rk = self._keys(r)
        wk = self._keys(w)
        if (rk and self._conflict(rk, self.dirty[e])) or \
           (wk and (self._conflict(wk, self.rdirty[e]) or self._conflict(wk, self.dirty[e]))):
            self.q[e].append([lambda eng: eng.drain(), []])
            self.dirty[e] = set()
            self.rdirty[e] = set()
        self.q[e].append([f, []])
        self.dirty[e] |= wk
        self.rdirty[e] |= rk

    def drain(self, e):
        if self.dirty[e] or self.rdirty[e]:
            self.q[e].append([lambda eng: eng.drain(), []])
            self.dirty[e] = set()
            self.rdirty[e] = set()


class K:
    pass


def ew(k, eng, f, r=(), w=()):
    k.em.on(eng, f, r, w)


def tg(ap, tag):
    return f"{ap.name}:{tag}"


def wait(k, eng, sem, mark=None):
    v = sem.n if mark is None else mark
    if v > 0:
        k.em.on(eng, lambda e, s=sem.h, v=v: e.wait_ge(s, v))


def milestone(k, eng, sem):
    q = k.em.q[eng]
    if not q or q[-1][1]:
        k.em.q[eng].append([lambda e: e.drain(), []])
        k.em.dirty[eng] = set()
        v = sem.inc(1)
        k.em.on(eng, lambda e, s=sem.h: e.nop().then_inc(s, 1))
        return v
    prev = sem.n
    if prev > 0:
        q.insert(len(q) - 1, [lambda e, s=sem.h, p=prev: e.wait_ge(s, p), []])
    v = sem.inc(1)
    q[-1][1].append(sem.h)
    return v


SEGS_T = ((0, 512), (512, 512))



def emit_ln(k, src, dst, segs, sqbuf):
    stat, rb, mrb, ps = k.stat_scr, k.rb, k.mrb, k.ps_ln
    for si, (c0, wd) in enumerate(segs):
        ps_s = ps[:, 0, 0:wd]
        ps_q = ps[:, 1, 0:wd]
        for c in range(NCD):
            ew(k, "scalar", lambda e, c=c, c0=c0, wd=wd: e.activation(
                out=dst[:, c, c0:c0 + wd], in_=src[:, c, c0:c0 + wd], func=AF.Copy),
               r=[src], w=[dst])
        m_cast = milestone(k, "scalar", k.s_sc)
        wait(k, "vector", k.s_sc, m_cast)
        wait(k, "tensor", k.s_sc, m_cast)
        for c in range(NCD):
            ew(k, "vector", lambda e, c=c, c0=c0, wd=wd: e.tensor_tensor(
                out=sqbuf[:, c, 0:wd], in0=dst[:, c, c0:c0 + wd],
                in1=dst[:, c, c0:c0 + wd], op=ALU.mult), r=[dst], w=[sqbuf])
        m_sq = milestone(k, "vector", k.s_v)
        for c in range(NCD):
            ew(k, "tensor", lambda e, c=c, c0=c0, wd=wd, ps_s=ps_s: e.matmul(
                ps_s, k.ones, dst[:, c, c0:c0 + wd], start=(c == 0), stop=(c == NCD - 1)))
        wait(k, "tensor", k.s_v, m_sq)
        for c in range(NCD):
            ew(k, "tensor", lambda e, c=c, wd=wd, ps_q=ps_q: e.matmul(
                ps_q, k.ones, sqbuf[:, c, 0:wd], start=(c == 0), stop=(c == NCD - 1)))
        m_mm = milestone(k, "tensor", k.s_pe)
        wait(k, "vector", k.s_pe, m_mm)
        mean = stat[:, 0, 0:wd]
        var = stat[:, 1, 0:wd]
        msq = stat[:, 2, 0:wd]
        ew(k, "vector", lambda e, mean=mean, ps_s=ps_s: e.tensor_scalar_mul(
            out=mean, in0=ps_s, scalar1=1.0 / D), w=[stat])
        ew(k, "vector", lambda e, var=var, ps_q=ps_q: e.tensor_scalar(
            out=var, in0=ps_q, scalar1=1.0 / D, scalar2=LN_EPS, op0=ALU.mult, op1=ALU.add),
           w=[stat])
        ew(k, "vector", lambda e, mean=mean, msq=msq: e.tensor_tensor(
            out=msq, in0=mean, in1=mean, op=ALU.mult), r=[stat], w=[stat])
        ew(k, "vector", lambda e, var=var, msq=msq: e.tensor_tensor(
            out=var, in0=var, in1=msq, op=ALU.subtract), r=[stat], w=[stat])
        m_v1 = milestone(k, "vector", k.s_v)
        wait(k, "scalar", k.s_v, m_v1)
        ew(k, "scalar", lambda e, var=var: e.activation(out=var, in_=var, func=AF.Sqrt),
           r=[stat], w=[stat])
        m_s2 = milestone(k, "scalar", k.s_sc)
        wait(k, "vector", k.s_sc, m_s2)
        rbs = rb[:, si % 2, 0:wd]
        mrbs = mrb[:, si % 2, 0:wd]

        def _recip(e, var=var, rbs=rbs, nc_=k.nc):
            with nc_.allow_low_precision(reason="rstd broadcast tile stored bf16"):
                return e.reciprocal(out=rbs, in_=var)
        ew(k, "vector", _recip, r=[stat], w=[rb])
        ew(k, "vector", lambda e, mean=mean, rbs=rbs, mrbs=mrbs: e.tensor_tensor(
            out=mrbs, in0=mean, in1=rbs, op=ALU.mult), r=[stat, rb], w=[mrb])
        for c in range(NCD):
            ew(k, "vector", lambda e, c=c, c0=c0, wd=wd, rbs=rbs: e.tensor_tensor(
                out=dst[:, c, c0:c0 + wd], in0=dst[:, c, c0:c0 + wd], in1=rbs, op=ALU.mult),
               r=[tg(dst, c), rb], w=[tg(dst, c)])
        for c in range(NCD):
            ew(k, "vector", lambda e, c=c, c0=c0, wd=wd, mrbs=mrbs: e.tensor_tensor(
                out=dst[:, c, c0:c0 + wd], in0=dst[:, c, c0:c0 + wd], in1=mrbs,
                op=ALU.subtract), r=[tg(dst, c), mrb], w=[tg(dst, c)])
    milestone(k, "vector", k.s_v)



def emit_proj(k, n_oc, n_kc, w_src, rhs, evac, evac_eng, wslab, ps_pool, n_banks,
              segs=SEGS_T, w_queue="gpsimd", w_sems=None, load_wait=None):
    w_sems = w_sems or (k.s_w, k.s_w2)
    evac_sem = k.s_v if evac_eng == "vector" else k.s_sc
    nseg = len(segs)
    evac_marks = {}
    load_marks = {}
    mm_marks = {}

    def emit_load(oc):
        if oc >= 2:
            wait(k, w_queue, k.s_pe, mm_marks[oc - 2])
        else:
            wait(k, w_queue, k.s_pe)
            if load_wait is not None:
                wait(k, w_queue, load_wait[0], load_wait[1])
        ld = w_sems[oc % 2].inc(16)
        ew(k, w_queue, lambda e, oc=oc, h=w_sems[oc % 2].h: e.dma_start(
            out=wslab[:, oc % 2, :, :].rearrange("p a b -> p (a b)"),
            in_=w_src(oc)).then_inc(h, 16))
        load_marks[oc] = ld

    emit_load(0)
    if n_oc > 1:
        emit_load(1)
    for oc in range(n_oc):
        wait(k, "tensor", w_sems[oc % 2], load_marks[oc])
        if oc >= max(1, n_banks // nseg):
            wait(k, "tensor", evac_sem, evac_marks[oc - max(1, n_banks // nseg)])
        elif oc == 0:
            wait(k, "tensor", k.s_v)
            wait(k, "tensor", k.s_sc)
        for dc in range(n_kc):
            for si, (c0, wd) in enumerate(segs):
                pb = (nseg * oc + si) % n_banks
                ew(k, "tensor", lambda e, oc=oc, dc=dc, c0=c0, wd=wd, pb=pb: e.matmul(
                    ps_pool[:, pb, 0:wd], wslab[:, oc % 2, dc, :], rhs(dc, c0, wd),
                    start=(dc == 0), stop=(dc == n_kc - 1)))
        mm = milestone(k, "tensor", k.s_pe)
        mm_marks[oc] = mm
        if oc + 2 < n_oc:
            emit_load(oc + 2)
        wait(k, evac_eng, k.s_pe, mm)
        for si, (c0, wd) in enumerate(segs):
            pb = (nseg * oc + si) % n_banks
            evac(oc, si, c0, wd, ps_pool[:, pb, 0:wd])
        evac_marks[oc] = milestone(k, evac_eng, evac_sem)
    return mm_marks





def emit_phase0(k, wgu_slab, wproj_slab, w2_slab, cc_w_in, WP_OFF,
                cc_w2_in, w1_sl, w3_sl, W13_OFF):
    nc = k.nc
    es0 = ExitStack()
    ld = es0.enter_context(nc.sbuf_tensor("p0_ld", [128, 2, FFH], BF16))[:]
    m1 = es0.enter_context(nc.sbuf_tensor("p0_m1", [128, FFH], BF16))[:]
    m2 = es0.enter_context(nc.sbuf_tensor("p0_m2", [128, FFH], BF16))[:]
    st_ = es0.enter_context(nc.sbuf_tensor("p0_st", [128, 64, 128], BF16))[:]
    qs = es0.enter_context(nc.sbuf_tensor("p0_qs", [128, 3, 128], F32))[:]
    sbf = es0.enter_context(nc.sbuf_tensor("p0_sbf", [128, 128], BF16))[:]
    pp = es0.enter_context(nc.sbuf_tensor("p0_pp", [128, 2, FFH], F32))[:]
    pt = es0.enter_context(nc.psum_tensor("p0_pt", [128, 2, 1024], BF16))[:]
    in_cc = cc_w_in.ap()
    in_cc2 = cc_w2_in.ap()
    last_stage = [0]

    def quant(par, ncols):
        ng = ncols // 64
        lv = ld[:, par, 0:ncols].rearrange("p (g e) -> p g e", e=64)
        gb = lambda a: a.rearrange("p (g o) -> p g o", o=1).broadcast_to([128, ng, 64])
        ew(k, "vector", lambda e, lv=lv, ng=ng: e.reduce_sum(
            out=qs[:, 0, 0:ng].rearrange("p (g o) -> p g o", o=1), in_=lv,
            axis=AX.X, apply_absolute_value=True), r=[tg(ld, par)], w=[qs])
        ew(k, "vector", lambda e, ng=ng: e.tensor_scalar(
            out=qs[:, 0, 0:ng], in0=qs[:, 0, 0:ng], scalar1=1.0 / 64, scalar2=1e-8,
            op0=ALU.mult, op1=ALU.max), r=[qs], w=[qs])
        ew(k, "vector", lambda e, ng=ng: e.tensor_copy(
            out=sbf[:, 0:ng], in_=qs[:, 0, 0:ng]), r=[qs], w=[sbf])

        def _rcp(e, ng=ng, nc_=k.nc):
            with nc_.allow_low_precision(reason="group scale reciprocal"):
                e.reciprocal(out=qs[:, 1, 0:ng], in_=sbf[:, 0:ng])
        ew(k, "vector", _rcp, r=[sbf], w=[qs])
        m1v = m1[:, 0:ncols].rearrange("p (g e) -> p g e", e=64)
        m2v = m2[:, 0:ncols].rearrange("p (g e) -> p g e", e=64)
        ew(k, "vector", lambda e, lv=lv, m1v=m1v, ng=ng: e.tensor_tensor(
            out=m1v, in0=lv, in1=gb(qs[:, 1, 0:ng]), op=ALU.mult),
           r=[tg(ld, par), qs], w=[m1])
        ew(k, "vector", lambda e, ncols=ncols: e.tensor_single_scalar(
            out=m2[:, 0:ncols], in_=m1[:, 0:ncols], scalar=0.5, op=ALU.is_gt),
           r=[m1], w=[m2])
        ew(k, "vector", lambda e, ncols=ncols: e.tensor_single_scalar(
            out=m1[:, 0:ncols], in_=m1[:, 0:ncols], scalar=-0.5, op=ALU.is_lt),
           r=[m1], w=[m1])
        ew(k, "vector", lambda e, ncols=ncols: e.tensor_tensor(
            out=m1[:, 0:ncols], in0=m2[:, 0:ncols], in1=m1[:, 0:ncols], op=ALU.subtract),
           r=[m1, m2], w=[m1])
        ew(k, "vector", lambda e, m1v=m1v, ng=ng: e.tensor_tensor(
            out=m1v, in0=m1v, in1=gb(sbf[:, 0:ng]), op=ALU.mult),
           r=[m1, sbf], w=[m1])
        return milestone(k, "vector", k.s_v)

    def transpose_stage(ntiles, off, m_src, dst_cc=None):
        dst_cc = in_cc if dst_cc is None else dst_cc
        wait(k, "tensor", k.s_v, m_src)
        wait(k, "vector", k.s_g, last_stage[0])
        ev = {}
        for g0 in range(0, ntiles, 8):
            gcnt = min(8, ntiles - g0)
            bank = (g0 // 8) % 2
            if g0 >= 16:
                wait(k, "tensor", k.s_v, ev[g0 - 16])
            for t in range(gcnt):
                cc = g0 + t
                ew(k, "tensor", lambda e, cc=cc, bank=bank, t=t: e.transpose(
                    pt[:, bank, t * 128:(t + 1) * 128],
                    m1[:, cc * 128:(cc + 1) * 128], k.ident))
            mk = milestone(k, "tensor", k.s_pe)
            wait(k, "vector", k.s_pe, mk)
            ew(k, "vector", lambda e, g0=g0, bank=bank, gcnt=gcnt: e.tensor_copy(
                out=st_[:, g0:g0 + gcnt, :].rearrange("p a b -> p (a b)"),
                in_=pt[:, bank, 0:gcnt * 128]), w=[st_])
            ev[g0] = milestone(k, "vector", k.s_v)
        wait(k, "gpsimd", k.s_v, ev[(ntiles - 1) // 8 * 8])
        sz = ntiles * 128 * 128
        last_stage[0] = k.s_g.inc(16)
        ew(k, "gpsimd", lambda e, off=off, sz=sz, ntiles=ntiles, dst_cc=dst_cc,
           h=k.s_g.h: e.dma_start(
            out=dst_cc[off:off + sz].rearrange("(p m) -> p m", p=128),
            in_=st_[:, 0:ntiles, :].rearrange("p a b -> p (a b)")).then_inc(h, 16))
        return milestone(k, "tensor", k.s_pe)

    qm = {}
    tm = {}
    for l in range(16):
        par = l % 2
        if l >= 2:
            wait(k, "gpsimd", k.s_v, qm[l - 2])
        sem = (k.s_w, k.s_w2)[par]
        lm = sem.inc(16)
        ew(k, "gpsimd", lambda e, l=l, par=par, h=sem.h: e.dma_start(
            out=ld[:, par, 0:2048], in_=wgu_slab[l * 128:(l + 1) * 128, :]).then_inc(h, 16))
        wait(k, "vector", sem, lm)
        if l >= 1:
            wait(k, "vector", k.s_pe, tm[l - 1])
        qm[l] = quant(par, 2048)
        tm[l] = transpose_stage(16, l * 262144, qm[l])

    for l in range(2):
        par = l % 2
        sem = (k.s_w, k.s_w2)[par]
        wait(k, "gpsimd", k.s_v, qm[14 + l] if l else qm[14])
        lm = sem.inc(16)
        ew(k, "gpsimd", lambda e, l=l, par=par, h=sem.h: e.dma_start(
            out=ld[:, par, :], in_=wproj_slab[l * 128:(l + 1) * 128, :]).then_inc(h, 16))
        wait(k, "vector", sem, lm)
        wait(k, "vector", k.s_pe, tm[15 + l] if l else tm[15])
        qm[16 + l] = quant(par, FFH)
        tm[16 + l] = transpose_stage(64, WP_OFF + l * 1048576, qm[16 + l])

    SCL = float(FFH) ** -0.5
    bm = {}
    for l in range(2):
        lm = k.s_k.inc(16)
        if l == 1:
            wait(k, "sync", k.s_v, bm[0])
        ew(k, "sync", lambda e, l=l, h=k.s_k.h: e.dma_start(
            out=pp[:, 0, :], in_=w2_slab[l * 128:(l + 1) * 128, :]).then_inc(h, 16))
        wait(k, "vector", k.s_k, lm)
        par = 0
        for s in range(13):
            h2 = 1 << s
            sv = pp[:, par, :].rearrange("p (nb two h) -> p nb two h", two=2, h=h2)
            dv = pp[:, 1 - par, :].rearrange("p (nb two h) -> p nb two h", two=2, h=h2)
            ew(k, "vector", lambda e, sv=sv, dv=dv: e.tensor_tensor(
                out=dv[:, :, 0, :], in0=sv[:, :, 0, :], in1=sv[:, :, 1, :], op=ALU.add),
               r=[tg(pp, par)], w=[tg(pp, 1 - par)])
            ew(k, "vector", lambda e, sv=sv, dv=dv: e.tensor_tensor(
                out=dv[:, :, 1, :], in0=sv[:, :, 0, :], in1=sv[:, :, 1, :], op=ALU.subtract),
               r=[tg(pp, par)], w=[tg(pp, 1 - par)])
            par = 1 - par
        wait(k, "vector", k.s_pe, tm[17 + l])
        ew(k, "vector", lambda e, par=par: e.tensor_scalar_mul(
            out=m1[:, :], in0=pp[:, par, :], scalar1=SCL), r=[tg(pp, par)], w=[m1])
        bm[l] = milestone(k, "vector", k.s_v)
        tm[18 + l] = transpose_stage(64, l * 1048576, bm[l], dst_cc=in_cc2)

    for j in range(8):
        for which, srcap in ((0, w1_sl), (1, w3_sl)):
            off = W13_OFF + (which * 8 + j) * 262144
            k.s_g.inc(16)
            ew(k, "gpsimd", lambda e, j=j, srcap=srcap, off=off, h=k.s_g.h: e.dma_start(
                out=in_cc2[off:off + 262144].rearrange("(p m) -> p m", p=128),
                in_=srcap[j]).then_inc(h, 16))

    wait(k, "gpsimd", k.s_g)
    k.m_ccw = k.s_cc.inc(1)
    ew(k, "gpsimd", lambda e, h=k.s_cc.h: e.collective_compute(
        "AllGather", ALU.bypass, replica_groups=[list(range(8))],
        ins=[cc_w_in.ap().opt()], outs=[k.cc_w_out_ap.opt()]).then_inc(h, 1))
    k.m_ccw2 = k.s_cc.inc(1)
    ew(k, "gpsimd", lambda e, h=k.s_cc.h: e.collective_compute(
        "AllGather", ALU.bypass, replica_groups=[list(range(8))],
        ins=[cc_w2_in.ap().opt()], outs=[k.cc_w2_out_ap.opt()]).then_inc(h, 1))
    es0.close()



def emit_ffn0(k, xres, xhat, bufA_t, bufB_t, bufC, bufC_t, wslab, ps_mm, ps_r,
              WP_OFF, RANK_SZ):
    occ = k.cc_w_out_ap

    def wgu_slab_ap(hc):
        r, l = divmod(hc, 16)
        off = r * RANK_SZ + l * 262144
        return occ[off:off + 262144].rearrange("(p m) -> p m", p=128)

    def wp_slab_ap(dc):
        r, l = divmod(dc, 2)
        off = r * RANK_SZ + WP_OFF + l * 1048576
        return occ[off:off + 1048576].rearrange("(p m) -> p m", p=128)

    hA = bufA_t[:, 0:16384].rearrange("p (a b) -> p a b", b=512)
    hB = bufB_t[:, 0:16384].rearrange("p (a b) -> p a b", b=512)

    def hbuf(fc):
        return (hA if fc < 32 else hB)[:, fc % 32, :]

    def htag(fc):
        return tg(bufA_t if fc < 32 else bufB_t, fc % 32)

    gslab = bufC_t[:, 0:8192].rearrange("p (a b c) -> p a b c", b=16, c=128)
    sqt = bufC_t[:, 8192:10240].rearrange("p (a b) -> p a b", a=4)
    sg = bufC_t[:, 10240:11264].rearrange("p (a b) -> p a b", a=2)
    tstat = bufC_t[:, 11264:12288].bitcast(F32)
    rsb = bufC_t[:, 12288:13312].rearrange("p (a b) -> p a b", a=2)
    pslab = bufC_t[:, 0:16384].rearrange("p (a b c) -> p a b c", b=64, c=128)
    ksl = (k.s_k, k.s_k2, k.s_k3, k.s_k4)

    wait(k, "scalar", k.s_v)
    wait(k, "vector", k.s_pe)
    emit_ln(k, xres, xhat, ((1, 512), (513, 512)), bufC)

    for th in range(2):
        cx = 1 + th * 512
        wait(k, "sync", k.s_pe)
        wait(k, "vector", k.s_pe)
        wait(k, "scalar", k.s_pe)
        wait(k, "tensor", k.s_v)
        wait(k, "tensor", k.s_sc)
        if th == 0:
            wait(k, "sync", k.s_cc, k.m_ccw)
        load_marks = {}

        def gu_load(fc, which):
            slot = (2 * fc + which) % 4
            sem = ksl[slot]
            lm = sem.inc(16)
            ew(k, "sync", lambda e, fc=fc, which=which, slot=slot, h=sem.h: e.dma_start(
                out=gslab[:, slot, :, :].rearrange("p a b -> p (a b)"),
                in_=wgu_slab_ap(fc + 64 * which)).then_inc(h, 16))
            load_marks[(fc, which)] = lm

        gu_load(0, 0)
        gu_load(0, 1)
        gu_load(1, 0)
        gu_load(1, 1)
        gmark = {}
        umark = {}
        hmark = {}
        smark = {}
        sqmark = {}
        ssmark = {}
        for fc in range(64):
            slot_g = (2 * fc) % 4
            slot_u = (2 * fc + 1) % 4
            wait(k, "tensor", ksl[slot_g], load_marks[(fc, 0)])
            if fc >= 2:
                wait(k, "tensor", k.s_sc, smark[fc - 2])
            for dc in range(NCD):
                ew(k, "tensor", lambda e, dc=dc, slot_g=slot_g, cx=cx: e.matmul(
                    ps_mm[:, slot_g % 4, :], gslab[:, slot_g, dc, :],
                    xhat[:, dc, cx:cx + 512], start=(dc == 0), stop=(dc == NCD - 1)))
            gmark[fc] = milestone(k, "tensor", k.s_pe)
            wait(k, "tensor", ksl[slot_u], load_marks[(fc, 1)])
            if fc >= 2:
                wait(k, "tensor", k.s_v, hmark[fc - 2])
            for dc in range(NCD):
                ew(k, "tensor", lambda e, dc=dc, slot_u=slot_u, cx=cx: e.matmul(
                    ps_mm[:, slot_u % 4, :], gslab[:, slot_u, dc, :],
                    xhat[:, dc, cx:cx + 512], start=(dc == 0), stop=(dc == NCD - 1)))
            umark[fc] = milestone(k, "tensor", k.s_pe)
            if fc >= 2:
                f2 = fc - 2
                wait(k, "tensor", k.s_sc, sqmark[f2])
                ew(k, "tensor", lambda e, f2=f2: e.matmul(
                    ps_r[:, 0, :], k.ones, sqt[:, f2 % 4, :],
                    start=(f2 == 0), stop=False))
                ssmark[f2] = milestone(k, "tensor", k.s_pe)
            if fc + 2 < 64:
                wait(k, "sync", k.s_pe, gmark[fc])
                gu_load(fc + 2, 0)
                wait(k, "sync", k.s_pe, umark[fc])
                gu_load(fc + 2, 1)
            wait(k, "scalar", k.s_pe, gmark[fc])
            if fc >= 2:
                wait(k, "scalar", k.s_v, hmark[fc - 2])
            ew(k, "scalar", lambda e, fc=fc, slot_g=slot_g: e.activation(
                out=sg[:, fc % 2, :], in_=ps_mm[:, slot_g % 4, :], func=AF.Silu),
               w=[f"bufC:sg{fc % 2}"])
            smark[fc] = milestone(k, "scalar", k.s_sc)
            wait(k, "vector", k.s_pe, umark[fc])
            wait(k, "vector", k.s_sc, smark[fc])
            ew(k, "vector", lambda e, fc=fc, slot_u=slot_u: e.tensor_tensor(
                out=hbuf(fc), in0=sg[:, fc % 2, :], in1=ps_mm[:, slot_u % 4, :],
                op=ALU.mult), r=[f"bufC:sg{fc % 2}"], w=[htag(fc)])
            hmark[fc] = milestone(k, "vector", k.s_v)
            if fc >= 4:
                wait(k, "scalar", k.s_pe, ssmark[fc - 4])
            wait(k, "scalar", k.s_v, hmark[fc])
            ew(k, "scalar", lambda e, fc=fc: e.activation(
                out=sqt[:, fc % 4, :], in_=hbuf(fc), func=AF.Square),
               w=[f"bufC:sq{fc % 4}"])
            sqmark[fc] = milestone(k, "scalar", k.s_sc)
        for f2 in (62, 63):
            wait(k, "tensor", k.s_sc, sqmark[f2])
            ew(k, "tensor", lambda e, f2=f2: e.matmul(
                ps_r[:, 0, :], k.ones, sqt[:, f2 % 4, :], start=False, stop=(f2 == 63)))
        mss = milestone(k, "tensor", k.s_pe)
        wait(k, "vector", k.s_pe, mss)
        wait(k, "vector", k.s_sc)
        ew(k, "vector", lambda e: e.tensor_scalar(
            out=tstat[:, 0:512], in0=ps_r[:, 0, :], scalar1=1.0 / FFH, scalar2=RMS_EPS,
            op0=ALU.mult, op1=ALU.add), w=["bufC:ts"])
        mvst = milestone(k, "vector", k.s_v)
        wait(k, "scalar", k.s_v, mvst)
        ew(k, "scalar", lambda e: e.activation(
            out=tstat[:, 0:512], in_=tstat[:, 0:512], func=AF.Sqrt),
           r=["bufC:ts"], w=["bufC:ts"])
        msqr = milestone(k, "scalar", k.s_sc)
        wait(k, "vector", k.s_sc, msqr)

        def _recipr(e, th=th, nc_=k.nc):
            with nc_.allow_low_precision(reason="rms rstd tile stored bf16"):
                return e.reciprocal(out=rsb[:, th, :], in_=tstat[:, 0:512])
        ew(k, "vector", _recipr, r=["bufC:ts"], w=["bufC:rs"])
        for fc in range(64):
            ew(k, "vector", lambda e, fc=fc, th=th: e.tensor_tensor(
                out=hbuf(fc), in0=hbuf(fc), in1=rsb[:, th, :], op=ALU.mult),
               r=[htag(fc), "bufC:rs"], w=[htag(fc)])
        mhn = milestone(k, "vector", k.s_v)
        wait(k, "tensor", k.s_v, mhn)

        def evac_ffn(oc, si, c0_, wd, ps, th=th, cx=cx):
            ew(k, "vector", lambda e, oc=oc, ps=ps, cx=cx: e.tensor_tensor(
                out=xres[:, oc, cx:cx + 512], in0=xres[:, oc, cx:cx + 512],
                in1=ps, op=ALU.add),
               r=[tg(xres, f"f{oc}.{th}")], w=[tg(xres, f"f{oc}.{th}")])

        emit_proj(k, 16, 64, wp_slab_ap, lambda dc, c0_, wd: hbuf(dc),
                  evac_ffn, "vector", pslab, ps_mm, 4, segs=((0, 512),),
                  w_queue="sync", w_sems=(k.s_k, k.s_k2), load_wait=(k.s_v, mhn))





def emit_l1attn(k, xres, xhat, bufA, bufB, bufC, bufC_t, wslab, ps_mm,
                wq_t, wk_t, wv_t, wo_t, ropec_d, band_d, killv, cc_kv_in, cc_kv_out):
    nc = k.nc
    ropec = bufC_t[:, 14016:16064].rearrange("p (a b) -> p a b", a=2)
    band = bufC_t[:, 12864:14016]
    KHD = 131072
    kin = cc_kv_in.ap()
    kout = cc_kv_out.ap()

    def kv_in_K(h):
        return kin[h * KHD:(h + 1) * KHD].rearrange("(p m) -> p m", p=128)

    def kv_in_V(h):
        off = 16 * KHD + h * KHD
        return kin[off:off + KHD].rearrange("(p m) -> p m", p=128)

    def kv_out_K(h):
        return kout[h * KHD:(h + 1) * KHD].rearrange("(p m) -> p m", p=128)

    def kv_out_V(h):
        off = 16 * KHD + h * KHD
        return kout[off:off + KHD].rearrange("(p m) -> p m", p=128)

    ps_s = ps_mm[:, 0:3, :].rearrange("p a b -> p (a b)")
    ps_pt = ps_mm[:, 3, :].bitcast(BF16)
    ps_pt2 = k.ps_ln[:, 0, 0:64].bitcast(BF16)
    ps_ot = [k.ps_ln[:, 1, 0:128], k.ps_r[:, 0, 0:128]]

    wait(k, "scalar", k.s_v)
    wait(k, "scalar", k.s_pe)
    wait(k, "vector", k.s_pe)
    emit_ln(k, xres, xhat, ((1, 512), (513, 512)), bufC)

    def proj_rhs_x(dc, c0, wd):
        return xhat[:, dc, 1 + c0:1 + c0 + wd]

    def evac_copy_to(buf):
        def f(oc, si, c0, wd, ps):
            ew(k, "vector", lambda e, oc=oc, c0=c0, wd=wd, ps=ps: e.tensor_copy(
                out=buf[:, oc, c0:c0 + wd], in_=ps), w=[tg(buf, f"a{oc}")])
        return f

    def emit_rope(buf, h):
        wait(k, "tensor", k.s_v)
        for s2 in range(2):
            ew(k, "tensor", lambda e, h=h, s2=s2, buf=buf: e.matmul(
                ps_mm[:, 2 + s2, :], k.swap, buf[:, h, s2 * 512:(s2 + 1) * 512],
                start=True, stop=True))
        mpe = milestone(k, "tensor", k.s_pe)
        ew(k, "vector", lambda e, h=h, buf=buf: e.tensor_tensor(
            out=bufC[:, 0, 0:T], in0=buf[:, h, 0:T], in1=ropec[:, 0, :], op=ALU.mult),
           r=[tg(buf, f"a{h}")], w=[tg(bufC_t, "rt0")])
        wait(k, "vector", k.s_pe, mpe)
        for s2 in range(2):
            ew(k, "vector", lambda e, s2=s2: e.tensor_tensor(
                out=bufC[:, 1, s2 * 512:(s2 + 1) * 512], in0=ps_mm[:, 2 + s2, :],
                in1=ropec[:, 1, s2 * 512:(s2 + 1) * 512], op=ALU.mult),
               w=[tg(bufC_t, "rt1")])
        ew(k, "vector", lambda e, h=h, buf=buf: e.tensor_tensor(
            out=buf[:, h, 0:T], in0=bufC[:, 0, 0:T], in1=bufC[:, 1, 0:T], op=ALU.add),
           r=[tg(bufC_t, "rt0"), tg(bufC_t, "rt1")], w=[tg(buf, f"a{h}")])
        return milestone(k, "vector", k.s_v)

    wait(k, "sync", k.s_pe)
    m_rc1 = k.s_ld.inc(16)
    ew(k, "sync", lambda e, h=k.s_ld.h: e.dma_start(
        out=ropec, in_=ropec_d).then_inc(h, 16))
    wait(k, "vector", k.s_ld, m_rc1)
    wait(k, "tensor", k.s_v)
    emit_proj(k, 16, 16, lambda oc: wk_t[oc], proj_rhs_x,
              evac_copy_to(bufB), "vector", wslab, ps_mm, 2)
    for h in range(NH):
        mr = emit_rope(bufB, h)
        k.s_g.inc(16)
        wait(k, "gpsimd", k.s_v, mr)
        ew(k, "gpsimd", lambda e, h=h, hh=k.s_g.h: e.dma_start(
            out=kv_in_K(h), in_=bufB[:, h, 0:T]).then_inc(hh, 16))

    wait(k, "vector", k.s_pe)
    emit_proj(k, 16, 16, lambda oc: wv_t[oc], proj_rhs_x,
              evac_copy_to(bufC), "vector", wslab, ps_mm, 2)
    ps_vt = ps_mm[:, 2:4, :].rearrange("p a b -> p (a b)").bitcast(BF16)
    wait(k, "tensor", k.s_v)
    mv_prev = None
    for h in range(NH):
        if mv_prev is not None:
            wait(k, "tensor", k.s_v, mv_prev)
        for t8 in range(8):
            ew(k, "tensor", lambda e, h=h, t8=t8: e.transpose(
                ps_vt[:, t8 * 128:(t8 + 1) * 128],
                bufC[:, h, t8 * 128:(t8 + 1) * 128], k.ident))
        mpe = milestone(k, "tensor", k.s_pe)
        wait(k, "vector", k.s_pe, mpe)
        if h > 0:
            wait(k, "vector", k.s_g)
        ew(k, "vector", lambda e, h=h: e.tensor_copy(
            out=bufC[:, h, 0:T], in_=ps_vt[:, 0:1024]), w=[tg(bufC_t, f"v{h}")])
        mv = milestone(k, "vector", k.s_v)
        mv_prev = mv
        k.s_g.inc(16)
        wait(k, "gpsimd", k.s_v, mv)
        ew(k, "gpsimd", lambda e, h=h, hh=k.s_g.h: e.dma_start(
            out=kv_in_V(h), in_=bufC[:, h, 0:T]).then_inc(hh, 16))

    wait(k, "gpsimd", k.s_g)
    m_kv = k.s_cc.inc(1)
    ew(k, "gpsimd", lambda e, hh=k.s_cc.h: e.collective_compute(
        "AllGather", ALU.bypass, replica_groups=PAIRS,
        ins=[cc_kv_in.ap().opt()], outs=[cc_kv_out.ap().opt()]).then_inc(hh, 1))

    emit_proj(k, 16, 16, lambda oc: wq_t[oc], proj_rhs_x,
              evac_copy_to(bufA), "vector", wslab, ps_mm, 2)
    wait(k, "sync", k.s_g)
    m_rc2 = k.s_ld.inc(32)
    ew(k, "sync", lambda e, h=k.s_ld.h: e.dma_start(
        out=ropec, in_=ropec_d).then_inc(h, 16))
    ew(k, "sync", lambda e, h=k.s_ld.h: e.dma_start(
        out=band[:, :], in_=band_d).then_inc(h, 16))
    wait(k, "vector", k.s_ld, m_rc2)
    wait(k, "vector", k.s_g)
    rope_q_marks = {}
    for h in range(NH):
        rope_q_marks[h] = emit_rope(bufA, h)

    kbuf = bufC_t[:, 0:4096].rearrange("p (s m) -> p s m", s=2)
    vbuf = bufC_t[:, 4096:8192].rearrange("p (s a c) -> p s a c", s=2, a=16)
    Pt = bufC_t[:, 8192:10496].rearrange("p (s m) -> p s m", s=2)
    ptile = bufC_t[:, 10496:12800].rearrange("p (s m) -> p s m", s=2)
    smal = bufC_t[:, 12800:12864].bitcast(F32)

    wait(k, "vector", k.s_pe)
    wait(k, "sync", k.s_cc, m_kv)
    wait(k, "sync", k.s_v)
    wait(k, "sync", k.s_g)

    NIT = NH * 8
    kvsem = (k.s_k3, k.s_k4)
    kvload = {}

    def kv_load(h):
        sl = h % 2
        sem = kvsem[sl]
        ew(k, "sync", lambda e, h=h, sl=sl, hh=sem.h: e.dma_start(
            out=kbuf[:, sl, 0:1024], in_=kv_out_K(h)).then_inc(hh, 16))
        ew(k, "sync", lambda e, h=h, sl=sl, hh=sem.h: e.dma_start(
            out=kbuf[:, sl, 1024:2048], in_=bufB[:, h, 0:T]).then_inc(hh, 16))
        ew(k, "sync", lambda e, h=h, sl=sl, hh=sem.h: e.dma_start(
            out=vbuf[:, sl, 0:8, :].rearrange("p a c -> p (a c)"),
            in_=kv_out_V(h)).then_inc(hh, 16))
        ew(k, "sync", lambda e, h=h, sl=sl, hh=sem.h: e.dma_start(
            out=vbuf[:, sl, 8:16, :].rearrange("p a c -> p (a c)"),
            in_=kv_in_V(h)).then_inc(hh, 16))
        kvload[h] = sem.inc(64)

    kv_load(0)
    kv_load(1)

    ps_marks = {}
    mx_marks = {}
    exp_marks = {}
    pn_marks = {}
    pt_marks = {}
    pte_marks = {}
    av_marks = {}
    oe_marks = {}

    for i in range(NIT + 2):
        h, qt = divmod(i, 8)
        if i < NIT:
            sl = h % 2
            if qt == 0:
                wait(k, "tensor", kvsem[sl], kvload[h])
                wait(k, "tensor", k.s_v, rope_q_marks[h])
            if i >= 1:
                wait(k, "tensor", k.s_sc, exp_marks[i - 1])
            ws = qt * 128
            ew(k, "tensor", lambda e, h=h, qt=qt, sl=sl, ws=ws: e.matmul(
                ps_s[:, 0:512], bufA[:, h, qt * 128:(qt + 1) * 128],
                kbuf[:, sl, ws:ws + 512], start=True, stop=False))
            ew(k, "tensor", lambda e, h=h, qt=qt, sl=sl, ws=ws: e.matmul(
                ps_s[:, 512:1024], bufA[:, h, qt * 128:(qt + 1) * 128],
                kbuf[:, sl, ws + 512:ws + 1024], start=True, stop=(qt >= 4)))
            ew(k, "tensor", lambda e, h=h, qt=qt, sl=sl, ws=ws: e.matmul(
                ps_s[:, 1024:1152], bufA[:, h, qt * 128:(qt + 1) * 128],
                kbuf[:, sl, ws + 1024:ws + 1152], start=True, stop=False))
            ew(k, "tensor", lambda e: e.matmul(
                ps_s[:, 0:512], k.ident, band[:, 0:512], start=False, stop=False))
            ew(k, "tensor", lambda e: e.matmul(
                ps_s[:, 1024:1152], k.ident, band[:, 1024:1152], start=False, stop=True))
            ew(k, "tensor", lambda e, ws=ws: e.matmul(
                ps_s[:, 0:512], k.ones[0:1, :], killv[0:1, ws:ws + 512],
                start=False, stop=True))
            if qt < 4:
                ew(k, "tensor", lambda e, ws=ws: e.matmul(
                    ps_s[:, 512:1024], k.ones[0:1, :], killv[0:1, ws + 512:ws + 1024],
                    start=False, stop=True))
            ps_marks[i] = milestone(k, "tensor", k.s_pe)
            if qt == 2 and 2 <= h + 1 < NH:
                wait(k, "sync", k.s_pe, av_marks[(h - 1) * 8 + 7])
                kv_load(h + 1)
        j = i - 1
        if 0 <= j < NIT:
            wait(k, "tensor", k.s_sc, pn_marks[j])
            if j >= 1:
                wait(k, "tensor", k.s_v, pte_marks[j - 1])
            for t9 in range(8):
                ew(k, "tensor", lambda e, j=j, t9=t9: e.transpose(
                    ps_pt[:, t9 * 128:(t9 + 1) * 128],
                    Pt[:, j % 2, t9 * 128:(t9 + 1) * 128], k.ident))
            ew(k, "tensor", lambda e, j=j: e.transpose(
                ps_pt2[:, 0:128], Pt[:, j % 2, 1024:1152], k.ident))
            pt_marks[j] = milestone(k, "tensor", k.s_pe)
        m = i - 2
        if m >= 0:
            hm, qtm = divmod(m, 8)
            slm = hm % 2
            wait(k, "tensor", k.s_v, pte_marks[m])
            if m >= 2:
                wait(k, "tensor", k.s_v, oe_marks[m - 2])
            for t9 in range(9):
                ew(k, "tensor", lambda e, m=m, t9=t9, slm=slm, qtm=qtm: e.matmul(
                    ps_ot[m % 2], vbuf[:, slm, qtm + t9, :],
                    ptile[:, m % 2, t9 * 128:(t9 + 1) * 128],
                    start=(t9 == 0), stop=(t9 == 8)))
            av_marks[m] = milestone(k, "tensor", k.s_pe)
        if i < NIT:
            c0s = (i % 2) * 8
            wait(k, "vector", k.s_pe, ps_marks[i])
            ew(k, "vector", lambda e, c0s=c0s: e.reduce_max(
                out=smal[:, c0s:c0s + 1], in_=ps_s[:, 0:1152], axis=AX.X),
               w=[tg(bufC_t, f"sm{i % 2}")])
            ew(k, "vector", lambda e, c0s=c0s: e.tensor_scalar_mul(
                out=smal[:, c0s + 1:c0s + 2], in0=smal[:, c0s:c0s + 1],
                scalar1=-ATT_SCALE),
               r=[tg(bufC_t, f"sm{i % 2}")], w=[tg(bufC_t, f"sm{i % 2}")])
            mx_marks[i] = milestone(k, "vector", k.s_v)
            wait(k, "scalar", k.s_v, mx_marks[i])
            ew(k, "scalar", lambda e, i=i, c0s=c0s: e.activation(
                out=Pt[:, i % 2, :], in_=ps_s[:, 0:1152], func=AF.Exp,
                scale=ATT_SCALE, bias=smal[:, c0s + 1:c0s + 2],
                accum_out=smal[:, c0s + 2:c0s + 3]),
               r=[tg(bufC_t, f"sm{i % 2}")],
               w=[tg(bufC_t, f"p{i % 2}"), tg(bufC_t, f"sm{i % 2}")])
            exp_marks[i] = milestone(k, "scalar", k.s_sc)
            wait(k, "vector", k.s_sc, exp_marks[i])

            def _rrin(e, c0s=c0s, nc_=k.nc):
                with nc_.allow_low_precision(reason="softmax denom reciprocal"):
                    return e.reciprocal(out=smal[:, c0s + 3:c0s + 4],
                                        in_=smal[:, c0s + 2:c0s + 3])
            ew(k, "vector", _rrin, r=[tg(bufC_t, f"sm{i % 2}")],
               w=[tg(bufC_t, f"sm{i % 2}")])
            mrin = milestone(k, "vector", k.s_v)
            wait(k, "scalar", k.s_v, mrin)
            ew(k, "scalar", lambda e, i=i, c0s=c0s: e.activation(
                out=Pt[:, i % 2, :], in_=Pt[:, i % 2, :], func=AF.Copy,
                scale=smal[:, c0s + 3:c0s + 4]),
               r=[tg(bufC_t, f"p{i % 2}"), tg(bufC_t, f"sm{i % 2}")],
               w=[tg(bufC_t, f"p{i % 2}")])
            pn_marks[i] = milestone(k, "scalar", k.s_sc)
        if 0 <= j < NIT:
            wait(k, "vector", k.s_pe, pt_marks[j])
            ew(k, "vector", lambda e, j=j: e.tensor_copy(
                out=ptile[:, j % 2, 0:1024], in_=ps_pt[:, :]),
               w=[tg(bufC_t, f"pt{j % 2}")])
            ew(k, "vector", lambda e, j=j: e.tensor_copy(
                out=ptile[:, j % 2, 1024:1152], in_=ps_pt2[:, 0:128]),
               w=[tg(bufC_t, f"pt{j % 2}")])
            pte_marks[j] = milestone(k, "vector", k.s_v)
        if m >= 0:
            hm, qtm = divmod(m, 8)
            wait(k, "vector", k.s_pe, av_marks[m])
            ew(k, "vector", lambda e, m=m, hm=hm, qtm=qtm: e.tensor_copy(
                out=xhat[:, hm, qtm * 128:(qtm + 1) * 128], in_=ps_ot[m % 2]),
               w=[tg(xhat, f"at{hm}.{qtm}")])
            oe_marks[m] = milestone(k, "vector", k.s_v)

    wait(k, "tensor", k.s_v)

    def evac_addres1(oc, si, c0, wd, ps):
        ew(k, "vector", lambda e, oc=oc, c0=c0, wd=wd, ps=ps: e.tensor_tensor(
            out=xres[:, oc, 1 + c0:1 + c0 + wd], in0=xres[:, oc, 1 + c0:1 + c0 + wd],
            in1=ps, op=ALU.add), r=[tg(xres, f"o{oc}.{si}")], w=[tg(xres, f"o{oc}.{si}")])

    emit_proj(k, 16, 16, lambda oc: wo_t[oc],
              lambda dc, c0, wd: xhat[:, dc, c0:c0 + wd],
              evac_addres1, "vector", wslab, ps_mm, 2)




def emit_ffn1(k, xres, xhat, bufA_t, bufB_t, bufC, bufC_t, ps_mm,
              W13_OFF, RANK2_SZ):
    occ2 = k.cc_w2_out_ap

    def w2h_slab_ap(dc):
        r, l = divmod(dc, 2)
        off = r * RANK2_SZ + l * 1048576
        return occ2[off:off + 1048576].rearrange("(p m) -> p m", p=128)

    def w13_slab_ap(fc, which):
        r, j = divmod(fc, 8)
        off = r * RANK2_SZ + W13_OFF + (which * 8 + j) * 262144
        return occ2[off:off + 262144].rearrange("(p m) -> p m", p=128)

    hA = bufA_t[:, 0:16384].rearrange("p (a b) -> p a b", b=512)
    hB = bufB_t[:, 0:16384].rearrange("p (a b) -> p a b", b=512)

    def hbuf(fc):
        return (hA if fc < 32 else hB)[:, fc % 32, :]

    def htag(fc):
        return tg(bufA_t if fc < 32 else bufB_t, fc % 32)

    gslab = bufC_t[:, 0:8192].rearrange("p (a b c) -> p a b c", a=4, c=128)
    sg = bufC_t[:, 10240:11264].rearrange("p (a b) -> p a b", a=2)
    pslab = bufC_t[:, 0:16384].rearrange("p (a b c) -> p a b c", b=64, c=128)
    wsl = (k.s_k, k.s_k2, k.s_k3, k.s_k4)

    wait(k, "scalar", k.s_v)
    wait(k, "scalar", k.s_pe)
    wait(k, "vector", k.s_pe)
    emit_ln(k, xres, xhat, ((1, 512), (513, 512)), bufC)

    for th in range(2):
        cx = 1 + th * 512
        wait(k, "sync", k.s_pe)
        wait(k, "vector", k.s_pe)
        wait(k, "scalar", k.s_pe)
        wait(k, "tensor", k.s_v)
        wait(k, "tensor", k.s_sc)
        if th == 0:
            wait(k, "sync", k.s_cc, k.m_ccw2)
        load_marks = {}

        def w_load(fc, which):
            slot = (2 * fc + which) % 4
            sem = wsl[slot]
            lm = sem.inc(16)
            ew(k, "sync", lambda e, fc=fc, which=which, slot=slot, h=sem.h: e.dma_start(
                out=gslab[:, slot, :, :].rearrange("p a b -> p (a b)"),
                in_=w13_slab_ap(fc, which)).then_inc(h, 16))
            load_marks[(fc, which)] = lm

        w_load(0, 0)
        w_load(0, 1)
        w_load(1, 0)
        w_load(1, 1)
        gmark = {}
        umark = {}
        hmark = {}
        smark = {}
        for fc in range(64):
            slot_g = (2 * fc) % 4
            slot_u = (2 * fc + 1) % 4
            wait(k, "tensor", wsl[slot_g], load_marks[(fc, 0)])
            if fc >= 2:
                wait(k, "tensor", k.s_sc, smark[fc - 2])
            for dc in range(NCD):
                ew(k, "tensor", lambda e, dc=dc, slot_g=slot_g, cx=cx: e.matmul(
                    ps_mm[:, slot_g, :], gslab[:, slot_g, dc, :],
                    xhat[:, dc, cx:cx + 512], start=(dc == 0), stop=(dc == NCD - 1)))
            gmark[fc] = milestone(k, "tensor", k.s_pe)
            wait(k, "tensor", wsl[slot_u], load_marks[(fc, 1)])
            if fc >= 2:
                wait(k, "tensor", k.s_v, hmark[fc - 2])
            for dc in range(NCD):
                ew(k, "tensor", lambda e, dc=dc, slot_u=slot_u, cx=cx: e.matmul(
                    ps_mm[:, slot_u, :], gslab[:, slot_u, dc, :],
                    xhat[:, dc, cx:cx + 512], start=(dc == 0), stop=(dc == NCD - 1)))
            umark[fc] = milestone(k, "tensor", k.s_pe)
            if fc + 2 < 64:
                wait(k, "sync", k.s_pe, gmark[fc])
                w_load(fc + 2, 0)
                wait(k, "sync", k.s_pe, umark[fc])
                w_load(fc + 2, 1)
            wait(k, "scalar", k.s_pe, gmark[fc])
            if fc >= 2:
                wait(k, "scalar", k.s_v, hmark[fc - 2])
            ew(k, "scalar", lambda e, fc=fc, slot_g=slot_g: e.activation(
                out=sg[:, fc % 2, :], in_=ps_mm[:, slot_g, :], func=AF.Silu),
               w=[f"bufC:sg{fc % 2}"])
            smark[fc] = milestone(k, "scalar", k.s_sc)
            wait(k, "vector", k.s_pe, umark[fc])
            wait(k, "vector", k.s_sc, smark[fc])
            ew(k, "vector", lambda e, fc=fc, slot_u=slot_u: e.tensor_tensor(
                out=hbuf(fc), in0=sg[:, fc % 2, :], in1=ps_mm[:, slot_u, :],
                op=ALU.mult), r=[f"bufC:sg{fc % 2}"], w=[htag(fc)])
            hmark[fc] = milestone(k, "vector", k.s_v)
        mhn = milestone(k, "vector", k.s_v)
        wait(k, "tensor", k.s_v, mhn)

        def evac_ffn1(oc, si, c0_, wd, ps, th=th, cx=cx):
            ew(k, "vector", lambda e, oc=oc, ps=ps, cx=cx: e.tensor_tensor(
                out=xres[:, oc, cx:cx + 512], in0=xres[:, oc, cx:cx + 512],
                in1=ps, op=ALU.add),
               r=[tg(xres, f"g{oc}.{th}")], w=[tg(xres, f"g{oc}.{th}")])

        emit_proj(k, 16, 64, w2h_slab_ap, lambda dc, c0_, wd: hbuf(dc),
                  evac_ffn1, "vector", pslab, ps_mm, 4, segs=((0, 512),),
                  w_queue="sync", w_sems=(k.s_k, k.s_k2), load_wait=(k.s_v, mhn))



def build(debug_stop=None):
    nc = bass.Bass("TRN2", target_bir_lowering=False)
    es = ExitStack()

    xT = nc.dram_tensor("xT", [NCD, 128, W], F32, kind="ExternalInput").ap()
    wr_t = nc.dram_tensor("wr_t", [16, 128, D], F32, kind="ExternalInput").ap()
    wk_t = nc.dram_tensor("wk_t", [16, 128, D], F32, kind="ExternalInput").ap()
    wv_t = nc.dram_tensor("wv_t", [16, 128, D], F32, kind="ExternalInput").ap()
    wo_t = nc.dram_tensor("wo_t", [16, 128, D], F32, kind="ExternalInput").ap()
    vecs = nc.dram_tensor("vecs", [128, 6, 16], F32, kind="ExternalInput").ap()
    cid = nc.dram_tensor("cid", [128, 3, 128], BF16, kind="ExternalInput").ap()
    outT = nc.dram_tensor("outT", [NCD, 128, T], F32, kind="ExternalOutput").ap()

    l1wq_t = nc.dram_tensor("l1wq_t", [16, 128, D], F32, kind="ExternalInput").ap()
    l1wk_t = nc.dram_tensor("l1wk_t", [16, 128, D], F32, kind="ExternalInput").ap()
    l1wv_t = nc.dram_tensor("l1wv_t", [16, 128, D], F32, kind="ExternalInput").ap()
    l1wo_t = nc.dram_tensor("l1wo_t", [16, 128, D], F32, kind="ExternalInput").ap()
    w1_sl = nc.dram_tensor("w1_sl", [8, 128, D], F32, kind="ExternalInput").ap()
    w3_sl = nc.dram_tensor("w3_sl", [8, 128, D], F32, kind="ExternalInput").ap()
    ropec_d = nc.dram_tensor("ropec_d", [128, 2, 1024], BF16, kind="ExternalInput").ap()
    band_d = nc.dram_tensor("band_d", [128, 1152], BF16, kind="ExternalInput").ap()
    killv_d = nc.dram_tensor("killv_d", [1, 2048], BF16, kind="ExternalInput").ap()
    wgu_slab = nc.dram_tensor("wgu_slab", [2048, 2048], F32, kind="ExternalInput").ap()
    wproj_slab = nc.dram_tensor("wproj_slab", [256, FFH], F32, kind="ExternalInput").ap()
    w2_slab = nc.dram_tensor("w2_slab", [256, FFH], F32, kind="ExternalInput").ap()

    cc_s_in = nc.dram_tensor("cc_s_in", [128, 16], F32)
    cc_s_out = nc.dram_tensor("cc_s_out", [256, 16], F32)
    RANK_SZ = 16 * 262144 + 2 * 1048576
    WP_OFF = 16 * 262144
    RANK2_SZ = 2 * 1048576 + 16 * 262144
    W13_OFF = 2 * 1048576
    cc_w_in = nc.dram_tensor("cc_w_in", [RANK_SZ], BF16)
    cc_w_out = nc.dram_tensor("cc_w_out", [8 * RANK_SZ], BF16)
    cc_w2_in = nc.dram_tensor("cc_w2_in", [RANK2_SZ], BF16)
    cc_w2_out = nc.dram_tensor("cc_w2_out", [8 * RANK2_SZ], BF16)
    cc_kv_in = nc.dram_tensor("cc_kv_in", [32 * 131072], BF16)
    cc_kv_out = nc.dram_tensor("cc_kv_out", [64 * 131072], BF16)
    k_rank_sz = RANK_SZ

    k = K()
    k.nc = nc
    k.em = Em()

    for s in ("s_ld", "s_w", "s_w2", "s_w3", "s_w4", "s_k", "s_k2", "s_k3", "s_k4", "s_pe", "s_v", "s_sc", "s_g", "s_cc", "s_st"):
        setattr(k, s, Sem(es.enter_context(nc.semaphore(s))))

    onesid = es.enter_context(nc.sbuf_tensor("onesid", [128, 3, 128], BF16))[:]
    k.ones = onesid[:, 0, :]
    k.ident = onesid[:, 1, :]
    k.swap = onesid[:, 2, :]
    killv = es.enter_context(nc.sbuf_tensor("killv", [1, 2048], BF16))[:]
    vtile = es.enter_context(nc.sbuf_tensor("vtile", [128, 10, 16], F32))[:]
    stile = es.enter_context(nc.sbuf_tensor("stile", [128, 3, 16], F32))[:]

    k.s_ld.inc(16)
    ew(k, "sync", lambda e, h=k.s_ld.h: e.dma_start(out=onesid, in_=cid).then_inc(h, 16))
    k.s_ld.inc(16)
    ew(k, "sync", lambda e, h=k.s_ld.h: e.dma_start(
        out=vtile[:, 0:6, :], in_=vecs).then_inc(h, 16))
    k.s_ld.inc(16)
    ew(k, "sync", lambda e, h=k.s_ld.h: e.dma_start(out=killv[:], in_=killv_d).then_inc(h, 16))
    m_in = k.s_ld.n
    for e in ("scalar", "vector", "tensor"):
        wait(k, e, k.s_ld, m_in)

    for i in range(3):
        ew(k, "vector", lambda e, i=i: e.tensor_scalar(
            out=vtile[:, 6 + i, :], in0=vtile[:, i, :], scalar1=-1.0, scalar2=1.0,
            op0=ALU.mult, op1=ALU.add), r=[vtile], w=[vtile])
    ew(k, "scalar", lambda e: e.activation(out=vtile[:, 9, :], in_=vtile[:, 3, :],
                                           func=AF.Exp), r=[vtile], w=[vtile])
    ew(k, "scalar", lambda e: e.activation(out=vtile[:, 9, :], in_=vtile[:, 9, :],
                                           func=AF.Exp, scale=-1.0), r=[vtile], w=[vtile])
    m_vec_s = milestone(k, "scalar", k.s_sc)
    milestone(k, "vector", k.s_v)

    k.cc_w_out_ap = cc_w_out.ap()
    k.cc_w2_out_ap = cc_w2_out.ap()
    emit_phase0(k, wgu_slab, wproj_slab, w2_slab, cc_w_in, WP_OFF,
                cc_w2_in, w1_sl, w3_sl, W13_OFF)
    m_p0 = k.s_g.n

    xres = es.enter_context(nc.sbuf_tensor("xres", [128, NCD, W], F32))[:]
    xhat = es.enter_context(nc.sbuf_tensor("xhat", [128, NCD, W], BF16))[:]
    bufA_t = es.enter_context(nc.sbuf_tensor("bufA", [128, NCD * W], BF16))[:]
    bufA = bufA_t.rearrange("p (a b) -> p a b", b=W)
    bufB_t = es.enter_context(nc.sbuf_tensor("bufB", [128, NCD * W], BF16))[:]
    bufB = bufB_t.rearrange("p (a b) -> p a b", b=W)
    bufC_t = es.enter_context(nc.sbuf_tensor("bufC", [128, NCD * W], BF16))[:]
    bufC = bufC_t.rearrange("p (a b) -> p a b", b=W)
    k.stat_scr = bufA_t[:, 0:3072].bitcast(F32).rearrange("p (a b) -> p a b", a=3)
    k.rb = bufA_t[:, 3072:5120].rearrange("p (a b) -> p a b", a=2)
    k.mrb = bufA_t[:, 5120:7168].rearrange("p (a b) -> p a b", a=2)
    wslab = es.enter_context(nc.sbuf_tensor("wslab", [128, 2, 16, 128], BF16))[:]

    k.ps_ln = es.enter_context(nc.psum_tensor("ps_ln", [128, 2, 512], F32))[:]
    ps_mm = es.enter_context(nc.psum_tensor("ps_mm", [128, 4, 512], F32))[:]
    ps_r = es.enter_context(nc.psum_tensor("ps_r", [128, 1, 512], F32))[:]
    k.ps_r = ps_r

    wait(k, "sync", k.s_g, m_p0)
    for c in range(NCD):
        k.s_ld.inc(16)
        ew(k, "sync", lambda e, c=c, h=k.s_ld.h: e.dma_start(
            out=xres[:, c, :], in_=xT[c]).then_inc(h, 16))
    m_in = k.s_ld.n
    for e in ("scalar", "vector", "tensor"):
        wait(k, e, k.s_ld, m_in)

    emit_ln(k, xres, xhat, ((0, 512), (512, 512), (1024, 1)), bufC)
    wait(k, "vector", k.s_sc, m_vec_s)

    def emit_mix(tm_i, om_i, dstbuf):
        for c in range(NCD):
            ew(k, "vector", lambda e, c=c, om_i=om_i, dstbuf=dstbuf: e.tensor_scalar_mul(
                out=dstbuf[:, c, 0:T], in0=xhat[:, c, 0:T],
                scalar1=vtile[:, om_i, c:c + 1]), r=[xhat, vtile], w=[tg(dstbuf, c)])
        for c in range(NCD):
            ew(k, "vector", lambda e, c=c, tm_i=tm_i, dstbuf=dstbuf: e.scalar_tensor_tensor(
                out=dstbuf[:, c, 0:T], in0=xhat[:, c, 1:W], scalar=vtile[:, tm_i, c:c + 1],
                in1=dstbuf[:, c, 0:T], op0=ALU.mult, op1=ALU.add),
               r=[xhat, vtile, tg(dstbuf, c)], w=[tg(dstbuf, c)])
        return milestone(k, "vector", k.s_v)

    def proj_rhs(buf):
        return lambda dc, c0, wd: buf[:, dc, c0:c0 + wd]

    def evac_copy_to(buf, act=None):
        def f(oc, si, c0, wd, ps):
            if act is None:
                ew(k, "vector", lambda e, oc=oc, c0=c0, wd=wd, ps=ps: e.tensor_copy(
                    out=buf[:, oc, c0:c0 + wd], in_=ps), w=[tg(buf, oc)])
            else:
                ew(k, "scalar", lambda e, oc=oc, c0=c0, wd=wd, ps=ps: e.activation(
                    out=buf[:, oc, c0:c0 + wd], in_=ps, func=act), w=[tg(buf, oc)])
        return f

    mk = emit_mix(0, 6, bufB)
    mvx = emit_mix(1, 7, bufA)
    wait(k, "tensor", k.s_v, mk)
    mm_k = emit_proj(k, 16, 16, lambda oc: wk_t[oc], proj_rhs(bufB),
                     evac_copy_to(bufC), "vector", wslab, ps_mm, 4)
    wait(k, "tensor", k.s_v, mvx)
    mm_v = emit_proj(k, 16, 16, lambda oc: wv_t[oc], proj_rhs(bufA),
                     evac_copy_to(bufB), "vector", wslab, ps_mm, 4)
    wait(k, "vector", k.s_pe, mm_v[15])
    mr = emit_mix(2, 8, bufA)
    wait(k, "tensor", k.s_v, mr)
    emit_proj(k, 16, 16, lambda oc: wr_t[oc], proj_rhs(bufA),
              evac_copy_to(xhat, act=AF.Sigmoid), "scalar", wslab, ps_mm, 4)
    m_rdone = k.s_sc.n

    for c in range(NCD):
        ew(k, "vector", lambda e, c=c: e.tensor_tensor(
            out=bufC[:, c, 0:T], in0=bufC[:, c, 0:T], in1=bufB[:, c, 0:T], op=ALU.mult),
           r=[tg(bufB, c), tg(bufC, c)], w=[tg(bufC, c)])
    wait(k, "vector", k.s_pe)
    for c in range(NCD):
        ew(k, "vector", lambda e, c=c: e.tensor_tensor_scan(
            out=bufB[:, c, 1:W], data0=vtile[:, 9, c:c + 1].broadcast_to([128, T]),
            data1=bufC[:, c, 0:T], initial=0.0, op0=ALU.mult, op1=ALU.add),
           r=[tg(bufC, c), vtile], w=[tg(bufB, c)])
    ew(k, "vector", lambda e: e.tensor_copy(
        out=stile[:, 0, :], in_=bufB[:, :, W - 1:W].rearrange("p c o -> p (c o)")),
       r=[bufB], w=[stile])
    m_scanA = milestone(k, "vector", k.s_v)
    wait(k, "gpsimd", k.s_v, m_scanA)
    k.s_g.inc(16)
    ew(k, "gpsimd", lambda e, h=k.s_g.h: e.dma_start(
        out=cc_s_in[:, :], in_=stile[:, 0, :]).then_inc(h, 16))
    wait(k, "gpsimd", k.s_g)
    m_cc = k.s_cc.inc(1)
    ew(k, "gpsimd", lambda e, h=k.s_cc.h: e.collective_compute(
        "AllGather", ALU.bypass, replica_groups=PAIRS,
        ins=[cc_s_in.ap().opt()], outs=[cc_s_out.ap().opt()]).then_inc(h, 1))
    wait(k, "gpsimd", k.s_cc, m_cc)
    m_srecv = k.s_g.inc(16)
    ew(k, "gpsimd", lambda e, h=k.s_g.h: e.dma_start(
        out=stile[:, 1, :], in_=cc_s_out.ap()[0:128, :]).then_inc(h, 16))
    wait(k, "vector", k.s_g, m_srecv)
    ew(k, "vector", lambda e: e.tensor_tensor(
        out=stile[:, 2, :], in0=stile[:, 1, :], in1=vtile[:, 5, :], op=ALU.mult),
       r=[stile, vtile], w=[stile])
    for c in range(NCD):
        ew(k, "vector", lambda e, c=c: e.tensor_tensor_scan(
            out=bufB[:, c, 1:W], data0=vtile[:, 9, c:c + 1].broadcast_to([128, T]),
            data1=bufC[:, c, 0:T], initial=stile[:, 2, c:c + 1], op0=ALU.mult, op1=ALU.add),
           r=[tg(bufC, c), vtile, stile], w=[tg(bufB, c)])
    ew(k, "vector", lambda e: e.tensor_copy(
        out=bufB[:, :, 0:1].rearrange("p c o -> p (c o)"), in_=stile[:, 2, :]),
       r=[stile], w=[bufB])
    wait(k, "vector", k.s_sc, m_rdone)
    wait(k, "vector", k.s_pe)
    for c in range(NCD):
        ew(k, "vector", lambda e, c=c: e.scalar_tensor_tensor(
            out=bufA[:, c, 0:T], in0=bufC[:, c, 0:T], scalar=vtile[:, 4, c:c + 1],
            in1=bufB[:, c, 0:T], op0=ALU.mult, op1=ALU.add),
           r=[tg(bufC, c), tg(bufB, c), vtile], w=[tg(bufA, c)])
    for c in range(NCD):
        ew(k, "vector", lambda e, c=c: e.tensor_tensor(
            out=bufC[:, c, 0:T], in0=bufA[:, c, 0:T], in1=xhat[:, c, 0:T], op=ALU.mult),
           r=[tg(bufA, c), xhat], w=[tg(bufC, c)])
    m_wkv = milestone(k, "vector", k.s_v)
    wait(k, "tensor", k.s_v, m_wkv)

    def evac_addres(oc, si, c0, wd, ps):
        ew(k, "vector", lambda e, oc=oc, c0=c0, wd=wd, ps=ps: e.tensor_tensor(
            out=xres[:, oc, 1 + c0:1 + c0 + wd], in0=xres[:, oc, 1 + c0:1 + c0 + wd],
            in1=ps, op=ALU.add), r=[tg(xres, f"{oc}.{si}")], w=[tg(xres, f"{oc}.{si}")])

    emit_proj(k, 16, 16, lambda oc: wo_t[oc], proj_rhs(bufC),
              evac_addres, "vector", wslab, ps_mm, 4)

    emit_ffn0(k, xres, xhat, bufA_t, bufB_t, bufC, bufC_t, wslab, ps_mm, ps_r,
              WP_OFF, RANK_SZ)

    emit_l1attn(k, xres, xhat, bufA, bufB, bufC, bufC_t, wslab, ps_mm,
                l1wq_t, l1wk_t, l1wv_t, l1wo_t, ropec_d, band_d, killv,
                cc_kv_in, cc_kv_out)

    emit_ffn1(k, xres, xhat, bufA_t, bufB_t, bufC, bufC_t, ps_mm,
              W13_OFF, RANK2_SZ)

    m_done_v = milestone(k, "vector", k.s_v)
    m_done_s = milestone(k, "scalar", k.s_sc)
    wait(k, "sync", k.s_v, m_done_v)
    wait(k, "sync", k.s_sc, m_done_s)
    for c in range(NCD):
        k.s_st.inc(16)
        ew(k, "sync", lambda e, c=c, h=k.s_st.h: e.dma_start(
            out=outT[c], in_=xres[:, c, 1:W]).then_inc(h, 16))
    wait(k, "sync", k.s_st)

    def _replay(e, lst):
        for f, posts in lst:
            r = f(e)
            for s in posts:
                r.then_inc(s, 1)

    with nc.Block() as block:
        @block.sync
        def _(e):
            _replay(e, k.em.q["sync"])

        @block.scalar
        def _(e):
            _replay(e, k.em.q["scalar"])

        @block.vector
        def _(e):
            _replay(e, k.em.q["vector"])

        @block.tensor
        def _(e):
            _replay(e, k.em.q["tensor"])

        @block.gpsimd
        def _(e):
            _replay(e, k.em.q["gpsimd"])

    es.close()
    return nc



def _tile_lhsT(Wm, contract_axis):
    if contract_axis == 1:
        O, Kd = Wm.shape
        Oc, Kc = O // 128, Kd // 128
        t = Wm.reshape(Oc, 128, Kc, 128).transpose(0, 3, 2, 1)
    else:
        Kd, O = Wm.shape
        Kc, Oc = Kd // 128, O // 128
        t = Wm.reshape(Kc, 128, Oc, 128).transpose(2, 1, 0, 3)
    return np.ascontiguousarray(t.reshape(Oc, 128, Kc * 128))


def kernel(**inputs):
    L0x = np.asarray(inputs["L0_attn_hidden_states"])

    wr_t = _tile_lhsT(np.asarray(inputs["L0_attn_Wr"]), 1)
    wk_t = _tile_lhsT(np.asarray(inputs["L0_attn_Wk"]), 1)
    wv_t = _tile_lhsT(np.asarray(inputs["L0_attn_Wv"]), 1)
    wo_t = _tile_lhsT(np.asarray(inputs["L0_attn_Wo"]), 1)

    def vec_tile(v):
        return np.ascontiguousarray(np.asarray(v).reshape(-1).reshape(16, 128).T)

    tmk = vec_tile(inputs["L0_attn_time_mix_k"])
    tmv = vec_tile(inputs["L0_attn_time_mix_v"])
    tmr = vec_tile(inputs["L0_attn_time_mix_r"])
    td = vec_tile(inputs["L0_attn_time_decay"])
    tf = vec_tile(inputs["L0_attn_time_first"])

    w1_t = _tile_lhsT(np.asarray(inputs["L1_ffn_W1"]), 1)
    w3_t = _tile_lhsT(np.asarray(inputs["L1_ffn_W3"]), 1)
    l1wq_t = _tile_lhsT(np.asarray(inputs["L1_attn_wq"]), 0)
    l1wk_t = _tile_lhsT(np.asarray(inputs["L1_attn_wk"]), 0)
    l1wv_t = _tile_lhsT(np.asarray(inputs["L1_attn_wv"]), 0)
    l1wo_t = _tile_lhsT(np.asarray(inputs["L1_attn_wo"]), 0)

    wgu = np.asarray(inputs["L0_ffn_w_gate_up"])
    wproj = np.asarray(inputs["L0_ffn_w_proj"])
    W2 = np.asarray(inputs["L1_ffn_W2"])

    cid = np.zeros((128, 3, 128), dtype=ml_dtypes.bfloat16)
    cid[:, 0, :] = 1.0
    cid[:, 1, :] = np.eye(128)
    sw = np.zeros((128, 128), dtype=np.float32)
    for ii in range(128):
        sw[ii, ii ^ 64] = 1.0
    cid[:, 2, :] = sw

    ii = np.arange(128)[:, None]
    jj = np.arange(1152)[None, :]
    band = np.where((jj >= ii + 1) & (jj <= ii + 1024), 0.0, NEG).astype(ml_dtypes.bfloat16)
    inv_freq = (1.0 / 10000.0 ** (np.arange(0, 128, 2, dtype=np.float32) / 128.0))

    in_maps = []
    for c in range(8):
        b, p = c // 2, c % 2
        t0 = p * T
        xTc = np.zeros((NCD, 128, W), dtype=np.float32)
        xTc[:, :, 1:] = L0x[b, t0:t0 + T, :].T.reshape(NCD, 128, T)
        if p == 1:
            xTc[:, :, 0] = L0x[b, t0 - 1, :].reshape(NCD, 128)
        vv = np.stack([tmk, tmv, tmr, td, tf,
                       np.full((128, 16), 1.0 if p == 1 else 0.0, np.float32)], axis=1)
        pos = np.arange(t0, t0 + T, dtype=np.float32)
        fr = pos[None, :] * inv_freq[:, None]
        cos2 = np.concatenate([np.cos(fr), np.cos(fr)], axis=0)
        sin2 = np.concatenate([-np.sin(fr), np.sin(fr)], axis=0)
        ropec_h = np.stack([cos2, sin2], axis=1).astype(ml_dtypes.bfloat16)
        killv_h = np.zeros((1, 2048), dtype=ml_dtypes.bfloat16)
        if p == 0:
            killv_h[0, 0:1024] = NEG
        in_maps.append({
            "xT": np.ascontiguousarray(xTc),
            "wr_t": wr_t, "wk_t": wk_t, "wv_t": wv_t, "wo_t": wo_t,
            "vecs": np.ascontiguousarray(vv.astype(np.float32)),
            "cid": cid,
            "wgu_slab": np.ascontiguousarray(wgu[c * 2048:(c + 1) * 2048, :]),
            "wproj_slab": np.ascontiguousarray(wproj[c * 256:(c + 1) * 256, :]),
            "w2_slab": np.ascontiguousarray(W2[c * 256:(c + 1) * 256, :]),
            "l1wq_t": l1wq_t, "l1wk_t": l1wk_t, "l1wv_t": l1wv_t, "l1wo_t": l1wo_t,
            "w1_sl": np.ascontiguousarray(w1_t[c * 8:(c + 1) * 8]),
            "w3_sl": np.ascontiguousarray(w3_t[c * 8:(c + 1) * 8]),
            "ropec_d": np.ascontiguousarray(ropec_h),
            "band_d": band, "killv_d": killv_h,
        })

    nc = build()
    res = run_bass_kernel_spmd(nc, in_maps, core_ids=list(range(8)))

    out = np.empty((4, 2048, 2048), dtype=np.float32)
    for c in range(8):
        b, p = c // 2, c % 2
        out[b, p * T:(p + 1) * T, :] = res.results[c]["outT"].reshape(D, T).T
    return out


# version 05
# speedup vs baseline: 1.0158x; 1.0158x over previous
import numpy as np
import ml_dtypes
from contextlib import ExitStack

import concourse.bass as bass
import concourse.mybir as mybir
from concourse.bass_utils import run_bass_kernel_spmd

F32 = mybir.dt.float32
BF16 = mybir.dt.bfloat16
AF = mybir.ActivationFunctionType
ALU = mybir.AluOpType
AX = mybir.AxisListType

ENGINES = ("sync", "scalar", "vector", "tensor", "gpsimd")

D = 2048
NCD = 16
T = 1024
W = T + 1
NH = 16
HDIM = 128
FFH = 8192
NFH = FFH // 128
NEG = -1.0e9
LN_EPS = 1e-5
RMS_EPS = 1.1920929e-7
ATT_SCALE = 0.08838834764831845
PAIRS = [[0, 1], [2, 3], [4, 5], [6, 7]]


class Sem:
    __slots__ = ("h", "n")

    def __init__(self, h):
        self.h = h
        self.n = 0

    def inc(self, k=1):
        self.n += k
        return self.n


class Em:

    def __init__(self):
        self.q = {e: [] for e in ENGINES}
        self.dirty = {e: set() for e in ENGINES}
        self.rdirty = {e: set() for e in ENGINES}

    @staticmethod
    def _keys(aps):
        out = set()
        for a in aps:
            if a is None:
                continue
            s = a if isinstance(a, str) else a.name
            name, _, tag = s.partition(":")
            out.add((name, tag or None))
        return out

    @staticmethod
    def _conflict(ks, dirty):
        for n, t in ks:
            for dn, dt in dirty:
                if n == dn and (t is None or dt is None or t == dt):
                    return True
        return False

    def on(self, e, f, r=(), w=()):
        rk = self._keys(r)
        wk = self._keys(w)
        if (rk and self._conflict(rk, self.dirty[e])) or \
           (wk and (self._conflict(wk, self.rdirty[e]) or self._conflict(wk, self.dirty[e]))):
            self.q[e].append([lambda eng: eng.drain(), []])
            self.dirty[e] = set()
            self.rdirty[e] = set()
        self.q[e].append([f, []])
        self.dirty[e] |= wk
        self.rdirty[e] |= rk

    def drain(self, e):
        if self.dirty[e] or self.rdirty[e]:
            self.q[e].append([lambda eng: eng.drain(), []])
            self.dirty[e] = set()
            self.rdirty[e] = set()


class K:
    pass


def ew(k, eng, f, r=(), w=()):
    k.em.on(eng, f, r, w)


def tg(ap, tag):
    return f"{ap.name}:{tag}"


def wait(k, eng, sem, mark=None):
    v = sem.n if mark is None else mark
    if v > 0:
        k.em.on(eng, lambda e, s=sem.h, v=v: e.wait_ge(s, v))


def milestone(k, eng, sem):
    q = k.em.q[eng]
    if not q or q[-1][1]:
        k.em.q[eng].append([lambda e: e.drain(), []])
        k.em.dirty[eng] = set()
        v = sem.inc(1)
        k.em.on(eng, lambda e, s=sem.h: e.nop().then_inc(s, 1))
        return v
    prev = sem.n
    if prev > 0:
        q.insert(len(q) - 1, [lambda e, s=sem.h, p=prev: e.wait_ge(s, p), []])
    v = sem.inc(1)
    q[-1][1].append(sem.h)
    return v


SEGS_T = ((0, 512), (512, 512))



def emit_ln(k, src, dst, segs, sqbuf):
    stat, rb, mrb, ps = k.stat_scr, k.rb, k.mrb, k.ps_ln
    for si, (c0, wd) in enumerate(segs):
        ps_s = ps[:, 0, 0:wd]
        ps_q = ps[:, 1, 0:wd]
        for c in range(NCD):
            ew(k, "scalar", lambda e, c=c, c0=c0, wd=wd: e.activation(
                out=dst[:, c, c0:c0 + wd], in_=src[:, c, c0:c0 + wd], func=AF.Copy),
               r=[src], w=[dst])
        m_cast = milestone(k, "scalar", k.s_sc)
        wait(k, "vector", k.s_sc, m_cast)
        wait(k, "tensor", k.s_sc, m_cast)
        for c in range(NCD):
            ew(k, "vector", lambda e, c=c, c0=c0, wd=wd: e.tensor_tensor(
                out=sqbuf[:, c, 0:wd], in0=dst[:, c, c0:c0 + wd],
                in1=dst[:, c, c0:c0 + wd], op=ALU.mult), r=[dst], w=[sqbuf])
        m_sq = milestone(k, "vector", k.s_v)
        for c in range(NCD):
            ew(k, "tensor", lambda e, c=c, c0=c0, wd=wd, ps_s=ps_s: e.matmul(
                ps_s, k.ones, dst[:, c, c0:c0 + wd], start=(c == 0), stop=(c == NCD - 1)))
        wait(k, "tensor", k.s_v, m_sq)
        for c in range(NCD):
            ew(k, "tensor", lambda e, c=c, wd=wd, ps_q=ps_q: e.matmul(
                ps_q, k.ones, sqbuf[:, c, 0:wd], start=(c == 0), stop=(c == NCD - 1)))
        m_mm = milestone(k, "tensor", k.s_pe)
        wait(k, "vector", k.s_pe, m_mm)
        mean = stat[:, 0, 0:wd]
        var = stat[:, 1, 0:wd]
        msq = stat[:, 2, 0:wd]
        ew(k, "vector", lambda e, mean=mean, ps_s=ps_s: e.tensor_scalar_mul(
            out=mean, in0=ps_s, scalar1=1.0 / D), w=[stat])
        ew(k, "vector", lambda e, var=var, ps_q=ps_q: e.tensor_scalar(
            out=var, in0=ps_q, scalar1=1.0 / D, scalar2=LN_EPS, op0=ALU.mult, op1=ALU.add),
           w=[stat])
        ew(k, "vector", lambda e, mean=mean, msq=msq: e.tensor_tensor(
            out=msq, in0=mean, in1=mean, op=ALU.mult), r=[stat], w=[stat])
        ew(k, "vector", lambda e, var=var, msq=msq: e.tensor_tensor(
            out=var, in0=var, in1=msq, op=ALU.subtract), r=[stat], w=[stat])
        m_v1 = milestone(k, "vector", k.s_v)
        wait(k, "scalar", k.s_v, m_v1)
        ew(k, "scalar", lambda e, var=var: e.activation(out=var, in_=var, func=AF.Sqrt),
           r=[stat], w=[stat])
        m_s2 = milestone(k, "scalar", k.s_sc)
        wait(k, "vector", k.s_sc, m_s2)
        rbs = rb[:, si % 2, 0:wd]
        mrbs = mrb[:, si % 2, 0:wd]

        def _recip(e, var=var, rbs=rbs, nc_=k.nc):
            with nc_.allow_low_precision(reason="rstd broadcast tile stored bf16"):
                return e.reciprocal(out=rbs, in_=var)
        ew(k, "vector", _recip, r=[stat], w=[rb])
        ew(k, "vector", lambda e, mean=mean, rbs=rbs, mrbs=mrbs: e.tensor_tensor(
            out=mrbs, in0=mean, in1=rbs, op=ALU.mult), r=[stat, rb], w=[mrb])
        for c in range(NCD):
            ew(k, "vector", lambda e, c=c, c0=c0, wd=wd, rbs=rbs: e.tensor_tensor(
                out=dst[:, c, c0:c0 + wd], in0=dst[:, c, c0:c0 + wd], in1=rbs, op=ALU.mult),
               r=[tg(dst, c), rb], w=[tg(dst, c)])
        for c in range(NCD):
            ew(k, "vector", lambda e, c=c, c0=c0, wd=wd, mrbs=mrbs: e.tensor_tensor(
                out=dst[:, c, c0:c0 + wd], in0=dst[:, c, c0:c0 + wd], in1=mrbs,
                op=ALU.subtract), r=[tg(dst, c), mrb], w=[tg(dst, c)])
    milestone(k, "vector", k.s_v)



def emit_proj(k, n_oc, n_kc, w_src, rhs, evac, evac_eng, wslab, ps_pool, n_banks,
              segs=SEGS_T, w_queue="gpsimd", w_sems=None, load_wait=None):
    w_sems = w_sems or (k.s_w, k.s_w2)
    evac_sem = k.s_v if evac_eng == "vector" else k.s_sc
    nseg = len(segs)
    evac_marks = {}
    load_marks = {}
    mm_marks = {}

    def emit_load(oc):
        if oc >= 2:
            wait(k, w_queue, k.s_pe, mm_marks[oc - 2])
        else:
            wait(k, w_queue, k.s_pe)
            if load_wait is not None:
                wait(k, w_queue, load_wait[0], load_wait[1])
        ld = w_sems[oc % 2].inc(16)
        ew(k, w_queue, lambda e, oc=oc, h=w_sems[oc % 2].h: e.dma_start(
            out=wslab[:, oc % 2, :, :].rearrange("p a b -> p (a b)"),
            in_=w_src(oc)).then_inc(h, 16))
        load_marks[oc] = ld

    emit_load(0)
    if n_oc > 1:
        emit_load(1)
    for oc in range(n_oc):
        wait(k, "tensor", w_sems[oc % 2], load_marks[oc])
        if oc >= max(1, n_banks // nseg):
            wait(k, "tensor", evac_sem, evac_marks[oc - max(1, n_banks // nseg)])
        elif oc == 0:
            wait(k, "tensor", k.s_v)
            wait(k, "tensor", k.s_sc)
        for si, (c0, wd) in enumerate(segs):
            pb = (nseg * oc + si) % n_banks
            for dc in range(n_kc):
                ew(k, "tensor", lambda e, oc=oc, dc=dc, c0=c0, wd=wd, pb=pb: e.matmul(
                    ps_pool[:, pb, 0:wd], wslab[:, oc % 2, dc, :], rhs(dc, c0, wd),
                    start=(dc == 0), stop=(dc == n_kc - 1)))
        mm = milestone(k, "tensor", k.s_pe)
        mm_marks[oc] = mm
        if oc + 2 < n_oc:
            emit_load(oc + 2)
        wait(k, evac_eng, k.s_pe, mm)
        for si, (c0, wd) in enumerate(segs):
            pb = (nseg * oc + si) % n_banks
            evac(oc, si, c0, wd, ps_pool[:, pb, 0:wd])
        evac_marks[oc] = milestone(k, evac_eng, evac_sem)
    return mm_marks





def emit_phase0(k, wgu_slab, wproj_slab, w2_slab, cc_w_in, WP_OFF,
                cc_w2_in, w1_sl, w3_sl, W13_OFF):
    nc = k.nc
    es0 = ExitStack()
    ld = es0.enter_context(nc.sbuf_tensor("p0_ld", [128, 2, FFH], BF16))[:]
    m1 = es0.enter_context(nc.sbuf_tensor("p0_m1", [128, FFH], BF16))[:]
    m2 = es0.enter_context(nc.sbuf_tensor("p0_m2", [128, FFH], BF16))[:]
    st_ = es0.enter_context(nc.sbuf_tensor("p0_st", [128, 64, 128], BF16))[:]
    qs = es0.enter_context(nc.sbuf_tensor("p0_qs", [128, 3, 128], F32))[:]
    sbf = es0.enter_context(nc.sbuf_tensor("p0_sbf", [128, 128], BF16))[:]
    pp = es0.enter_context(nc.sbuf_tensor("p0_pp", [128, 2, FFH], F32))[:]
    pt = es0.enter_context(nc.psum_tensor("p0_pt", [128, 2, 1024], BF16))[:]
    in_cc = cc_w_in.ap()
    in_cc2 = cc_w2_in.ap()
    last_stage = [0]

    def quant(par, ncols):
        ng = ncols // 64
        lv = ld[:, par, 0:ncols].rearrange("p (g e) -> p g e", e=64)
        gb = lambda a: a.rearrange("p (g o) -> p g o", o=1).broadcast_to([128, ng, 64])
        ew(k, "vector", lambda e, lv=lv, ng=ng: e.reduce_sum(
            out=qs[:, 0, 0:ng].rearrange("p (g o) -> p g o", o=1), in_=lv,
            axis=AX.X, apply_absolute_value=True), r=[tg(ld, par)], w=[qs])
        ew(k, "vector", lambda e, ng=ng: e.tensor_scalar(
            out=qs[:, 0, 0:ng], in0=qs[:, 0, 0:ng], scalar1=1.0 / 64, scalar2=1e-8,
            op0=ALU.mult, op1=ALU.max), r=[qs], w=[qs])
        ew(k, "vector", lambda e, ng=ng: e.tensor_copy(
            out=sbf[:, 0:ng], in_=qs[:, 0, 0:ng]), r=[qs], w=[sbf])

        def _rcp(e, ng=ng, nc_=k.nc):
            with nc_.allow_low_precision(reason="group scale reciprocal"):
                e.reciprocal(out=qs[:, 1, 0:ng], in_=sbf[:, 0:ng])
        ew(k, "vector", _rcp, r=[sbf], w=[qs])
        m1v = m1[:, 0:ncols].rearrange("p (g e) -> p g e", e=64)
        m2v = m2[:, 0:ncols].rearrange("p (g e) -> p g e", e=64)
        ew(k, "vector", lambda e, lv=lv, m1v=m1v, ng=ng: e.tensor_tensor(
            out=m1v, in0=lv, in1=gb(qs[:, 1, 0:ng]), op=ALU.mult),
           r=[tg(ld, par), qs], w=[m1])
        ew(k, "vector", lambda e, ncols=ncols: e.tensor_single_scalar(
            out=m2[:, 0:ncols], in_=m1[:, 0:ncols], scalar=0.5, op=ALU.is_gt),
           r=[m1], w=[m2])
        ew(k, "vector", lambda e, ncols=ncols: e.tensor_single_scalar(
            out=m1[:, 0:ncols], in_=m1[:, 0:ncols], scalar=-0.5, op=ALU.is_lt),
           r=[m1], w=[m1])
        ew(k, "vector", lambda e, ncols=ncols: e.tensor_tensor(
            out=m1[:, 0:ncols], in0=m2[:, 0:ncols], in1=m1[:, 0:ncols], op=ALU.subtract),
           r=[m1, m2], w=[m1])
        ew(k, "vector", lambda e, m1v=m1v, ng=ng: e.tensor_tensor(
            out=m1v, in0=m1v, in1=gb(sbf[:, 0:ng]), op=ALU.mult),
           r=[m1, sbf], w=[m1])
        return milestone(k, "vector", k.s_v)

    def transpose_stage(ntiles, off, m_src, dst_cc=None):
        dst_cc = in_cc if dst_cc is None else dst_cc
        wait(k, "tensor", k.s_v, m_src)
        wait(k, "vector", k.s_g, last_stage[0])
        ev = {}
        for g0 in range(0, ntiles, 8):
            gcnt = min(8, ntiles - g0)
            bank = (g0 // 8) % 2
            if g0 >= 16:
                wait(k, "tensor", k.s_v, ev[g0 - 16])
            for t in range(gcnt):
                cc = g0 + t
                ew(k, "tensor", lambda e, cc=cc, bank=bank, t=t: e.transpose(
                    pt[:, bank, t * 128:(t + 1) * 128],
                    m1[:, cc * 128:(cc + 1) * 128], k.ident))
            mk = milestone(k, "tensor", k.s_pe)
            wait(k, "vector", k.s_pe, mk)
            ew(k, "vector", lambda e, g0=g0, bank=bank, gcnt=gcnt: e.tensor_copy(
                out=st_[:, g0:g0 + gcnt, :].rearrange("p a b -> p (a b)"),
                in_=pt[:, bank, 0:gcnt * 128]), w=[st_])
            ev[g0] = milestone(k, "vector", k.s_v)
        wait(k, "gpsimd", k.s_v, ev[(ntiles - 1) // 8 * 8])
        sz = ntiles * 128 * 128
        last_stage[0] = k.s_g.inc(16)
        ew(k, "gpsimd", lambda e, off=off, sz=sz, ntiles=ntiles, dst_cc=dst_cc,
           h=k.s_g.h: e.dma_start(
            out=dst_cc[off:off + sz].rearrange("(p m) -> p m", p=128),
            in_=st_[:, 0:ntiles, :].rearrange("p a b -> p (a b)")).then_inc(h, 16))
        return milestone(k, "tensor", k.s_pe)

    qm = {}
    tm = {}
    for l in range(16):
        par = l % 2
        if l >= 2:
            wait(k, "gpsimd", k.s_v, qm[l - 2])
        sem = (k.s_w, k.s_w2)[par]
        lm = sem.inc(16)
        ew(k, "gpsimd", lambda e, l=l, par=par, h=sem.h: e.dma_start(
            out=ld[:, par, 0:2048], in_=wgu_slab[l * 128:(l + 1) * 128, :]).then_inc(h, 16))
        wait(k, "vector", sem, lm)
        if l >= 1:
            wait(k, "vector", k.s_pe, tm[l - 1])
        qm[l] = quant(par, 2048)
        tm[l] = transpose_stage(16, l * 262144, qm[l])

    for l in range(2):
        par = l % 2
        sem = (k.s_w, k.s_w2)[par]
        wait(k, "gpsimd", k.s_v, qm[14 + l] if l else qm[14])
        lm = sem.inc(16)
        ew(k, "gpsimd", lambda e, l=l, par=par, h=sem.h: e.dma_start(
            out=ld[:, par, :], in_=wproj_slab[l * 128:(l + 1) * 128, :]).then_inc(h, 16))
        wait(k, "vector", sem, lm)
        wait(k, "vector", k.s_pe, tm[15 + l] if l else tm[15])
        qm[16 + l] = quant(par, FFH)
        tm[16 + l] = transpose_stage(64, WP_OFF + l * 1048576, qm[16 + l])

    SCL = float(FFH) ** -0.5
    bm = {}
    for l in range(2):
        lm = k.s_k.inc(16)
        if l == 1:
            wait(k, "sync", k.s_v, bm[0])
        ew(k, "sync", lambda e, l=l, h=k.s_k.h: e.dma_start(
            out=pp[:, 0, :], in_=w2_slab[l * 128:(l + 1) * 128, :]).then_inc(h, 16))
        wait(k, "vector", k.s_k, lm)
        par = 0
        for s in range(13):
            h2 = 1 << s
            sv = pp[:, par, :].rearrange("p (nb two h) -> p nb two h", two=2, h=h2)
            dv = pp[:, 1 - par, :].rearrange("p (nb two h) -> p nb two h", two=2, h=h2)
            ew(k, "vector", lambda e, sv=sv, dv=dv: e.tensor_tensor(
                out=dv[:, :, 0, :], in0=sv[:, :, 0, :], in1=sv[:, :, 1, :], op=ALU.add),
               r=[tg(pp, par)], w=[tg(pp, 1 - par)])
            ew(k, "vector", lambda e, sv=sv, dv=dv: e.tensor_tensor(
                out=dv[:, :, 1, :], in0=sv[:, :, 0, :], in1=sv[:, :, 1, :], op=ALU.subtract),
               r=[tg(pp, par)], w=[tg(pp, 1 - par)])
            par = 1 - par
        wait(k, "vector", k.s_pe, tm[17 + l])
        ew(k, "vector", lambda e, par=par: e.tensor_scalar_mul(
            out=m1[:, :], in0=pp[:, par, :], scalar1=SCL), r=[tg(pp, par)], w=[m1])
        bm[l] = milestone(k, "vector", k.s_v)
        tm[18 + l] = transpose_stage(64, l * 1048576, bm[l], dst_cc=in_cc2)

    wait(k, "gpsimd", k.s_g)
    k.m_ccw = k.s_cc.inc(1)
    ew(k, "gpsimd", lambda e, h=k.s_cc.h: e.collective_compute(
        "AllGather", ALU.bypass, replica_groups=[list(range(8))],
        ins=[cc_w_in.ap().opt()], outs=[k.cc_w_out_ap.opt()]).then_inc(h, 1))

    def phase0_tail():
        for j in range(8):
            for which, srcap in ((0, w1_sl), (1, w3_sl)):
                off = W13_OFF + (which * 8 + j) * 262144
                k.s_g.inc(16)
                ew(k, "gpsimd", lambda e, j=j, srcap=srcap, off=off, h=k.s_g.h: e.dma_start(
                    out=in_cc2[off:off + 262144].rearrange("(p m) -> p m", p=128),
                    in_=srcap[j]).then_inc(h, 16))
        wait(k, "gpsimd", k.s_g)
        k.m_ccw2 = k.s_cc.inc(1)
        ew(k, "gpsimd", lambda e, h=k.s_cc.h: e.collective_compute(
            "AllGather", ALU.bypass, replica_groups=[list(range(8))],
            ins=[cc_w2_in.ap().opt()], outs=[k.cc_w2_out_ap.opt()]).then_inc(h, 1))
    k.phase0_tail = phase0_tail
    es0.close()



def emit_ffn0(k, xres, xhat, bufA_t, bufB_t, bufC, bufC_t, wslab, ps_mm, ps_r,
              WP_OFF, RANK_SZ):
    occ = k.cc_w_out_ap

    def wgu_slab_ap(hc):
        r, l = divmod(hc, 16)
        off = r * RANK_SZ + l * 262144
        return occ[off:off + 262144].rearrange("(p m) -> p m", p=128)

    def wp_slab_ap(dc):
        r, l = divmod(dc, 2)
        off = r * RANK_SZ + WP_OFF + l * 1048576
        return occ[off:off + 1048576].rearrange("(p m) -> p m", p=128)

    hA = bufA_t[:, 0:16384].rearrange("p (a b) -> p a b", b=512)
    hB = bufB_t[:, 0:16384].rearrange("p (a b) -> p a b", b=512)

    def hbuf(fc):
        return (hA if fc < 32 else hB)[:, fc % 32, :]

    def htag(fc):
        return tg(bufA_t if fc < 32 else bufB_t, fc % 32)

    gslab = bufC_t[:, 0:8192].rearrange("p (a b c) -> p a b c", b=16, c=128)
    sqt = bufC_t[:, 8192:10240].rearrange("p (a b) -> p a b", a=4)
    sg = bufC_t[:, 10240:11264].rearrange("p (a b) -> p a b", a=2)
    tstat = bufC_t[:, 11264:12288].bitcast(F32)
    rsb = bufC_t[:, 12288:13312].rearrange("p (a b) -> p a b", a=2)
    pslab = bufC_t[:, 0:16384].rearrange("p (a b c) -> p a b c", b=64, c=128)
    ksl = (k.s_k, k.s_k2, k.s_k3, k.s_k4)

    wait(k, "scalar", k.s_v)
    wait(k, "vector", k.s_pe)
    emit_ln(k, xres, xhat, ((1, 512), (513, 512)), bufC)

    for th in range(2):
        cx = 1 + th * 512
        wait(k, "sync", k.s_pe)
        wait(k, "vector", k.s_pe)
        wait(k, "scalar", k.s_pe)
        wait(k, "tensor", k.s_v)
        wait(k, "tensor", k.s_sc)
        if th == 0:
            wait(k, "sync", k.s_cc, k.m_ccw)
        load_marks = {}

        def gu_load(fc, which):
            slot = (2 * fc + which) % 4
            sem = ksl[slot]
            lm = sem.inc(16)
            ew(k, "sync", lambda e, fc=fc, which=which, slot=slot, h=sem.h: e.dma_start(
                out=gslab[:, slot, :, :].rearrange("p a b -> p (a b)"),
                in_=wgu_slab_ap(fc + 64 * which)).then_inc(h, 16))
            load_marks[(fc, which)] = lm

        gu_load(0, 0)
        gu_load(0, 1)
        gu_load(1, 0)
        gu_load(1, 1)
        gmark = {}
        umark = {}
        hmark = {}
        smark = {}
        sqmark = {}
        ssmark = {}
        for fc in range(64):
            slot_g = (2 * fc) % 4
            slot_u = (2 * fc + 1) % 4
            wait(k, "tensor", ksl[slot_g], load_marks[(fc, 0)])
            if fc >= 2:
                wait(k, "tensor", k.s_sc, smark[fc - 2])
            for dc in range(NCD):
                ew(k, "tensor", lambda e, dc=dc, slot_g=slot_g, cx=cx: e.matmul(
                    ps_mm[:, slot_g % 4, :], gslab[:, slot_g, dc, :],
                    xhat[:, dc, cx:cx + 512], start=(dc == 0), stop=(dc == NCD - 1)))
            gmark[fc] = milestone(k, "tensor", k.s_pe)
            wait(k, "tensor", ksl[slot_u], load_marks[(fc, 1)])
            if fc >= 2:
                wait(k, "tensor", k.s_v, hmark[fc - 2])
            for dc in range(NCD):
                ew(k, "tensor", lambda e, dc=dc, slot_u=slot_u, cx=cx: e.matmul(
                    ps_mm[:, slot_u % 4, :], gslab[:, slot_u, dc, :],
                    xhat[:, dc, cx:cx + 512], start=(dc == 0), stop=(dc == NCD - 1)))
            umark[fc] = milestone(k, "tensor", k.s_pe)
            if fc >= 2:
                f2 = fc - 2
                wait(k, "tensor", k.s_sc, sqmark[f2])
                ew(k, "tensor", lambda e, f2=f2: e.matmul(
                    ps_r[:, 0, :], k.ones, sqt[:, f2 % 4, :],
                    start=(f2 == 0), stop=False))
                ssmark[f2] = milestone(k, "tensor", k.s_pe)
            if fc + 2 < 64:
                wait(k, "sync", k.s_pe, gmark[fc])
                gu_load(fc + 2, 0)
                wait(k, "sync", k.s_pe, umark[fc])
                gu_load(fc + 2, 1)
            wait(k, "scalar", k.s_pe, gmark[fc])
            if fc >= 2:
                wait(k, "scalar", k.s_v, hmark[fc - 2])
            ew(k, "scalar", lambda e, fc=fc, slot_g=slot_g: e.activation(
                out=sg[:, fc % 2, :], in_=ps_mm[:, slot_g % 4, :], func=AF.Silu),
               w=[f"bufC:sg{fc % 2}"])
            smark[fc] = milestone(k, "scalar", k.s_sc)
            wait(k, "vector", k.s_pe, umark[fc])
            wait(k, "vector", k.s_sc, smark[fc])
            ew(k, "vector", lambda e, fc=fc, slot_u=slot_u: e.tensor_tensor(
                out=hbuf(fc), in0=sg[:, fc % 2, :], in1=ps_mm[:, slot_u % 4, :],
                op=ALU.mult), r=[f"bufC:sg{fc % 2}"], w=[htag(fc)])
            hmark[fc] = milestone(k, "vector", k.s_v)
            if fc >= 4:
                wait(k, "scalar", k.s_pe, ssmark[fc - 4])
            wait(k, "scalar", k.s_v, hmark[fc])
            ew(k, "scalar", lambda e, fc=fc: e.activation(
                out=sqt[:, fc % 4, :], in_=hbuf(fc), func=AF.Square),
               w=[f"bufC:sq{fc % 4}"])
            sqmark[fc] = milestone(k, "scalar", k.s_sc)
        for f2 in (62, 63):
            wait(k, "tensor", k.s_sc, sqmark[f2])
            ew(k, "tensor", lambda e, f2=f2: e.matmul(
                ps_r[:, 0, :], k.ones, sqt[:, f2 % 4, :], start=False, stop=(f2 == 63)))
        mss = milestone(k, "tensor", k.s_pe)
        wait(k, "vector", k.s_pe, mss)
        wait(k, "vector", k.s_sc)
        ew(k, "vector", lambda e: e.tensor_scalar(
            out=tstat[:, 0:512], in0=ps_r[:, 0, :], scalar1=1.0 / FFH, scalar2=RMS_EPS,
            op0=ALU.mult, op1=ALU.add), w=["bufC:ts"])
        mvst = milestone(k, "vector", k.s_v)
        wait(k, "scalar", k.s_v, mvst)
        ew(k, "scalar", lambda e: e.activation(
            out=tstat[:, 0:512], in_=tstat[:, 0:512], func=AF.Sqrt),
           r=["bufC:ts"], w=["bufC:ts"])
        msqr = milestone(k, "scalar", k.s_sc)
        wait(k, "vector", k.s_sc, msqr)

        def _recipr(e, th=th, nc_=k.nc):
            with nc_.allow_low_precision(reason="rms rstd tile stored bf16"):
                return e.reciprocal(out=rsb[:, th, :], in_=tstat[:, 0:512])
        ew(k, "vector", _recipr, r=["bufC:ts"], w=["bufC:rs"])
        for fc in range(64):
            ew(k, "vector", lambda e, fc=fc, th=th: e.tensor_tensor(
                out=hbuf(fc), in0=hbuf(fc), in1=rsb[:, th, :], op=ALU.mult),
               r=[htag(fc), "bufC:rs"], w=[htag(fc)])
        mhn = milestone(k, "vector", k.s_v)
        wait(k, "tensor", k.s_v, mhn)

        def evac_ffn(oc, si, c0_, wd, ps, th=th, cx=cx):
            ew(k, "vector", lambda e, oc=oc, ps=ps, cx=cx: e.tensor_tensor(
                out=xres[:, oc, cx:cx + 512], in0=xres[:, oc, cx:cx + 512],
                in1=ps, op=ALU.add),
               r=[tg(xres, f"f{oc}.{th}")], w=[tg(xres, f"f{oc}.{th}")])

        emit_proj(k, 16, 64, wp_slab_ap, lambda dc, c0_, wd: hbuf(dc),
                  evac_ffn, "vector", pslab, ps_mm, 4, segs=((0, 512),),
                  w_queue="sync", w_sems=(k.s_k, k.s_k2), load_wait=(k.s_v, mhn))





def emit_l1attn(k, xres, xhat, bufA, bufB, bufC, bufC_t, wslab, ps_mm,
                wq_t, wk_t, wv_t, wo_t, ropec_d, band_d, killv, cc_kv_in, cc_kv_out):
    nc = k.nc
    ropec = bufC_t[:, 14016:16064].rearrange("p (a b) -> p a b", a=2)
    band = bufC_t[:, 12864:14016]
    KHD = 131072
    kin = cc_kv_in.ap()
    kout = cc_kv_out.ap()

    def kv_in_K(h):
        return kin[h * KHD:(h + 1) * KHD].rearrange("(p m) -> p m", p=128)

    def kv_in_V(h):
        off = 16 * KHD + h * KHD
        return kin[off:off + KHD].rearrange("(p m) -> p m", p=128)

    def kv_out_K(h):
        return kout[h * KHD:(h + 1) * KHD].rearrange("(p m) -> p m", p=128)

    def kv_out_V(h):
        off = 16 * KHD + h * KHD
        return kout[off:off + KHD].rearrange("(p m) -> p m", p=128)

    ps_s = ps_mm[:, 0:3, :].rearrange("p a b -> p (a b)")
    ps_pt = ps_mm[:, 3, :].bitcast(BF16)
    ps_pt2 = k.ps_ln[:, 0, 0:64].bitcast(BF16)
    ps_ot = [k.ps_ln[:, 1, 0:128], k.ps_r[:, 0, 0:128]]

    wait(k, "scalar", k.s_v)
    wait(k, "scalar", k.s_pe)
    wait(k, "vector", k.s_pe)
    emit_ln(k, xres, xhat, ((1, 512), (513, 512)), bufC)

    def proj_rhs_x(dc, c0, wd):
        return xhat[:, dc, 1 + c0:1 + c0 + wd]

    def evac_copy_to(buf):
        def f(oc, si, c0, wd, ps):
            ew(k, "vector", lambda e, oc=oc, c0=c0, wd=wd, ps=ps: e.tensor_copy(
                out=buf[:, oc, c0:c0 + wd], in_=ps), w=[tg(buf, f"a{oc}")])
        return f

    def emit_rope(buf, h):
        wait(k, "tensor", k.s_v)
        for s2 in range(2):
            ew(k, "tensor", lambda e, h=h, s2=s2, buf=buf: e.matmul(
                ps_mm[:, 2 + s2, :], k.swap, buf[:, h, s2 * 512:(s2 + 1) * 512],
                start=True, stop=True))
        mpe = milestone(k, "tensor", k.s_pe)
        ew(k, "vector", lambda e, h=h, buf=buf: e.tensor_tensor(
            out=bufC[:, 0, 0:T], in0=buf[:, h, 0:T], in1=ropec[:, 0, :], op=ALU.mult),
           r=[tg(buf, f"a{h}")], w=[tg(bufC_t, "rt0")])
        wait(k, "vector", k.s_pe, mpe)
        for s2 in range(2):
            ew(k, "vector", lambda e, s2=s2: e.tensor_tensor(
                out=bufC[:, 1, s2 * 512:(s2 + 1) * 512], in0=ps_mm[:, 2 + s2, :],
                in1=ropec[:, 1, s2 * 512:(s2 + 1) * 512], op=ALU.mult),
               w=[tg(bufC_t, "rt1")])
        ew(k, "vector", lambda e, h=h, buf=buf: e.tensor_tensor(
            out=buf[:, h, 0:T], in0=bufC[:, 0, 0:T], in1=bufC[:, 1, 0:T], op=ALU.add),
           r=[tg(bufC_t, "rt0"), tg(bufC_t, "rt1")], w=[tg(buf, f"a{h}")])
        return milestone(k, "vector", k.s_v)

    wait(k, "sync", k.s_pe)
    m_rc1 = k.s_ld.inc(16)
    ew(k, "sync", lambda e, h=k.s_ld.h: e.dma_start(
        out=ropec, in_=ropec_d).then_inc(h, 16))
    wait(k, "vector", k.s_ld, m_rc1)
    wait(k, "tensor", k.s_v)
    emit_proj(k, 16, 16, lambda oc: wk_t[oc], proj_rhs_x,
              evac_copy_to(bufB), "vector", wslab, ps_mm, 2)
    for h in range(NH):
        mr = emit_rope(bufB, h)
        k.s_g.inc(16)
        wait(k, "gpsimd", k.s_v, mr)
        ew(k, "gpsimd", lambda e, h=h, hh=k.s_g.h: e.dma_start(
            out=kv_in_K(h), in_=bufB[:, h, 0:T]).then_inc(hh, 16))

    wait(k, "vector", k.s_pe)
    emit_proj(k, 16, 16, lambda oc: wv_t[oc], proj_rhs_x,
              evac_copy_to(bufC), "vector", wslab, ps_mm, 2)
    ps_vt = ps_mm[:, 2:4, :].rearrange("p a b -> p (a b)").bitcast(BF16)
    wait(k, "tensor", k.s_v)
    mv_prev = None
    for h in range(NH):
        if mv_prev is not None:
            wait(k, "tensor", k.s_v, mv_prev)
        for t8 in range(8):
            ew(k, "tensor", lambda e, h=h, t8=t8: e.transpose(
                ps_vt[:, t8 * 128:(t8 + 1) * 128],
                bufC[:, h, t8 * 128:(t8 + 1) * 128], k.ident))
        mpe = milestone(k, "tensor", k.s_pe)
        wait(k, "vector", k.s_pe, mpe)
        if h > 0:
            wait(k, "vector", k.s_g)
        ew(k, "vector", lambda e, h=h: e.tensor_copy(
            out=bufC[:, h, 0:T], in_=ps_vt[:, 0:1024]), w=[tg(bufC_t, f"v{h}")])
        mv = milestone(k, "vector", k.s_v)
        mv_prev = mv
        k.s_g.inc(16)
        wait(k, "gpsimd", k.s_v, mv)
        ew(k, "gpsimd", lambda e, h=h, hh=k.s_g.h: e.dma_start(
            out=kv_in_V(h), in_=bufC[:, h, 0:T]).then_inc(hh, 16))

    wait(k, "gpsimd", k.s_g)
    m_kv = k.s_cc.inc(1)
    ew(k, "gpsimd", lambda e, hh=k.s_cc.h: e.collective_compute(
        "AllGather", ALU.bypass, replica_groups=PAIRS,
        ins=[cc_kv_in.ap().opt()], outs=[cc_kv_out.ap().opt()]).then_inc(hh, 1))

    emit_proj(k, 16, 16, lambda oc: wq_t[oc], proj_rhs_x,
              evac_copy_to(bufA), "vector", wslab, ps_mm, 2)
    wait(k, "sync", k.s_g)
    m_rc2 = k.s_ld.inc(32)
    ew(k, "sync", lambda e, h=k.s_ld.h: e.dma_start(
        out=ropec, in_=ropec_d).then_inc(h, 16))
    ew(k, "sync", lambda e, h=k.s_ld.h: e.dma_start(
        out=band[:, :], in_=band_d).then_inc(h, 16))
    wait(k, "vector", k.s_ld, m_rc2)
    wait(k, "vector", k.s_g)
    rope_q_marks = {}
    for h in range(NH):
        rope_q_marks[h] = emit_rope(bufA, h)

    kbuf = bufC_t[:, 0:4096].rearrange("p (s m) -> p s m", s=2)
    vbuf = bufC_t[:, 4096:8192].rearrange("p (s a c) -> p s a c", s=2, a=16)
    Pt = bufC_t[:, 8192:10496].rearrange("p (s m) -> p s m", s=2)
    ptile = bufC_t[:, 10496:12800].rearrange("p (s m) -> p s m", s=2)
    smal = bufC_t[:, 12800:12864].bitcast(F32)

    wait(k, "vector", k.s_pe)
    wait(k, "sync", k.s_cc, m_kv)
    wait(k, "sync", k.s_v)
    wait(k, "sync", k.s_g)

    NIT = NH * 8
    kvsem = (k.s_k3, k.s_k4)
    kvload = {}

    def kv_load(h):
        sl = h % 2
        sem = kvsem[sl]
        ew(k, "sync", lambda e, h=h, sl=sl, hh=sem.h: e.dma_start(
            out=kbuf[:, sl, 0:1024], in_=kv_out_K(h)).then_inc(hh, 16))
        ew(k, "sync", lambda e, h=h, sl=sl, hh=sem.h: e.dma_start(
            out=kbuf[:, sl, 1024:2048], in_=bufB[:, h, 0:T]).then_inc(hh, 16))
        ew(k, "sync", lambda e, h=h, sl=sl, hh=sem.h: e.dma_start(
            out=vbuf[:, sl, 0:8, :].rearrange("p a c -> p (a c)"),
            in_=kv_out_V(h)).then_inc(hh, 16))
        ew(k, "sync", lambda e, h=h, sl=sl, hh=sem.h: e.dma_start(
            out=vbuf[:, sl, 8:16, :].rearrange("p a c -> p (a c)"),
            in_=kv_in_V(h)).then_inc(hh, 16))
        kvload[h] = sem.inc(64)

    kv_load(0)
    kv_load(1)

    ps_marks = {}
    mx_marks = {}
    exp_marks = {}
    pn_marks = {}
    pt_marks = {}
    pte_marks = {}
    av_marks = {}
    oe_marks = {}

    for i in range(NIT + 2):
        h, qt = divmod(i, 8)
        if i < NIT:
            sl = h % 2
            if qt == 0:
                wait(k, "tensor", kvsem[sl], kvload[h])
                wait(k, "tensor", k.s_v, rope_q_marks[h])
            if i >= 1:
                wait(k, "tensor", k.s_sc, exp_marks[i - 1])
            ws = qt * 128
            ew(k, "tensor", lambda e, h=h, qt=qt, sl=sl, ws=ws: e.matmul(
                ps_s[:, 0:512], bufA[:, h, qt * 128:(qt + 1) * 128],
                kbuf[:, sl, ws:ws + 512], start=True, stop=False))
            ew(k, "tensor", lambda e, h=h, qt=qt, sl=sl, ws=ws: e.matmul(
                ps_s[:, 512:1024], bufA[:, h, qt * 128:(qt + 1) * 128],
                kbuf[:, sl, ws + 512:ws + 1024], start=True, stop=(qt >= 4)))
            ew(k, "tensor", lambda e, h=h, qt=qt, sl=sl, ws=ws: e.matmul(
                ps_s[:, 1024:1152], bufA[:, h, qt * 128:(qt + 1) * 128],
                kbuf[:, sl, ws + 1024:ws + 1152], start=True, stop=False))
            ew(k, "tensor", lambda e: e.matmul(
                ps_s[:, 0:512], k.ident, band[:, 0:512], start=False, stop=False))
            ew(k, "tensor", lambda e: e.matmul(
                ps_s[:, 1024:1152], k.ident, band[:, 1024:1152], start=False, stop=True))
            ew(k, "tensor", lambda e, ws=ws: e.matmul(
                ps_s[:, 0:512], k.ones[0:1, :], killv[0:1, ws:ws + 512],
                start=False, stop=True))
            if qt < 4:
                ew(k, "tensor", lambda e, ws=ws: e.matmul(
                    ps_s[:, 512:1024], k.ones[0:1, :], killv[0:1, ws + 512:ws + 1024],
                    start=False, stop=True))
            ps_marks[i] = milestone(k, "tensor", k.s_pe)
            if qt == 2 and 2 <= h + 1 < NH:
                wait(k, "sync", k.s_pe, av_marks[(h - 1) * 8 + 7])
                kv_load(h + 1)
        j = i - 1
        if 0 <= j < NIT:
            wait(k, "tensor", k.s_sc, pn_marks[j])
            if j >= 1:
                wait(k, "tensor", k.s_v, pte_marks[j - 1])
            for t9 in range(8):
                ew(k, "tensor", lambda e, j=j, t9=t9: e.transpose(
                    ps_pt[:, t9 * 128:(t9 + 1) * 128],
                    Pt[:, j % 2, t9 * 128:(t9 + 1) * 128], k.ident))
            ew(k, "tensor", lambda e, j=j: e.transpose(
                ps_pt2[:, 0:128], Pt[:, j % 2, 1024:1152], k.ident))
            pt_marks[j] = milestone(k, "tensor", k.s_pe)
        m = i - 2
        if m >= 0:
            hm, qtm = divmod(m, 8)
            slm = hm % 2
            wait(k, "tensor", k.s_v, pte_marks[m])
            if m >= 2:
                wait(k, "tensor", k.s_v, oe_marks[m - 2])
            for t9 in range(9):
                ew(k, "tensor", lambda e, m=m, t9=t9, slm=slm, qtm=qtm: e.matmul(
                    ps_ot[m % 2], vbuf[:, slm, qtm + t9, :],
                    ptile[:, m % 2, t9 * 128:(t9 + 1) * 128],
                    start=(t9 == 0), stop=(t9 == 8)))
            av_marks[m] = milestone(k, "tensor", k.s_pe)
        if i < NIT:
            c0s = (i % 2) * 8
            wait(k, "vector", k.s_pe, ps_marks[i])
            ew(k, "vector", lambda e, c0s=c0s: e.reduce_max(
                out=smal[:, c0s:c0s + 1], in_=ps_s[:, 0:1152], axis=AX.X),
               w=[tg(bufC_t, f"sm{i % 2}")])
            ew(k, "vector", lambda e, c0s=c0s: e.tensor_scalar_mul(
                out=smal[:, c0s + 1:c0s + 2], in0=smal[:, c0s:c0s + 1],
                scalar1=-ATT_SCALE),
               r=[tg(bufC_t, f"sm{i % 2}")], w=[tg(bufC_t, f"sm{i % 2}")])
            mx_marks[i] = milestone(k, "vector", k.s_v)
            wait(k, "scalar", k.s_v, mx_marks[i])
            ew(k, "scalar", lambda e, i=i, c0s=c0s: e.activation(
                out=Pt[:, i % 2, :], in_=ps_s[:, 0:1152], func=AF.Exp,
                scale=ATT_SCALE, bias=smal[:, c0s + 1:c0s + 2],
                accum_out=smal[:, c0s + 2:c0s + 3]),
               r=[tg(bufC_t, f"sm{i % 2}")],
               w=[tg(bufC_t, f"p{i % 2}"), tg(bufC_t, f"sm{i % 2}")])
            exp_marks[i] = milestone(k, "scalar", k.s_sc)
            wait(k, "vector", k.s_sc, exp_marks[i])

            def _rrin(e, c0s=c0s, nc_=k.nc):
                with nc_.allow_low_precision(reason="softmax denom reciprocal"):
                    return e.reciprocal(out=smal[:, c0s + 3:c0s + 4],
                                        in_=smal[:, c0s + 2:c0s + 3])
            ew(k, "vector", _rrin, r=[tg(bufC_t, f"sm{i % 2}")],
               w=[tg(bufC_t, f"sm{i % 2}")])
            mrin = milestone(k, "vector", k.s_v)
            wait(k, "scalar", k.s_v, mrin)
            ew(k, "scalar", lambda e, i=i, c0s=c0s: e.activation(
                out=Pt[:, i % 2, :], in_=Pt[:, i % 2, :], func=AF.Copy,
                scale=smal[:, c0s + 3:c0s + 4]),
               r=[tg(bufC_t, f"p{i % 2}"), tg(bufC_t, f"sm{i % 2}")],
               w=[tg(bufC_t, f"p{i % 2}")])
            pn_marks[i] = milestone(k, "scalar", k.s_sc)
        if 0 <= j < NIT:
            wait(k, "vector", k.s_pe, pt_marks[j])
            ew(k, "vector", lambda e, j=j: e.tensor_copy(
                out=ptile[:, j % 2, 0:1024], in_=ps_pt[:, :]),
               w=[tg(bufC_t, f"pt{j % 2}")])
            ew(k, "vector", lambda e, j=j: e.tensor_copy(
                out=ptile[:, j % 2, 1024:1152], in_=ps_pt2[:, 0:128]),
               w=[tg(bufC_t, f"pt{j % 2}")])
            pte_marks[j] = milestone(k, "vector", k.s_v)
        if m >= 0:
            hm, qtm = divmod(m, 8)
            wait(k, "vector", k.s_pe, av_marks[m])
            ew(k, "vector", lambda e, m=m, hm=hm, qtm=qtm: e.tensor_copy(
                out=xhat[:, hm, qtm * 128:(qtm + 1) * 128], in_=ps_ot[m % 2]),
               w=[tg(xhat, f"at{hm}.{qtm}")])
            oe_marks[m] = milestone(k, "vector", k.s_v)

    wait(k, "tensor", k.s_v)

    def evac_addres1(oc, si, c0, wd, ps):
        ew(k, "vector", lambda e, oc=oc, c0=c0, wd=wd, ps=ps: e.tensor_tensor(
            out=xres[:, oc, 1 + c0:1 + c0 + wd], in0=xres[:, oc, 1 + c0:1 + c0 + wd],
            in1=ps, op=ALU.add), r=[tg(xres, f"o{oc}.{si}")], w=[tg(xres, f"o{oc}.{si}")])

    emit_proj(k, 16, 16, lambda oc: wo_t[oc],
              lambda dc, c0, wd: xhat[:, dc, c0:c0 + wd],
              evac_addres1, "vector", wslab, ps_mm, 2)




def emit_ffn1(k, xres, xhat, bufA_t, bufB_t, bufC, bufC_t, ps_mm,
              W13_OFF, RANK2_SZ):
    occ2 = k.cc_w2_out_ap

    def w2h_slab_ap(dc):
        r, l = divmod(dc, 2)
        off = r * RANK2_SZ + l * 1048576
        return occ2[off:off + 1048576].rearrange("(p m) -> p m", p=128)

    def w13_slab_ap(fc, which):
        r, j = divmod(fc, 8)
        off = r * RANK2_SZ + W13_OFF + (which * 8 + j) * 262144
        return occ2[off:off + 262144].rearrange("(p m) -> p m", p=128)

    hA = bufA_t[:, 0:16384].rearrange("p (a b) -> p a b", b=512)
    hB = bufB_t[:, 0:16384].rearrange("p (a b) -> p a b", b=512)

    def hbuf(fc):
        return (hA if fc < 32 else hB)[:, fc % 32, :]

    def htag(fc):
        return tg(bufA_t if fc < 32 else bufB_t, fc % 32)

    gslab = bufC_t[:, 0:8192].rearrange("p (a b c) -> p a b c", a=4, c=128)
    sg = bufC_t[:, 10240:11264].rearrange("p (a b) -> p a b", a=2)
    pslab = bufC_t[:, 0:16384].rearrange("p (a b c) -> p a b c", b=64, c=128)
    wsl = (k.s_k, k.s_k2, k.s_k3, k.s_k4)

    wait(k, "scalar", k.s_v)
    wait(k, "scalar", k.s_pe)
    wait(k, "vector", k.s_pe)
    emit_ln(k, xres, xhat, ((1, 512), (513, 512)), bufC)

    for th in range(2):
        cx = 1 + th * 512
        wait(k, "sync", k.s_pe)
        wait(k, "vector", k.s_pe)
        wait(k, "scalar", k.s_pe)
        wait(k, "tensor", k.s_v)
        wait(k, "tensor", k.s_sc)
        if th == 0:
            wait(k, "sync", k.s_cc, k.m_ccw2)
        load_marks = {}

        def w_load(fc, which):
            slot = (2 * fc + which) % 4
            sem = wsl[slot]
            lm = sem.inc(16)
            ew(k, "sync", lambda e, fc=fc, which=which, slot=slot, h=sem.h: e.dma_start(
                out=gslab[:, slot, :, :].rearrange("p a b -> p (a b)"),
                in_=w13_slab_ap(fc, which)).then_inc(h, 16))
            load_marks[(fc, which)] = lm

        w_load(0, 0)
        w_load(0, 1)
        w_load(1, 0)
        w_load(1, 1)
        gmark = {}
        umark = {}
        hmark = {}
        smark = {}
        for fc in range(64):
            slot_g = (2 * fc) % 4
            slot_u = (2 * fc + 1) % 4
            wait(k, "tensor", wsl[slot_g], load_marks[(fc, 0)])
            if fc >= 2:
                wait(k, "tensor", k.s_sc, smark[fc - 2])
            for dc in range(NCD):
                ew(k, "tensor", lambda e, dc=dc, slot_g=slot_g, cx=cx: e.matmul(
                    ps_mm[:, slot_g, :], gslab[:, slot_g, dc, :],
                    xhat[:, dc, cx:cx + 512], start=(dc == 0), stop=(dc == NCD - 1)))
            gmark[fc] = milestone(k, "tensor", k.s_pe)
            wait(k, "tensor", wsl[slot_u], load_marks[(fc, 1)])
            if fc >= 2:
                wait(k, "tensor", k.s_v, hmark[fc - 2])
            for dc in range(NCD):
                ew(k, "tensor", lambda e, dc=dc, slot_u=slot_u, cx=cx: e.matmul(
                    ps_mm[:, slot_u, :], gslab[:, slot_u, dc, :],
                    xhat[:, dc, cx:cx + 512], start=(dc == 0), stop=(dc == NCD - 1)))
            umark[fc] = milestone(k, "tensor", k.s_pe)
            if fc + 2 < 64:
                wait(k, "sync", k.s_pe, gmark[fc])
                w_load(fc + 2, 0)
                wait(k, "sync", k.s_pe, umark[fc])
                w_load(fc + 2, 1)
            wait(k, "scalar", k.s_pe, gmark[fc])
            if fc >= 2:
                wait(k, "scalar", k.s_v, hmark[fc - 2])
            ew(k, "scalar", lambda e, fc=fc, slot_g=slot_g: e.activation(
                out=sg[:, fc % 2, :], in_=ps_mm[:, slot_g, :], func=AF.Silu),
               w=[f"bufC:sg{fc % 2}"])
            smark[fc] = milestone(k, "scalar", k.s_sc)
            wait(k, "vector", k.s_pe, umark[fc])
            wait(k, "vector", k.s_sc, smark[fc])
            ew(k, "vector", lambda e, fc=fc, slot_u=slot_u: e.tensor_tensor(
                out=hbuf(fc), in0=sg[:, fc % 2, :], in1=ps_mm[:, slot_u, :],
                op=ALU.mult), r=[f"bufC:sg{fc % 2}"], w=[htag(fc)])
            hmark[fc] = milestone(k, "vector", k.s_v)
        mhn = milestone(k, "vector", k.s_v)
        wait(k, "tensor", k.s_v, mhn)

        def evac_ffn1(oc, si, c0_, wd, ps, th=th, cx=cx):
            ew(k, "vector", lambda e, oc=oc, ps=ps, cx=cx: e.tensor_tensor(
                out=xres[:, oc, cx:cx + 512], in0=xres[:, oc, cx:cx + 512],
                in1=ps, op=ALU.add),
               r=[tg(xres, f"g{oc}.{th}")], w=[tg(xres, f"g{oc}.{th}")])

        emit_proj(k, 16, 64, w2h_slab_ap, lambda dc, c0_, wd: hbuf(dc),
                  evac_ffn1, "vector", pslab, ps_mm, 4, segs=((0, 512),),
                  w_queue="sync", w_sems=(k.s_k, k.s_k2), load_wait=(k.s_v, mhn))



def build(debug_stop=None):
    nc = bass.Bass("TRN2", target_bir_lowering=False)
    es = ExitStack()

    xT = nc.dram_tensor("xT", [NCD, 128, W], F32, kind="ExternalInput").ap()
    wr_t = nc.dram_tensor("wr_t", [16, 128, D], F32, kind="ExternalInput").ap()
    wk_t = nc.dram_tensor("wk_t", [16, 128, D], F32, kind="ExternalInput").ap()
    wv_t = nc.dram_tensor("wv_t", [16, 128, D], F32, kind="ExternalInput").ap()
    wo_t = nc.dram_tensor("wo_t", [16, 128, D], F32, kind="ExternalInput").ap()
    vecs = nc.dram_tensor("vecs", [128, 6, 16], F32, kind="ExternalInput").ap()
    cid = nc.dram_tensor("cid", [128, 3, 128], BF16, kind="ExternalInput").ap()
    outT = nc.dram_tensor("outT", [NCD, 128, T], F32, kind="ExternalOutput").ap()

    l1wq_t = nc.dram_tensor("l1wq_t", [16, 128, D], F32, kind="ExternalInput").ap()
    l1wk_t = nc.dram_tensor("l1wk_t", [16, 128, D], F32, kind="ExternalInput").ap()
    l1wv_t = nc.dram_tensor("l1wv_t", [16, 128, D], F32, kind="ExternalInput").ap()
    l1wo_t = nc.dram_tensor("l1wo_t", [16, 128, D], F32, kind="ExternalInput").ap()
    w1_sl = nc.dram_tensor("w1_sl", [8, 128, D], F32, kind="ExternalInput").ap()
    w3_sl = nc.dram_tensor("w3_sl", [8, 128, D], F32, kind="ExternalInput").ap()
    ropec_d = nc.dram_tensor("ropec_d", [128, 2, 1024], BF16, kind="ExternalInput").ap()
    band_d = nc.dram_tensor("band_d", [128, 1152], BF16, kind="ExternalInput").ap()
    killv_d = nc.dram_tensor("killv_d", [1, 2048], BF16, kind="ExternalInput").ap()
    wgu_slab = nc.dram_tensor("wgu_slab", [2048, 2048], F32, kind="ExternalInput").ap()
    wproj_slab = nc.dram_tensor("wproj_slab", [256, FFH], F32, kind="ExternalInput").ap()
    w2_slab = nc.dram_tensor("w2_slab", [256, FFH], F32, kind="ExternalInput").ap()

    cc_s_in = nc.dram_tensor("cc_s_in", [128, 16], F32)
    cc_s_out = nc.dram_tensor("cc_s_out", [256, 16], F32)
    RANK_SZ = 16 * 262144 + 2 * 1048576
    WP_OFF = 16 * 262144
    RANK2_SZ = 2 * 1048576 + 16 * 262144
    W13_OFF = 2 * 1048576
    cc_w_in = nc.dram_tensor("cc_w_in", [RANK_SZ], BF16)
    cc_w_out = nc.dram_tensor("cc_w_out", [8 * RANK_SZ], BF16)
    cc_w2_in = nc.dram_tensor("cc_w2_in", [RANK2_SZ], BF16)
    cc_w2_out = nc.dram_tensor("cc_w2_out", [8 * RANK2_SZ], BF16)
    cc_kv_in = nc.dram_tensor("cc_kv_in", [32 * 131072], BF16)
    cc_kv_out = nc.dram_tensor("cc_kv_out", [64 * 131072], BF16)
    k_rank_sz = RANK_SZ

    k = K()
    k.nc = nc
    k.em = Em()

    for s in ("s_ld", "s_w", "s_w2", "s_w3", "s_w4", "s_k", "s_k2", "s_k3", "s_k4", "s_pe", "s_v", "s_sc", "s_g", "s_cc", "s_st"):
        setattr(k, s, Sem(es.enter_context(nc.semaphore(s))))

    onesid = es.enter_context(nc.sbuf_tensor("onesid", [128, 3, 128], BF16))[:]
    k.ones = onesid[:, 0, :]
    k.ident = onesid[:, 1, :]
    k.swap = onesid[:, 2, :]
    killv = es.enter_context(nc.sbuf_tensor("killv", [1, 2048], BF16))[:]
    vtile = es.enter_context(nc.sbuf_tensor("vtile", [128, 10, 16], F32))[:]
    stile = es.enter_context(nc.sbuf_tensor("stile", [128, 3, 16], F32))[:]

    k.s_ld.inc(16)
    ew(k, "sync", lambda e, h=k.s_ld.h: e.dma_start(out=onesid, in_=cid).then_inc(h, 16))
    k.s_ld.inc(16)
    ew(k, "sync", lambda e, h=k.s_ld.h: e.dma_start(
        out=vtile[:, 0:6, :], in_=vecs).then_inc(h, 16))
    k.s_ld.inc(16)
    ew(k, "sync", lambda e, h=k.s_ld.h: e.dma_start(out=killv[:], in_=killv_d).then_inc(h, 16))
    m_in = k.s_ld.n
    for e in ("scalar", "vector", "tensor"):
        wait(k, e, k.s_ld, m_in)

    for i in range(3):
        ew(k, "vector", lambda e, i=i: e.tensor_scalar(
            out=vtile[:, 6 + i, :], in0=vtile[:, i, :], scalar1=-1.0, scalar2=1.0,
            op0=ALU.mult, op1=ALU.add), r=[vtile], w=[vtile])
    ew(k, "scalar", lambda e: e.activation(out=vtile[:, 9, :], in_=vtile[:, 3, :],
                                           func=AF.Exp), r=[vtile], w=[vtile])
    ew(k, "scalar", lambda e: e.activation(out=vtile[:, 9, :], in_=vtile[:, 9, :],
                                           func=AF.Exp, scale=-1.0), r=[vtile], w=[vtile])
    m_vec_s = milestone(k, "scalar", k.s_sc)
    milestone(k, "vector", k.s_v)

    k.cc_w_out_ap = cc_w_out.ap()
    k.cc_w2_out_ap = cc_w2_out.ap()
    emit_phase0(k, wgu_slab, wproj_slab, w2_slab, cc_w_in, WP_OFF,
                cc_w2_in, w1_sl, w3_sl, W13_OFF)
    m_p0 = k.s_g.n

    xres = es.enter_context(nc.sbuf_tensor("xres", [128, NCD, W], F32))[:]
    xhat = es.enter_context(nc.sbuf_tensor("xhat", [128, NCD, W], BF16))[:]
    bufA_t = es.enter_context(nc.sbuf_tensor("bufA", [128, NCD * W], BF16))[:]
    bufA = bufA_t.rearrange("p (a b) -> p a b", b=W)
    bufB_t = es.enter_context(nc.sbuf_tensor("bufB", [128, NCD * W], BF16))[:]
    bufB = bufB_t.rearrange("p (a b) -> p a b", b=W)
    bufC_t = es.enter_context(nc.sbuf_tensor("bufC", [128, NCD * W], BF16))[:]
    bufC = bufC_t.rearrange("p (a b) -> p a b", b=W)
    k.stat_scr = bufA_t[:, 0:3072].bitcast(F32).rearrange("p (a b) -> p a b", a=3)
    k.rb = bufA_t[:, 3072:5120].rearrange("p (a b) -> p a b", a=2)
    k.mrb = bufA_t[:, 5120:7168].rearrange("p (a b) -> p a b", a=2)
    wslab = es.enter_context(nc.sbuf_tensor("wslab", [128, 2, 16, 128], BF16))[:]

    k.ps_ln = es.enter_context(nc.psum_tensor("ps_ln", [128, 2, 512], F32))[:]
    ps_mm = es.enter_context(nc.psum_tensor("ps_mm", [128, 4, 512], F32))[:]
    ps_r = es.enter_context(nc.psum_tensor("ps_r", [128, 1, 512], F32))[:]
    k.ps_r = ps_r

    wait(k, "sync", k.s_g, m_p0)
    for c in range(NCD):
        k.s_ld.inc(16)
        ew(k, "sync", lambda e, c=c, h=k.s_ld.h: e.dma_start(
            out=xres[:, c, :], in_=xT[c]).then_inc(h, 16))
    m_in = k.s_ld.n
    for e in ("scalar", "vector", "tensor"):
        wait(k, e, k.s_ld, m_in)

    emit_ln(k, xres, xhat, ((0, 512), (512, 512), (1024, 1)), bufC)
    wait(k, "vector", k.s_sc, m_vec_s)

    def emit_mix(tm_i, om_i, dstbuf):
        for c in range(NCD):
            ew(k, "vector", lambda e, c=c, om_i=om_i, dstbuf=dstbuf: e.tensor_scalar_mul(
                out=dstbuf[:, c, 0:T], in0=xhat[:, c, 0:T],
                scalar1=vtile[:, om_i, c:c + 1]), r=[xhat, vtile], w=[tg(dstbuf, c)])
        for c in range(NCD):
            ew(k, "vector", lambda e, c=c, tm_i=tm_i, dstbuf=dstbuf: e.scalar_tensor_tensor(
                out=dstbuf[:, c, 0:T], in0=xhat[:, c, 1:W], scalar=vtile[:, tm_i, c:c + 1],
                in1=dstbuf[:, c, 0:T], op0=ALU.mult, op1=ALU.add),
               r=[xhat, vtile, tg(dstbuf, c)], w=[tg(dstbuf, c)])
        return milestone(k, "vector", k.s_v)

    def proj_rhs(buf):
        return lambda dc, c0, wd: buf[:, dc, c0:c0 + wd]

    def evac_copy_to(buf, act=None):
        def f(oc, si, c0, wd, ps):
            if act is None:
                ew(k, "vector", lambda e, oc=oc, c0=c0, wd=wd, ps=ps: e.tensor_copy(
                    out=buf[:, oc, c0:c0 + wd], in_=ps), w=[tg(buf, oc)])
            else:
                ew(k, "scalar", lambda e, oc=oc, c0=c0, wd=wd, ps=ps: e.activation(
                    out=buf[:, oc, c0:c0 + wd], in_=ps, func=act), w=[tg(buf, oc)])
        return f

    mk = emit_mix(0, 6, bufB)
    mvx = emit_mix(1, 7, bufA)
    wait(k, "tensor", k.s_v, mk)
    mm_k = emit_proj(k, 16, 16, lambda oc: wk_t[oc], proj_rhs(bufB),
                     evac_copy_to(bufC), "vector", wslab, ps_mm, 4)
    wait(k, "tensor", k.s_v, mvx)
    mm_v = emit_proj(k, 16, 16, lambda oc: wv_t[oc], proj_rhs(bufA),
                     evac_copy_to(bufB), "vector", wslab, ps_mm, 4)
    wait(k, "vector", k.s_pe, mm_v[15])
    mr = emit_mix(2, 8, bufA)
    wait(k, "tensor", k.s_v, mr)
    emit_proj(k, 16, 16, lambda oc: wr_t[oc], proj_rhs(bufA),
              evac_copy_to(xhat, act=AF.Sigmoid), "scalar", wslab, ps_mm, 4)
    m_rdone = k.s_sc.n

    for c in range(NCD):
        ew(k, "vector", lambda e, c=c: e.tensor_tensor(
            out=bufC[:, c, 0:T], in0=bufC[:, c, 0:T], in1=bufB[:, c, 0:T], op=ALU.mult),
           r=[tg(bufB, c), tg(bufC, c)], w=[tg(bufC, c)])
    wait(k, "vector", k.s_pe)
    for c in range(NCD):
        ew(k, "vector", lambda e, c=c: e.tensor_tensor_scan(
            out=bufB[:, c, 1:W], data0=vtile[:, 9, c:c + 1].broadcast_to([128, T]),
            data1=bufC[:, c, 0:T], initial=0.0, op0=ALU.mult, op1=ALU.add),
           r=[tg(bufC, c), vtile], w=[tg(bufB, c)])
    ew(k, "vector", lambda e: e.tensor_copy(
        out=stile[:, 0, :], in_=bufB[:, :, W - 1:W].rearrange("p c o -> p (c o)")),
       r=[bufB], w=[stile])
    m_scanA = milestone(k, "vector", k.s_v)
    wait(k, "gpsimd", k.s_v, m_scanA)
    k.s_g.inc(16)
    ew(k, "gpsimd", lambda e, h=k.s_g.h: e.dma_start(
        out=cc_s_in[:, :], in_=stile[:, 0, :]).then_inc(h, 16))
    wait(k, "gpsimd", k.s_g)
    m_cc = k.s_cc.inc(1)
    ew(k, "gpsimd", lambda e, h=k.s_cc.h: e.collective_compute(
        "AllGather", ALU.bypass, replica_groups=PAIRS,
        ins=[cc_s_in.ap().opt()], outs=[cc_s_out.ap().opt()]).then_inc(h, 1))
    wait(k, "gpsimd", k.s_cc, m_cc)
    m_srecv = k.s_g.inc(16)
    ew(k, "gpsimd", lambda e, h=k.s_g.h: e.dma_start(
        out=stile[:, 1, :], in_=cc_s_out.ap()[0:128, :]).then_inc(h, 16))
    wait(k, "vector", k.s_g, m_srecv)
    ew(k, "vector", lambda e: e.tensor_tensor(
        out=stile[:, 2, :], in0=stile[:, 1, :], in1=vtile[:, 5, :], op=ALU.mult),
       r=[stile, vtile], w=[stile])
    for c in range(NCD):
        ew(k, "vector", lambda e, c=c: e.tensor_tensor_scan(
            out=bufB[:, c, 1:W], data0=vtile[:, 9, c:c + 1].broadcast_to([128, T]),
            data1=bufC[:, c, 0:T], initial=stile[:, 2, c:c + 1], op0=ALU.mult, op1=ALU.add),
           r=[tg(bufC, c), vtile, stile], w=[tg(bufB, c)])
    ew(k, "vector", lambda e: e.tensor_copy(
        out=bufB[:, :, 0:1].rearrange("p c o -> p (c o)"), in_=stile[:, 2, :]),
       r=[stile], w=[bufB])
    wait(k, "vector", k.s_sc, m_rdone)
    wait(k, "vector", k.s_pe)
    for c in range(NCD):
        ew(k, "vector", lambda e, c=c: e.scalar_tensor_tensor(
            out=bufA[:, c, 0:T], in0=bufC[:, c, 0:T], scalar=vtile[:, 4, c:c + 1],
            in1=bufB[:, c, 0:T], op0=ALU.mult, op1=ALU.add),
           r=[tg(bufC, c), tg(bufB, c), vtile], w=[tg(bufA, c)])
    for c in range(NCD):
        ew(k, "vector", lambda e, c=c: e.tensor_tensor(
            out=bufC[:, c, 0:T], in0=bufA[:, c, 0:T], in1=xhat[:, c, 0:T], op=ALU.mult),
           r=[tg(bufA, c), xhat], w=[tg(bufC, c)])
    m_wkv = milestone(k, "vector", k.s_v)
    wait(k, "tensor", k.s_v, m_wkv)

    def evac_addres(oc, si, c0, wd, ps):
        ew(k, "vector", lambda e, oc=oc, c0=c0, wd=wd, ps=ps: e.tensor_tensor(
            out=xres[:, oc, 1 + c0:1 + c0 + wd], in0=xres[:, oc, 1 + c0:1 + c0 + wd],
            in1=ps, op=ALU.add), r=[tg(xres, f"{oc}.{si}")], w=[tg(xres, f"{oc}.{si}")])

    emit_proj(k, 16, 16, lambda oc: wo_t[oc], proj_rhs(bufC),
              evac_addres, "vector", wslab, ps_mm, 4)

    k.phase0_tail()

    emit_ffn0(k, xres, xhat, bufA_t, bufB_t, bufC, bufC_t, wslab, ps_mm, ps_r,
              WP_OFF, RANK_SZ)

    emit_l1attn(k, xres, xhat, bufA, bufB, bufC, bufC_t, wslab, ps_mm,
                l1wq_t, l1wk_t, l1wv_t, l1wo_t, ropec_d, band_d, killv,
                cc_kv_in, cc_kv_out)

    emit_ffn1(k, xres, xhat, bufA_t, bufB_t, bufC, bufC_t, ps_mm,
              W13_OFF, RANK2_SZ)

    m_done_v = milestone(k, "vector", k.s_v)
    m_done_s = milestone(k, "scalar", k.s_sc)
    wait(k, "sync", k.s_v, m_done_v)
    wait(k, "sync", k.s_sc, m_done_s)
    for c in range(NCD):
        k.s_st.inc(16)
        ew(k, "sync", lambda e, c=c, h=k.s_st.h: e.dma_start(
            out=outT[c], in_=xres[:, c, 1:W]).then_inc(h, 16))
    wait(k, "sync", k.s_st)

    def _replay(e, lst):
        for f, posts in lst:
            r = f(e)
            for s in posts:
                r.then_inc(s, 1)

    with nc.Block() as block:
        @block.sync
        def _(e):
            _replay(e, k.em.q["sync"])

        @block.scalar
        def _(e):
            _replay(e, k.em.q["scalar"])

        @block.vector
        def _(e):
            _replay(e, k.em.q["vector"])

        @block.tensor
        def _(e):
            _replay(e, k.em.q["tensor"])

        @block.gpsimd
        def _(e):
            _replay(e, k.em.q["gpsimd"])

    es.close()
    return nc



def _tile_lhsT(Wm, contract_axis):
    if contract_axis == 1:
        O, Kd = Wm.shape
        Oc, Kc = O // 128, Kd // 128
        t = Wm.reshape(Oc, 128, Kc, 128).transpose(0, 3, 2, 1)
    else:
        Kd, O = Wm.shape
        Kc, Oc = Kd // 128, O // 128
        t = Wm.reshape(Kc, 128, Oc, 128).transpose(2, 1, 0, 3)
    return np.ascontiguousarray(t.reshape(Oc, 128, Kc * 128))


def kernel(**inputs):
    L0x = np.asarray(inputs["L0_attn_hidden_states"])

    wr_t = _tile_lhsT(np.asarray(inputs["L0_attn_Wr"]), 1)
    wk_t = _tile_lhsT(np.asarray(inputs["L0_attn_Wk"]), 1)
    wv_t = _tile_lhsT(np.asarray(inputs["L0_attn_Wv"]), 1)
    wo_t = _tile_lhsT(np.asarray(inputs["L0_attn_Wo"]), 1)

    def vec_tile(v):
        return np.ascontiguousarray(np.asarray(v).reshape(-1).reshape(16, 128).T)

    tmk = vec_tile(inputs["L0_attn_time_mix_k"])
    tmv = vec_tile(inputs["L0_attn_time_mix_v"])
    tmr = vec_tile(inputs["L0_attn_time_mix_r"])
    td = vec_tile(inputs["L0_attn_time_decay"])
    tf = vec_tile(inputs["L0_attn_time_first"])

    w1_t = _tile_lhsT(np.asarray(inputs["L1_ffn_W1"]), 1)
    w3_t = _tile_lhsT(np.asarray(inputs["L1_ffn_W3"]), 1)
    l1wq_t = _tile_lhsT(np.asarray(inputs["L1_attn_wq"]), 0)
    l1wk_t = _tile_lhsT(np.asarray(inputs["L1_attn_wk"]), 0)
    l1wv_t = _tile_lhsT(np.asarray(inputs["L1_attn_wv"]), 0)
    l1wo_t = _tile_lhsT(np.asarray(inputs["L1_attn_wo"]), 0)

    wgu = np.asarray(inputs["L0_ffn_w_gate_up"])
    wproj = np.asarray(inputs["L0_ffn_w_proj"])
    W2 = np.asarray(inputs["L1_ffn_W2"])

    cid = np.zeros((128, 3, 128), dtype=ml_dtypes.bfloat16)
    cid[:, 0, :] = 1.0
    cid[:, 1, :] = np.eye(128)
    sw = np.zeros((128, 128), dtype=np.float32)
    for ii in range(128):
        sw[ii, ii ^ 64] = 1.0
    cid[:, 2, :] = sw

    ii = np.arange(128)[:, None]
    jj = np.arange(1152)[None, :]
    band = np.where((jj >= ii + 1) & (jj <= ii + 1024), 0.0, NEG).astype(ml_dtypes.bfloat16)
    inv_freq = (1.0 / 10000.0 ** (np.arange(0, 128, 2, dtype=np.float32) / 128.0))

    in_maps = []
    for c in range(8):
        b, p = c // 2, c % 2
        t0 = p * T
        xTc = np.zeros((NCD, 128, W), dtype=np.float32)
        xTc[:, :, 1:] = L0x[b, t0:t0 + T, :].T.reshape(NCD, 128, T)
        if p == 1:
            xTc[:, :, 0] = L0x[b, t0 - 1, :].reshape(NCD, 128)
        vv = np.stack([tmk, tmv, tmr, td, tf,
                       np.full((128, 16), 1.0 if p == 1 else 0.0, np.float32)], axis=1)
        pos = np.arange(t0, t0 + T, dtype=np.float32)
        fr = pos[None, :] * inv_freq[:, None]
        cos2 = np.concatenate([np.cos(fr), np.cos(fr)], axis=0)
        sin2 = np.concatenate([-np.sin(fr), np.sin(fr)], axis=0)
        ropec_h = np.stack([cos2, sin2], axis=1).astype(ml_dtypes.bfloat16)
        killv_h = np.zeros((1, 2048), dtype=ml_dtypes.bfloat16)
        if p == 0:
            killv_h[0, 0:1024] = NEG
        in_maps.append({
            "xT": np.ascontiguousarray(xTc),
            "wr_t": wr_t, "wk_t": wk_t, "wv_t": wv_t, "wo_t": wo_t,
            "vecs": np.ascontiguousarray(vv.astype(np.float32)),
            "cid": cid,
            "wgu_slab": np.ascontiguousarray(wgu[c * 2048:(c + 1) * 2048, :]),
            "wproj_slab": np.ascontiguousarray(wproj[c * 256:(c + 1) * 256, :]),
            "w2_slab": np.ascontiguousarray(W2[c * 256:(c + 1) * 256, :]),
            "l1wq_t": l1wq_t, "l1wk_t": l1wk_t, "l1wv_t": l1wv_t, "l1wo_t": l1wo_t,
            "w1_sl": np.ascontiguousarray(w1_t[c * 8:(c + 1) * 8]),
            "w3_sl": np.ascontiguousarray(w3_t[c * 8:(c + 1) * 8]),
            "ropec_d": np.ascontiguousarray(ropec_h),
            "band_d": band, "killv_d": killv_h,
        })

    nc = build()
    res = run_bass_kernel_spmd(nc, in_maps, core_ids=list(range(8)))

    out = np.empty((4, 2048, 2048), dtype=np.float32)
    for c in range(8):
        b, p = c // 2, c % 2
        out[b, p * T:(p + 1) * T, :] = res.results[c]["outT"].reshape(D, T).T
    return out


# version 06
# speedup vs baseline: 1.0208x; 1.0050x over previous
import numpy as np
import ml_dtypes
from contextlib import ExitStack

import concourse.bass as bass
import concourse.mybir as mybir
from concourse.bass_utils import run_bass_kernel_spmd

F32 = mybir.dt.float32
BF16 = mybir.dt.bfloat16
AF = mybir.ActivationFunctionType
ALU = mybir.AluOpType
AX = mybir.AxisListType

ENGINES = ("sync", "scalar", "vector", "tensor", "gpsimd")

D = 2048
NCD = 16
T = 1024
W = T + 1
NH = 16
HDIM = 128
FFH = 8192
NFH = FFH // 128
NEG = -1.0e9
LN_EPS = 1e-5
RMS_EPS = 1.1920929e-7
ATT_SCALE = 0.08838834764831845
PAIRS = [[0, 1], [2, 3], [4, 5], [6, 7]]


class Sem:
    __slots__ = ("h", "n")

    def __init__(self, h):
        self.h = h
        self.n = 0

    def inc(self, k=1):
        self.n += k
        return self.n


class Em:

    def __init__(self):
        self.q = {e: [] for e in ENGINES}
        self.dirty = {e: set() for e in ENGINES}
        self.rdirty = {e: set() for e in ENGINES}

    @staticmethod
    def _keys(aps):
        out = set()
        for a in aps:
            if a is None:
                continue
            s = a if isinstance(a, str) else a.name
            name, _, tag = s.partition(":")
            out.add((name, tag or None))
        return out

    @staticmethod
    def _conflict(ks, dirty):
        for n, t in ks:
            for dn, dt in dirty:
                if n == dn and (t is None or dt is None or t == dt):
                    return True
        return False

    def on(self, e, f, r=(), w=()):
        rk = self._keys(r)
        wk = self._keys(w)
        if (rk and self._conflict(rk, self.dirty[e])) or \
           (wk and (self._conflict(wk, self.rdirty[e]) or self._conflict(wk, self.dirty[e]))):
            self.q[e].append([lambda eng: eng.drain(), []])
            self.dirty[e] = set()
            self.rdirty[e] = set()
        self.q[e].append([f, []])
        self.dirty[e] |= wk
        self.rdirty[e] |= rk

    def drain(self, e):
        if self.dirty[e] or self.rdirty[e]:
            self.q[e].append([lambda eng: eng.drain(), []])
            self.dirty[e] = set()
            self.rdirty[e] = set()


class K:
    pass


def ew(k, eng, f, r=(), w=()):
    k.em.on(eng, f, r, w)


def tg(ap, tag):
    return f"{ap.name}:{tag}"


def wait(k, eng, sem, mark=None):
    v = sem.n if mark is None else mark
    if v > 0:
        k.em.on(eng, lambda e, s=sem.h, v=v: e.wait_ge(s, v))


def milestone(k, eng, sem):
    q = k.em.q[eng]
    if not q or q[-1][1]:
        k.em.q[eng].append([lambda e: e.drain(), []])
        k.em.dirty[eng] = set()
        v = sem.inc(1)
        k.em.on(eng, lambda e, s=sem.h: e.nop().then_inc(s, 1))
        return v
    prev = sem.n
    if prev > 0:
        q.insert(len(q) - 1, [lambda e, s=sem.h, p=prev: e.wait_ge(s, p), []])
    v = sem.inc(1)
    q[-1][1].append(sem.h)
    return v


SEGS_T = ((0, 512), (512, 512))



def emit_ln(k, src, dst, segs, sqbuf):
    stat, rb, mrb, ps = k.stat_scr, k.rb, k.mrb, k.ps_ln
    for si, (c0, wd) in enumerate(segs):
        ps_s = ps[:, 0, 0:wd]
        ps_q = ps[:, 1, 0:wd]
        for c in range(NCD):
            ew(k, "scalar", lambda e, c=c, c0=c0, wd=wd: e.activation(
                out=dst[:, c, c0:c0 + wd], in_=src[:, c, c0:c0 + wd], func=AF.Copy),
               r=[src], w=[dst])
        m_cast = milestone(k, "scalar", k.s_sc)
        wait(k, "vector", k.s_sc, m_cast)
        wait(k, "tensor", k.s_sc, m_cast)
        for c in range(NCD):
            ew(k, "vector", lambda e, c=c, c0=c0, wd=wd: e.tensor_tensor(
                out=sqbuf[:, c, 0:wd], in0=dst[:, c, c0:c0 + wd],
                in1=dst[:, c, c0:c0 + wd], op=ALU.mult), r=[dst], w=[sqbuf])
        m_sq = milestone(k, "vector", k.s_v)
        for c in range(NCD):
            ew(k, "tensor", lambda e, c=c, c0=c0, wd=wd, ps_s=ps_s: e.matmul(
                ps_s, k.ones, dst[:, c, c0:c0 + wd], start=(c == 0), stop=(c == NCD - 1)))
        wait(k, "tensor", k.s_v, m_sq)
        for c in range(NCD):
            ew(k, "tensor", lambda e, c=c, wd=wd, ps_q=ps_q: e.matmul(
                ps_q, k.ones, sqbuf[:, c, 0:wd], start=(c == 0), stop=(c == NCD - 1)))
        m_mm = milestone(k, "tensor", k.s_pe)
        wait(k, "vector", k.s_pe, m_mm)
        mean = stat[:, 0, 0:wd]
        var = stat[:, 1, 0:wd]
        msq = stat[:, 2, 0:wd]
        ew(k, "vector", lambda e, mean=mean, ps_s=ps_s: e.tensor_scalar_mul(
            out=mean, in0=ps_s, scalar1=1.0 / D), w=[stat])
        ew(k, "vector", lambda e, var=var, ps_q=ps_q: e.tensor_scalar(
            out=var, in0=ps_q, scalar1=1.0 / D, scalar2=LN_EPS, op0=ALU.mult, op1=ALU.add),
           w=[stat])
        ew(k, "vector", lambda e, mean=mean, msq=msq: e.tensor_tensor(
            out=msq, in0=mean, in1=mean, op=ALU.mult), r=[stat], w=[stat])
        ew(k, "vector", lambda e, var=var, msq=msq: e.tensor_tensor(
            out=var, in0=var, in1=msq, op=ALU.subtract), r=[stat], w=[stat])
        m_v1 = milestone(k, "vector", k.s_v)
        wait(k, "scalar", k.s_v, m_v1)
        ew(k, "scalar", lambda e, var=var: e.activation(out=var, in_=var, func=AF.Sqrt),
           r=[stat], w=[stat])
        m_s2 = milestone(k, "scalar", k.s_sc)
        wait(k, "vector", k.s_sc, m_s2)
        rbs = rb[:, si % 2, 0:wd]
        mrbs = mrb[:, si % 2, 0:wd]

        def _recip(e, var=var, rbs=rbs, nc_=k.nc):
            with nc_.allow_low_precision(reason="rstd broadcast tile stored bf16"):
                return e.reciprocal(out=rbs, in_=var)
        ew(k, "vector", _recip, r=[stat], w=[rb])
        ew(k, "vector", lambda e, mean=mean, rbs=rbs, mrbs=mrbs: e.tensor_tensor(
            out=mrbs, in0=mean, in1=rbs, op=ALU.mult), r=[stat, rb], w=[mrb])
        for c in range(NCD):
            ew(k, "vector", lambda e, c=c, c0=c0, wd=wd, rbs=rbs: e.tensor_tensor(
                out=dst[:, c, c0:c0 + wd], in0=dst[:, c, c0:c0 + wd], in1=rbs, op=ALU.mult),
               r=[tg(dst, c), rb], w=[tg(dst, c)])
        for c in range(NCD):
            ew(k, "vector", lambda e, c=c, c0=c0, wd=wd, mrbs=mrbs: e.tensor_tensor(
                out=dst[:, c, c0:c0 + wd], in0=dst[:, c, c0:c0 + wd], in1=mrbs,
                op=ALU.subtract), r=[tg(dst, c), mrb], w=[tg(dst, c)])
    milestone(k, "vector", k.s_v)



def emit_proj(k, n_oc, n_kc, w_src, rhs, evac, evac_eng, wslab, ps_pool, n_banks,
              segs=SEGS_T, w_queue="gpsimd", w_sems=None, load_wait=None):
    w_sems = w_sems or (k.s_w, k.s_w2)
    evac_sem = k.s_v if evac_eng == "vector" else k.s_sc
    nseg = len(segs)
    evac_marks = {}
    load_marks = {}
    mm_marks = {}

    def emit_load(oc):
        if oc >= 2:
            wait(k, w_queue, k.s_pe, mm_marks[oc - 2])
        else:
            wait(k, w_queue, k.s_pe)
            if load_wait is not None:
                wait(k, w_queue, load_wait[0], load_wait[1])
        ld = w_sems[oc % 2].inc(16)
        ew(k, w_queue, lambda e, oc=oc, h=w_sems[oc % 2].h: e.dma_start(
            out=wslab[:, oc % 2, :, :].rearrange("p a b -> p (a b)"),
            in_=w_src(oc)).then_inc(h, 16))
        load_marks[oc] = ld

    emit_load(0)
    if n_oc > 1:
        emit_load(1)
    for oc in range(n_oc):
        wait(k, "tensor", w_sems[oc % 2], load_marks[oc])
        if oc >= max(1, n_banks // nseg):
            wait(k, "tensor", evac_sem, evac_marks[oc - max(1, n_banks // nseg)])
        elif oc == 0:
            wait(k, "tensor", k.s_v)
            wait(k, "tensor", k.s_sc)
        for si, (c0, wd) in enumerate(segs):
            pb = (nseg * oc + si) % n_banks
            for dc in range(n_kc):
                ew(k, "tensor", lambda e, oc=oc, dc=dc, c0=c0, wd=wd, pb=pb: e.matmul(
                    ps_pool[:, pb, 0:wd], wslab[:, oc % 2, dc, :], rhs(dc, c0, wd),
                    start=(dc == 0), stop=(dc == n_kc - 1)))
        mm = milestone(k, "tensor", k.s_pe)
        mm_marks[oc] = mm
        if oc + 2 < n_oc:
            emit_load(oc + 2)
        wait(k, evac_eng, k.s_pe, mm)
        for si, (c0, wd) in enumerate(segs):
            pb = (nseg * oc + si) % n_banks
            evac(oc, si, c0, wd, ps_pool[:, pb, 0:wd])
        evac_marks[oc] = milestone(k, evac_eng, evac_sem)
    return mm_marks





def emit_phase0(k, wgu_slab, wproj_slab, w2_slab, cc_w_in, WP_OFF,
                cc_w2_in, w1_sl, w3_sl, W13_OFF):
    nc = k.nc
    es0 = ExitStack()
    ld = es0.enter_context(nc.sbuf_tensor("p0_ld", [128, 2, FFH], BF16))[:]
    m1 = es0.enter_context(nc.sbuf_tensor("p0_m1", [128, FFH], BF16))[:]
    m2 = es0.enter_context(nc.sbuf_tensor("p0_m2", [128, FFH], BF16))[:]
    st_ = es0.enter_context(nc.sbuf_tensor("p0_st", [128, 64, 128], BF16))[:]
    qs = es0.enter_context(nc.sbuf_tensor("p0_qs", [128, 3, 128], F32))[:]
    sbf = es0.enter_context(nc.sbuf_tensor("p0_sbf", [128, 128], BF16))[:]
    pp = es0.enter_context(nc.sbuf_tensor("p0_pp", [128, 2, FFH], F32))[:]
    pt = es0.enter_context(nc.psum_tensor("p0_pt", [128, 2, 1024], BF16))[:]
    in_cc = cc_w_in.ap()
    in_cc2 = cc_w2_in.ap()
    last_stage = [0]

    def quant(par, ncols):
        ng = ncols // 64
        lv = ld[:, par, 0:ncols].rearrange("p (g e) -> p g e", e=64)
        gb = lambda a: a.rearrange("p (g o) -> p g o", o=1).broadcast_to([128, ng, 64])
        ew(k, "vector", lambda e, lv=lv, ng=ng: e.reduce_sum(
            out=qs[:, 0, 0:ng].rearrange("p (g o) -> p g o", o=1), in_=lv,
            axis=AX.X, apply_absolute_value=True), r=[tg(ld, par)], w=[qs])
        ew(k, "vector", lambda e, ng=ng: e.tensor_scalar(
            out=qs[:, 0, 0:ng], in0=qs[:, 0, 0:ng], scalar1=1.0 / 64, scalar2=1e-8,
            op0=ALU.mult, op1=ALU.max), r=[qs], w=[qs])
        ew(k, "vector", lambda e, ng=ng: e.tensor_copy(
            out=sbf[:, 0:ng], in_=qs[:, 0, 0:ng]), r=[qs], w=[sbf])

        def _rcp(e, ng=ng, nc_=k.nc):
            with nc_.allow_low_precision(reason="group scale reciprocal"):
                e.reciprocal(out=qs[:, 1, 0:ng], in_=sbf[:, 0:ng])
        ew(k, "vector", _rcp, r=[sbf], w=[qs])
        m1v = m1[:, 0:ncols].rearrange("p (g e) -> p g e", e=64)
        m2v = m2[:, 0:ncols].rearrange("p (g e) -> p g e", e=64)
        ew(k, "vector", lambda e, lv=lv, m1v=m1v, ng=ng: e.tensor_tensor(
            out=m1v, in0=lv, in1=gb(qs[:, 1, 0:ng]), op=ALU.mult),
           r=[tg(ld, par), qs], w=[m1])
        ew(k, "vector", lambda e, ncols=ncols: e.tensor_single_scalar(
            out=m2[:, 0:ncols], in_=m1[:, 0:ncols], scalar=0.5, op=ALU.is_gt),
           r=[m1], w=[m2])
        ew(k, "vector", lambda e, ncols=ncols: e.tensor_single_scalar(
            out=m1[:, 0:ncols], in_=m1[:, 0:ncols], scalar=-0.5, op=ALU.is_lt),
           r=[m1], w=[m1])
        ew(k, "vector", lambda e, ncols=ncols: e.tensor_tensor(
            out=m1[:, 0:ncols], in0=m2[:, 0:ncols], in1=m1[:, 0:ncols], op=ALU.subtract),
           r=[m1, m2], w=[m1])
        ew(k, "vector", lambda e, m1v=m1v, ng=ng: e.tensor_tensor(
            out=m1v, in0=m1v, in1=gb(sbf[:, 0:ng]), op=ALU.mult),
           r=[m1, sbf], w=[m1])
        return milestone(k, "vector", k.s_v)

    def transpose_stage(ntiles, off, m_src, dst_cc=None):
        dst_cc = in_cc if dst_cc is None else dst_cc
        wait(k, "tensor", k.s_v, m_src)
        wait(k, "vector", k.s_g, last_stage[0])
        ev = {}
        for g0 in range(0, ntiles, 8):
            gcnt = min(8, ntiles - g0)
            bank = (g0 // 8) % 2
            if g0 >= 16:
                wait(k, "tensor", k.s_v, ev[g0 - 16])
            for t in range(gcnt):
                cc = g0 + t
                ew(k, "tensor", lambda e, cc=cc, bank=bank, t=t: e.transpose(
                    pt[:, bank, t * 128:(t + 1) * 128],
                    m1[:, cc * 128:(cc + 1) * 128], k.ident))
            mk = milestone(k, "tensor", k.s_pe)
            wait(k, "vector", k.s_pe, mk)
            ew(k, "vector", lambda e, g0=g0, bank=bank, gcnt=gcnt: e.tensor_copy(
                out=st_[:, g0:g0 + gcnt, :].rearrange("p a b -> p (a b)"),
                in_=pt[:, bank, 0:gcnt * 128]), w=[st_])
            ev[g0] = milestone(k, "vector", k.s_v)
        wait(k, "gpsimd", k.s_v, ev[(ntiles - 1) // 8 * 8])
        sz = ntiles * 128 * 128
        last_stage[0] = k.s_g.inc(16)
        ew(k, "gpsimd", lambda e, off=off, sz=sz, ntiles=ntiles, dst_cc=dst_cc,
           h=k.s_g.h: e.dma_start(
            out=dst_cc[off:off + sz].rearrange("(p m) -> p m", p=128),
            in_=st_[:, 0:ntiles, :].rearrange("p a b -> p (a b)")).then_inc(h, 16))
        return milestone(k, "tensor", k.s_pe)

    qm = {}
    tm = {}
    for l in range(16):
        par = l % 2
        if l >= 2:
            wait(k, "gpsimd", k.s_v, qm[l - 2])
        sem = (k.s_w, k.s_w2)[par]
        lm = sem.inc(16)
        ew(k, "gpsimd", lambda e, l=l, par=par, h=sem.h: e.dma_start(
            out=ld[:, par, 0:2048], in_=wgu_slab[l * 128:(l + 1) * 128, :]).then_inc(h, 16))
        wait(k, "vector", sem, lm)
        if l >= 1:
            wait(k, "vector", k.s_pe, tm[l - 1])
        qm[l] = quant(par, 2048)
        tm[l] = transpose_stage(16, l * 262144, qm[l])

    for l in range(2):
        par = l % 2
        sem = (k.s_w, k.s_w2)[par]
        wait(k, "gpsimd", k.s_v, qm[14 + l] if l else qm[14])
        lm = sem.inc(16)
        ew(k, "gpsimd", lambda e, l=l, par=par, h=sem.h: e.dma_start(
            out=ld[:, par, :], in_=wproj_slab[l * 128:(l + 1) * 128, :]).then_inc(h, 16))
        wait(k, "vector", sem, lm)
        wait(k, "vector", k.s_pe, tm[15 + l] if l else tm[15])
        qm[16 + l] = quant(par, FFH)
        tm[16 + l] = transpose_stage(64, WP_OFF + l * 1048576, qm[16 + l])

    SCL = float(FFH) ** -0.5
    bm = {}
    for l in range(2):
        lm = k.s_k.inc(16)
        if l == 1:
            wait(k, "sync", k.s_v, bm[0])
        ew(k, "sync", lambda e, l=l, h=k.s_k.h: e.dma_start(
            out=pp[:, 0, :], in_=w2_slab[l * 128:(l + 1) * 128, :]).then_inc(h, 16))
        wait(k, "vector", k.s_k, lm)
        par = 0
        for s in range(13):
            h2 = 1 << s
            sv = pp[:, par, :].rearrange("p (nb two h) -> p nb two h", two=2, h=h2)
            dv = pp[:, 1 - par, :].rearrange("p (nb two h) -> p nb two h", two=2, h=h2)
            ew(k, "vector", lambda e, sv=sv, dv=dv: e.tensor_tensor(
                out=dv[:, :, 0, :], in0=sv[:, :, 0, :], in1=sv[:, :, 1, :], op=ALU.add),
               r=[tg(pp, par)], w=[tg(pp, 1 - par)])
            ew(k, "vector", lambda e, sv=sv, dv=dv: e.tensor_tensor(
                out=dv[:, :, 1, :], in0=sv[:, :, 0, :], in1=sv[:, :, 1, :], op=ALU.subtract),
               r=[tg(pp, par)], w=[tg(pp, 1 - par)])
            par = 1 - par
        wait(k, "vector", k.s_pe, tm[17 + l])
        ew(k, "vector", lambda e, par=par: e.tensor_scalar_mul(
            out=m1[:, :], in0=pp[:, par, :], scalar1=SCL), r=[tg(pp, par)], w=[m1])
        bm[l] = milestone(k, "vector", k.s_v)
        tm[18 + l] = transpose_stage(64, l * 1048576, bm[l], dst_cc=in_cc2)

    wait(k, "gpsimd", k.s_g)
    k.m_ccw = k.s_cc.inc(1)
    ew(k, "gpsimd", lambda e, h=k.s_cc.h: e.collective_compute(
        "AllGather", ALU.bypass, replica_groups=[list(range(8))],
        ins=[cc_w_in.ap().opt()], outs=[k.cc_w_out_ap.opt()]).then_inc(h, 1))

    def phase0_tail():
        for j in range(8):
            for which, srcap in ((0, w1_sl), (1, w3_sl)):
                off = W13_OFF + (which * 8 + j) * 262144
                k.s_g.inc(16)
                ew(k, "gpsimd", lambda e, j=j, srcap=srcap, off=off, h=k.s_g.h: e.dma_start(
                    out=in_cc2[off:off + 262144].rearrange("(p m) -> p m", p=128),
                    in_=srcap[j]).then_inc(h, 16))
        wait(k, "gpsimd", k.s_g)
        k.m_ccw2 = k.s_cc.inc(1)
        ew(k, "gpsimd", lambda e, h=k.s_cc.h: e.collective_compute(
            "AllGather", ALU.bypass, replica_groups=[list(range(8))],
            ins=[cc_w2_in.ap().opt()], outs=[k.cc_w2_out_ap.opt()]).then_inc(h, 1))
    k.phase0_tail = phase0_tail
    es0.close()



def emit_ffn0(k, xres, xhat, bufA_t, bufB_t, bufC, bufC_t, wslab, ps_mm, ps_r,
              WP_OFF, RANK_SZ):
    occ = k.cc_w_out_ap

    def wgu_slab_ap(hc):
        r, l = divmod(hc, 16)
        off = r * RANK_SZ + l * 262144
        return occ[off:off + 262144].rearrange("(p m) -> p m", p=128)

    def wp_slab_ap(dc):
        r, l = divmod(dc, 2)
        off = r * RANK_SZ + WP_OFF + l * 1048576
        return occ[off:off + 1048576].rearrange("(p m) -> p m", p=128)

    hA = bufA_t[:, 0:16384].rearrange("p (a b) -> p a b", b=512)
    hB = bufB_t[:, 0:16384].rearrange("p (a b) -> p a b", b=512)

    def hbuf(fc):
        return (hA if fc < 32 else hB)[:, fc % 32, :]

    def htag(fc):
        return tg(bufA_t if fc < 32 else bufB_t, fc % 32)

    gslab = bufC_t[:, 0:8192].rearrange("p (a b c) -> p a b c", b=16, c=128)
    sqt = bufC_t[:, 8192:10240].rearrange("p (a b) -> p a b", a=4)
    sg = bufC_t[:, 10240:11264].rearrange("p (a b) -> p a b", a=2)
    tstat = bufC_t[:, 11264:12288].bitcast(F32)
    rsb = bufC_t[:, 12288:13312].rearrange("p (a b) -> p a b", a=2)
    pslab = bufC_t[:, 0:16384].rearrange("p (a b c) -> p a b c", b=64, c=128)
    ksl = (k.s_k, k.s_k2, k.s_k3, k.s_k4)

    wait(k, "scalar", k.s_v)
    wait(k, "vector", k.s_pe)
    emit_ln(k, xres, xhat, ((1, 512), (513, 512)), bufC)

    for th in range(2):
        cx = 1 + th * 512
        wait(k, "sync", k.s_pe)
        wait(k, "vector", k.s_pe)
        wait(k, "scalar", k.s_pe)
        wait(k, "tensor", k.s_v)
        wait(k, "tensor", k.s_sc)
        if th == 0:
            wait(k, "sync", k.s_cc, k.m_ccw)
        load_marks = {}

        def gu_load(fc, which):
            slot = (2 * fc + which) % 4
            sem = ksl[slot]
            lm = sem.inc(16)
            ew(k, "sync", lambda e, fc=fc, which=which, slot=slot, h=sem.h: e.dma_start(
                out=gslab[:, slot, :, :].rearrange("p a b -> p (a b)"),
                in_=wgu_slab_ap(fc + 64 * which)).then_inc(h, 16))
            load_marks[(fc, which)] = lm

        gu_load(0, 0)
        gu_load(0, 1)
        gu_load(1, 0)
        gu_load(1, 1)
        gmark = {}
        umark = {}
        hmark = {}
        smark = {}
        sqmark = {}
        ssmark = {}
        for fc in range(64):
            slot_g = (2 * fc) % 4
            slot_u = (2 * fc + 1) % 4
            wait(k, "tensor", ksl[slot_g], load_marks[(fc, 0)])
            if fc >= 2:
                wait(k, "tensor", k.s_sc, smark[fc - 2])
            for dc in range(NCD):
                ew(k, "tensor", lambda e, dc=dc, slot_g=slot_g, cx=cx: e.matmul(
                    ps_mm[:, slot_g % 4, :], gslab[:, slot_g, dc, :],
                    xhat[:, dc, cx:cx + 512], start=(dc == 0), stop=(dc == NCD - 1)))
            gmark[fc] = milestone(k, "tensor", k.s_pe)
            wait(k, "tensor", ksl[slot_u], load_marks[(fc, 1)])
            if fc >= 2:
                wait(k, "tensor", k.s_v, hmark[fc - 2])
            for dc in range(NCD):
                ew(k, "tensor", lambda e, dc=dc, slot_u=slot_u, cx=cx: e.matmul(
                    ps_mm[:, slot_u % 4, :], gslab[:, slot_u, dc, :],
                    xhat[:, dc, cx:cx + 512], start=(dc == 0), stop=(dc == NCD - 1)))
            umark[fc] = milestone(k, "tensor", k.s_pe)
            if fc >= 2:
                f2 = fc - 2
                wait(k, "tensor", k.s_sc, sqmark[f2])
                ew(k, "tensor", lambda e, f2=f2: e.matmul(
                    ps_r[:, 0, :], k.ones, sqt[:, f2 % 4, :],
                    start=(f2 == 0), stop=False))
                ssmark[f2] = milestone(k, "tensor", k.s_pe)
            if fc + 2 < 64:
                wait(k, "sync", k.s_pe, gmark[fc])
                gu_load(fc + 2, 0)
                wait(k, "sync", k.s_pe, umark[fc])
                gu_load(fc + 2, 1)
            wait(k, "scalar", k.s_pe, gmark[fc])
            if fc >= 2:
                wait(k, "scalar", k.s_v, hmark[fc - 2])
            ew(k, "scalar", lambda e, fc=fc, slot_g=slot_g: e.activation(
                out=sg[:, fc % 2, :], in_=ps_mm[:, slot_g % 4, :], func=AF.Silu),
               w=[f"bufC:sg{fc % 2}"])
            smark[fc] = milestone(k, "scalar", k.s_sc)
            wait(k, "vector", k.s_pe, umark[fc])
            wait(k, "vector", k.s_sc, smark[fc])
            ew(k, "vector", lambda e, fc=fc, slot_u=slot_u: e.tensor_tensor(
                out=hbuf(fc), in0=sg[:, fc % 2, :], in1=ps_mm[:, slot_u % 4, :],
                op=ALU.mult), r=[f"bufC:sg{fc % 2}"], w=[htag(fc)])
            hmark[fc] = milestone(k, "vector", k.s_v)
            if fc >= 4:
                wait(k, "scalar", k.s_pe, ssmark[fc - 4])
            wait(k, "scalar", k.s_v, hmark[fc])
            ew(k, "scalar", lambda e, fc=fc: e.activation(
                out=sqt[:, fc % 4, :], in_=hbuf(fc), func=AF.Square),
               w=[f"bufC:sq{fc % 4}"])
            sqmark[fc] = milestone(k, "scalar", k.s_sc)
        for f2 in (62, 63):
            wait(k, "tensor", k.s_sc, sqmark[f2])
            ew(k, "tensor", lambda e, f2=f2: e.matmul(
                ps_r[:, 0, :], k.ones, sqt[:, f2 % 4, :], start=False, stop=(f2 == 63)))
        mss = milestone(k, "tensor", k.s_pe)
        wait(k, "vector", k.s_pe, mss)
        wait(k, "vector", k.s_sc)
        ew(k, "vector", lambda e: e.tensor_scalar(
            out=tstat[:, 0:512], in0=ps_r[:, 0, :], scalar1=1.0 / FFH, scalar2=RMS_EPS,
            op0=ALU.mult, op1=ALU.add), w=["bufC:ts"])
        mvst = milestone(k, "vector", k.s_v)
        wait(k, "scalar", k.s_v, mvst)
        ew(k, "scalar", lambda e: e.activation(
            out=tstat[:, 0:512], in_=tstat[:, 0:512], func=AF.Sqrt),
           r=["bufC:ts"], w=["bufC:ts"])
        msqr = milestone(k, "scalar", k.s_sc)
        wait(k, "vector", k.s_sc, msqr)

        def _recipr(e, th=th, nc_=k.nc):
            with nc_.allow_low_precision(reason="rms rstd tile stored bf16"):
                return e.reciprocal(out=rsb[:, th, :], in_=tstat[:, 0:512])
        ew(k, "vector", _recipr, r=["bufC:ts"], w=["bufC:rs"])
        for fc in range(64):
            ew(k, "vector", lambda e, fc=fc, th=th: e.tensor_tensor(
                out=hbuf(fc), in0=hbuf(fc), in1=rsb[:, th, :], op=ALU.mult),
               r=[htag(fc), "bufC:rs"], w=[htag(fc)])
        mhn = milestone(k, "vector", k.s_v)
        wait(k, "tensor", k.s_v, mhn)

        def evac_ffn(oc, si, c0_, wd, ps, th=th, cx=cx):
            ew(k, "vector", lambda e, oc=oc, ps=ps, cx=cx: e.tensor_tensor(
                out=xres[:, oc, cx:cx + 512], in0=xres[:, oc, cx:cx + 512],
                in1=ps, op=ALU.add),
               r=[tg(xres, f"f{oc}.{th}")], w=[tg(xres, f"f{oc}.{th}")])

        emit_proj(k, 16, 64, wp_slab_ap, lambda dc, c0_, wd: hbuf(dc),
                  evac_ffn, "vector", pslab, ps_mm, 4, segs=((0, 512),),
                  w_queue="sync", w_sems=(k.s_k, k.s_k2), load_wait=(k.s_v, mhn))





def emit_l1attn(k, xres, xhat, bufA, bufB, bufC, bufC_t, wslab, ps_mm,
                wq_t, wk_t, wv_t, wo_t, ropec_d, band_d, killv, cc_kv_in, cc_kv_out):
    nc = k.nc
    ropec = bufC_t[:, 14016:16064].rearrange("p (a b) -> p a b", a=2)
    band = bufC_t[:, 12864:14016]
    KHD = 131072
    kin = cc_kv_in.ap()
    kout = cc_kv_out.ap()

    def kv_in_K(h):
        return kin[h * KHD:(h + 1) * KHD].rearrange("(p m) -> p m", p=128)

    def kv_in_V(h):
        off = 16 * KHD + h * KHD
        return kin[off:off + KHD].rearrange("(p m) -> p m", p=128)

    def kv_out_K(h):
        return kout[h * KHD:(h + 1) * KHD].rearrange("(p m) -> p m", p=128)

    def kv_out_V(h):
        off = 16 * KHD + h * KHD
        return kout[off:off + KHD].rearrange("(p m) -> p m", p=128)

    ps_s = ps_mm[:, 0:3, :].rearrange("p a b -> p (a b)")
    ps_pt = ps_mm[:, 3, :].bitcast(BF16)
    ps_pt2 = k.ps_ln[:, 0, 0:64].bitcast(BF16)
    ps_ot = [k.ps_ln[:, 1, 0:128], k.ps_r[:, 0, 0:128]]

    wait(k, "scalar", k.s_v)
    wait(k, "scalar", k.s_pe)
    wait(k, "vector", k.s_pe)
    emit_ln(k, xres, xhat, ((1, 512), (513, 512)), bufC)

    def proj_rhs_x(dc, c0, wd):
        return xhat[:, dc, 1 + c0:1 + c0 + wd]

    def evac_copy_to(buf):
        def f(oc, si, c0, wd, ps):
            ew(k, "vector", lambda e, oc=oc, c0=c0, wd=wd, ps=ps: e.tensor_copy(
                out=buf[:, oc, c0:c0 + wd], in_=ps), w=[tg(buf, f"a{oc}")])
        return f

    def emit_rope(buf, h):
        wait(k, "tensor", k.s_v)
        for s2 in range(2):
            ew(k, "tensor", lambda e, h=h, s2=s2, buf=buf: e.matmul(
                ps_mm[:, 2 + s2, :], k.swap, buf[:, h, s2 * 512:(s2 + 1) * 512],
                start=True, stop=True))
        mpe = milestone(k, "tensor", k.s_pe)
        ew(k, "vector", lambda e, h=h, buf=buf: e.tensor_tensor(
            out=bufC[:, 0, 0:T], in0=buf[:, h, 0:T], in1=ropec[:, 0, :], op=ALU.mult),
           r=[tg(buf, f"a{h}")], w=[tg(bufC_t, "rt0")])
        wait(k, "vector", k.s_pe, mpe)
        for s2 in range(2):
            ew(k, "vector", lambda e, s2=s2: e.tensor_tensor(
                out=bufC[:, 1, s2 * 512:(s2 + 1) * 512], in0=ps_mm[:, 2 + s2, :],
                in1=ropec[:, 1, s2 * 512:(s2 + 1) * 512], op=ALU.mult),
               w=[tg(bufC_t, "rt1")])
        ew(k, "vector", lambda e, h=h, buf=buf: e.tensor_tensor(
            out=buf[:, h, 0:T], in0=bufC[:, 0, 0:T], in1=bufC[:, 1, 0:T], op=ALU.add),
           r=[tg(bufC_t, "rt0"), tg(bufC_t, "rt1")], w=[tg(buf, f"a{h}")])
        return milestone(k, "vector", k.s_v)

    wait(k, "sync", k.s_pe)
    m_rc1 = k.s_ld.inc(16)
    ew(k, "sync", lambda e, h=k.s_ld.h: e.dma_start(
        out=ropec, in_=ropec_d).then_inc(h, 16))
    wait(k, "vector", k.s_ld, m_rc1)
    wait(k, "tensor", k.s_v)
    emit_proj(k, 16, 16, lambda oc: wk_t[oc], proj_rhs_x,
              evac_copy_to(bufB), "vector", wslab, ps_mm, 2)
    for h in range(NH):
        mr = emit_rope(bufB, h)
        k.s_g.inc(16)
        wait(k, "gpsimd", k.s_v, mr)
        ew(k, "gpsimd", lambda e, h=h, hh=k.s_g.h: e.dma_start(
            out=kv_in_K(h), in_=bufB[:, h, 0:T]).then_inc(hh, 16))

    wait(k, "vector", k.s_pe)
    emit_proj(k, 16, 16, lambda oc: wv_t[oc], proj_rhs_x,
              evac_copy_to(bufC), "vector", wslab, ps_mm, 2)
    ps_vt = ps_mm[:, 2:4, :].rearrange("p a b -> p (a b)").bitcast(BF16)
    wait(k, "tensor", k.s_v)
    mv_prev = None
    for h in range(NH):
        if mv_prev is not None:
            wait(k, "tensor", k.s_v, mv_prev)
        for t8 in range(8):
            ew(k, "tensor", lambda e, h=h, t8=t8: e.transpose(
                ps_vt[:, t8 * 128:(t8 + 1) * 128],
                bufC[:, h, t8 * 128:(t8 + 1) * 128], k.ident))
        mpe = milestone(k, "tensor", k.s_pe)
        wait(k, "vector", k.s_pe, mpe)
        if h > 0:
            wait(k, "vector", k.s_g)
        ew(k, "vector", lambda e, h=h: e.tensor_copy(
            out=bufC[:, h, 0:T], in_=ps_vt[:, 0:1024]), w=[tg(bufC_t, f"v{h}")])
        mv = milestone(k, "vector", k.s_v)
        mv_prev = mv
        k.s_g.inc(16)
        wait(k, "gpsimd", k.s_v, mv)
        ew(k, "gpsimd", lambda e, h=h, hh=k.s_g.h: e.dma_start(
            out=kv_in_V(h), in_=bufC[:, h, 0:T]).then_inc(hh, 16))

    wait(k, "gpsimd", k.s_g)
    m_kv = k.s_cc.inc(1)
    ew(k, "gpsimd", lambda e, hh=k.s_cc.h: e.collective_compute(
        "AllGather", ALU.bypass, replica_groups=PAIRS,
        ins=[cc_kv_in.ap().opt()], outs=[cc_kv_out.ap().opt()]).then_inc(hh, 1))
    k.phase0_tail()

    emit_proj(k, 16, 16, lambda oc: wq_t[oc], proj_rhs_x,
              evac_copy_to(bufA), "vector", wslab, ps_mm, 2)
    wait(k, "sync", k.s_g)
    m_rc2 = k.s_ld.inc(32)
    ew(k, "sync", lambda e, h=k.s_ld.h: e.dma_start(
        out=ropec, in_=ropec_d).then_inc(h, 16))
    ew(k, "sync", lambda e, h=k.s_ld.h: e.dma_start(
        out=band[:, :], in_=band_d).then_inc(h, 16))
    wait(k, "vector", k.s_ld, m_rc2)
    wait(k, "vector", k.s_g)
    rope_q_marks = {}
    for h in range(NH):
        rope_q_marks[h] = emit_rope(bufA, h)

    kbuf = bufC_t[:, 0:4096].rearrange("p (s m) -> p s m", s=2)
    vbuf = bufC_t[:, 4096:8192].rearrange("p (s a c) -> p s a c", s=2, a=16)
    Pt = bufC_t[:, 8192:10496].rearrange("p (s m) -> p s m", s=2)
    ptile = bufC_t[:, 10496:12800].rearrange("p (s m) -> p s m", s=2)
    smal = bufC_t[:, 12800:12864].bitcast(F32)

    wait(k, "vector", k.s_pe)
    wait(k, "sync", k.s_cc, m_kv)
    wait(k, "sync", k.s_v)
    wait(k, "sync", k.s_g)

    NIT = NH * 8
    kvsem = (k.s_k3, k.s_k4)
    kvload = {}

    def kv_load(h):
        sl = h % 2
        sem = kvsem[sl]
        ew(k, "sync", lambda e, h=h, sl=sl, hh=sem.h: e.dma_start(
            out=kbuf[:, sl, 0:1024], in_=kv_out_K(h)).then_inc(hh, 16))
        ew(k, "sync", lambda e, h=h, sl=sl, hh=sem.h: e.dma_start(
            out=kbuf[:, sl, 1024:2048], in_=bufB[:, h, 0:T]).then_inc(hh, 16))
        ew(k, "sync", lambda e, h=h, sl=sl, hh=sem.h: e.dma_start(
            out=vbuf[:, sl, 0:8, :].rearrange("p a c -> p (a c)"),
            in_=kv_out_V(h)).then_inc(hh, 16))
        ew(k, "sync", lambda e, h=h, sl=sl, hh=sem.h: e.dma_start(
            out=vbuf[:, sl, 8:16, :].rearrange("p a c -> p (a c)"),
            in_=kv_in_V(h)).then_inc(hh, 16))
        kvload[h] = sem.inc(64)

    kv_load(0)
    kv_load(1)

    ps_marks = {}
    mx_marks = {}
    exp_marks = {}
    pn_marks = {}
    pt_marks = {}
    pte_marks = {}
    av_marks = {}
    oe_marks = {}

    for i in range(NIT + 2):
        h, qt = divmod(i, 8)
        if i < NIT:
            sl = h % 2
            if qt == 0:
                wait(k, "tensor", kvsem[sl], kvload[h])
                wait(k, "tensor", k.s_v, rope_q_marks[h])
            if i >= 1:
                wait(k, "tensor", k.s_sc, exp_marks[i - 1])
            ws = qt * 128
            ew(k, "tensor", lambda e, h=h, qt=qt, sl=sl, ws=ws: e.matmul(
                ps_s[:, 0:512], bufA[:, h, qt * 128:(qt + 1) * 128],
                kbuf[:, sl, ws:ws + 512], start=True, stop=False))
            ew(k, "tensor", lambda e, h=h, qt=qt, sl=sl, ws=ws: e.matmul(
                ps_s[:, 512:1024], bufA[:, h, qt * 128:(qt + 1) * 128],
                kbuf[:, sl, ws + 512:ws + 1024], start=True, stop=(qt >= 4)))
            ew(k, "tensor", lambda e, h=h, qt=qt, sl=sl, ws=ws: e.matmul(
                ps_s[:, 1024:1152], bufA[:, h, qt * 128:(qt + 1) * 128],
                kbuf[:, sl, ws + 1024:ws + 1152], start=True, stop=False))
            ew(k, "tensor", lambda e: e.matmul(
                ps_s[:, 0:512], k.ident, band[:, 0:512], start=False, stop=False))
            ew(k, "tensor", lambda e: e.matmul(
                ps_s[:, 1024:1152], k.ident, band[:, 1024:1152], start=False, stop=True))
            ew(k, "tensor", lambda e, ws=ws: e.matmul(
                ps_s[:, 0:512], k.ones[0:1, :], killv[0:1, ws:ws + 512],
                start=False, stop=True))
            if qt < 4:
                ew(k, "tensor", lambda e, ws=ws: e.matmul(
                    ps_s[:, 512:1024], k.ones[0:1, :], killv[0:1, ws + 512:ws + 1024],
                    start=False, stop=True))
            ps_marks[i] = milestone(k, "tensor", k.s_pe)
            if qt == 2 and 2 <= h + 1 < NH:
                wait(k, "sync", k.s_pe, av_marks[(h - 1) * 8 + 7])
                kv_load(h + 1)
        j = i - 1
        if 0 <= j < NIT:
            wait(k, "tensor", k.s_sc, pn_marks[j])
            if j >= 1:
                wait(k, "tensor", k.s_v, pte_marks[j - 1])
            for t9 in range(8):
                ew(k, "tensor", lambda e, j=j, t9=t9: e.transpose(
                    ps_pt[:, t9 * 128:(t9 + 1) * 128],
                    Pt[:, j % 2, t9 * 128:(t9 + 1) * 128], k.ident))
            ew(k, "tensor", lambda e, j=j: e.transpose(
                ps_pt2[:, 0:128], Pt[:, j % 2, 1024:1152], k.ident))
            pt_marks[j] = milestone(k, "tensor", k.s_pe)
        m = i - 2
        if m >= 0:
            hm, qtm = divmod(m, 8)
            slm = hm % 2
            wait(k, "tensor", k.s_v, pte_marks[m])
            if m >= 2:
                wait(k, "tensor", k.s_v, oe_marks[m - 2])
            for t9 in range(9):
                ew(k, "tensor", lambda e, m=m, t9=t9, slm=slm, qtm=qtm: e.matmul(
                    ps_ot[m % 2], vbuf[:, slm, qtm + t9, :],
                    ptile[:, m % 2, t9 * 128:(t9 + 1) * 128],
                    start=(t9 == 0), stop=(t9 == 8)))
            av_marks[m] = milestone(k, "tensor", k.s_pe)
        if i < NIT:
            c0s = (i % 2) * 8
            wait(k, "vector", k.s_pe, ps_marks[i])
            ew(k, "vector", lambda e, c0s=c0s: e.reduce_max(
                out=smal[:, c0s:c0s + 1], in_=ps_s[:, 0:1152], axis=AX.X),
               w=[tg(bufC_t, f"sm{i % 2}")])
            ew(k, "vector", lambda e, c0s=c0s: e.tensor_scalar_mul(
                out=smal[:, c0s + 1:c0s + 2], in0=smal[:, c0s:c0s + 1],
                scalar1=-ATT_SCALE),
               r=[tg(bufC_t, f"sm{i % 2}")], w=[tg(bufC_t, f"sm{i % 2}")])
            mx_marks[i] = milestone(k, "vector", k.s_v)
            wait(k, "scalar", k.s_v, mx_marks[i])
            ew(k, "scalar", lambda e, i=i, c0s=c0s: e.activation(
                out=Pt[:, i % 2, :], in_=ps_s[:, 0:1152], func=AF.Exp,
                scale=ATT_SCALE, bias=smal[:, c0s + 1:c0s + 2],
                accum_out=smal[:, c0s + 2:c0s + 3]),
               r=[tg(bufC_t, f"sm{i % 2}")],
               w=[tg(bufC_t, f"p{i % 2}"), tg(bufC_t, f"sm{i % 2}")])
            exp_marks[i] = milestone(k, "scalar", k.s_sc)
            wait(k, "vector", k.s_sc, exp_marks[i])

            def _rrin(e, c0s=c0s, nc_=k.nc):
                with nc_.allow_low_precision(reason="softmax denom reciprocal"):
                    return e.reciprocal(out=smal[:, c0s + 3:c0s + 4],
                                        in_=smal[:, c0s + 2:c0s + 3])
            ew(k, "vector", _rrin, r=[tg(bufC_t, f"sm{i % 2}")],
               w=[tg(bufC_t, f"sm{i % 2}")])
            mrin = milestone(k, "vector", k.s_v)
            wait(k, "scalar", k.s_v, mrin)
            ew(k, "scalar", lambda e, i=i, c0s=c0s: e.activation(
                out=Pt[:, i % 2, :], in_=Pt[:, i % 2, :], func=AF.Copy,
                scale=smal[:, c0s + 3:c0s + 4]),
               r=[tg(bufC_t, f"p{i % 2}"), tg(bufC_t, f"sm{i % 2}")],
               w=[tg(bufC_t, f"p{i % 2}")])
            pn_marks[i] = milestone(k, "scalar", k.s_sc)
        if 0 <= j < NIT:
            wait(k, "vector", k.s_pe, pt_marks[j])
            ew(k, "vector", lambda e, j=j: e.tensor_copy(
                out=ptile[:, j % 2, 0:1024], in_=ps_pt[:, :]),
               w=[tg(bufC_t, f"pt{j % 2}")])
            ew(k, "vector", lambda e, j=j: e.tensor_copy(
                out=ptile[:, j % 2, 1024:1152], in_=ps_pt2[:, 0:128]),
               w=[tg(bufC_t, f"pt{j % 2}")])
            pte_marks[j] = milestone(k, "vector", k.s_v)
        if m >= 0:
            hm, qtm = divmod(m, 8)
            wait(k, "vector", k.s_pe, av_marks[m])
            ew(k, "vector", lambda e, m=m, hm=hm, qtm=qtm: e.tensor_copy(
                out=xhat[:, hm, qtm * 128:(qtm + 1) * 128], in_=ps_ot[m % 2]),
               w=[tg(xhat, f"at{hm}.{qtm}")])
            oe_marks[m] = milestone(k, "vector", k.s_v)

    wait(k, "tensor", k.s_v)

    def evac_addres1(oc, si, c0, wd, ps):
        ew(k, "vector", lambda e, oc=oc, c0=c0, wd=wd, ps=ps: e.tensor_tensor(
            out=xres[:, oc, 1 + c0:1 + c0 + wd], in0=xres[:, oc, 1 + c0:1 + c0 + wd],
            in1=ps, op=ALU.add), r=[tg(xres, f"o{oc}.{si}")], w=[tg(xres, f"o{oc}.{si}")])

    emit_proj(k, 16, 16, lambda oc: wo_t[oc],
              lambda dc, c0, wd: xhat[:, dc, c0:c0 + wd],
              evac_addres1, "vector", wslab, ps_mm, 2)




def emit_ffn1(k, xres, xhat, bufA_t, bufB_t, bufC, bufC_t, ps_mm,
              W13_OFF, RANK2_SZ):
    occ2 = k.cc_w2_out_ap

    def w2h_slab_ap(dc):
        r, l = divmod(dc, 2)
        off = r * RANK2_SZ + l * 1048576
        return occ2[off:off + 1048576].rearrange("(p m) -> p m", p=128)

    def w13_slab_ap(fc, which):
        r, j = divmod(fc, 8)
        off = r * RANK2_SZ + W13_OFF + (which * 8 + j) * 262144
        return occ2[off:off + 262144].rearrange("(p m) -> p m", p=128)

    hA = bufA_t[:, 0:16384].rearrange("p (a b) -> p a b", b=512)
    hB = bufB_t[:, 0:16384].rearrange("p (a b) -> p a b", b=512)

    def hbuf(fc):
        return (hA if fc < 32 else hB)[:, fc % 32, :]

    def htag(fc):
        return tg(bufA_t if fc < 32 else bufB_t, fc % 32)

    gslab = bufC_t[:, 0:8192].rearrange("p (a b c) -> p a b c", a=4, c=128)
    sg = bufC_t[:, 10240:11264].rearrange("p (a b) -> p a b", a=2)
    pslab = bufC_t[:, 0:16384].rearrange("p (a b c) -> p a b c", b=64, c=128)
    wsl = (k.s_k, k.s_k2, k.s_k3, k.s_k4)

    wait(k, "scalar", k.s_v)
    wait(k, "scalar", k.s_pe)
    wait(k, "vector", k.s_pe)
    emit_ln(k, xres, xhat, ((1, 512), (513, 512)), bufC)

    for th in range(2):
        cx = 1 + th * 512
        wait(k, "sync", k.s_pe)
        wait(k, "gpsimd", k.s_pe)
        wait(k, "vector", k.s_pe)
        wait(k, "scalar", k.s_pe)
        wait(k, "tensor", k.s_v)
        wait(k, "tensor", k.s_sc)
        if th == 0:
            wait(k, "sync", k.s_cc, k.m_ccw2)
            wait(k, "gpsimd", k.s_cc, k.m_ccw2)
        load_marks = {}

        def w_load(fc, which):
            slot = (2 * fc + which) % 4
            sem = (k.s_k, k.s_w, k.s_k2, k.s_w2)[slot]
            qn = ("sync", "gpsimd")[which]
            lm = sem.inc(16)
            ew(k, qn, lambda e, fc=fc, which=which, slot=slot, h=sem.h: e.dma_start(
                out=gslab[:, slot, :, :].rearrange("p a b -> p (a b)"),
                in_=w13_slab_ap(fc, which)).then_inc(h, 16))
            load_marks[(fc, which)] = lm

        w_load(0, 0)
        w_load(0, 1)
        w_load(1, 0)
        w_load(1, 1)
        gmark = {}
        umark = {}
        hmark = {}
        smark = {}
        for fc in range(64):
            slot_g = (2 * fc) % 4
            slot_u = (2 * fc + 1) % 4
            sems4 = (k.s_k, k.s_w, k.s_k2, k.s_w2)
            wait(k, "tensor", sems4[slot_g], load_marks[(fc, 0)])
            if fc >= 2:
                wait(k, "tensor", k.s_sc, smark[fc - 2])
            for dc in range(NCD):
                ew(k, "tensor", lambda e, dc=dc, slot_g=slot_g, cx=cx: e.matmul(
                    ps_mm[:, slot_g, :], gslab[:, slot_g, dc, :],
                    xhat[:, dc, cx:cx + 512], start=(dc == 0), stop=(dc == NCD - 1)))
            gmark[fc] = milestone(k, "tensor", k.s_pe)
            wait(k, "tensor", sems4[slot_u], load_marks[(fc, 1)])
            if fc >= 2:
                wait(k, "tensor", k.s_v, hmark[fc - 2])
            for dc in range(NCD):
                ew(k, "tensor", lambda e, dc=dc, slot_u=slot_u, cx=cx: e.matmul(
                    ps_mm[:, slot_u, :], gslab[:, slot_u, dc, :],
                    xhat[:, dc, cx:cx + 512], start=(dc == 0), stop=(dc == NCD - 1)))
            umark[fc] = milestone(k, "tensor", k.s_pe)
            if fc + 2 < 64:
                wait(k, "sync", k.s_pe, gmark[fc])
                w_load(fc + 2, 0)
                wait(k, "gpsimd", k.s_pe, umark[fc])
                w_load(fc + 2, 1)
            wait(k, "scalar", k.s_pe, gmark[fc])
            if fc >= 2:
                wait(k, "scalar", k.s_v, hmark[fc - 2])
            ew(k, "scalar", lambda e, fc=fc, slot_g=slot_g: e.activation(
                out=sg[:, fc % 2, :], in_=ps_mm[:, slot_g, :], func=AF.Silu),
               w=[f"bufC:sg{fc % 2}"])
            smark[fc] = milestone(k, "scalar", k.s_sc)
            wait(k, "vector", k.s_pe, umark[fc])
            wait(k, "vector", k.s_sc, smark[fc])
            ew(k, "vector", lambda e, fc=fc, slot_u=slot_u: e.tensor_tensor(
                out=hbuf(fc), in0=sg[:, fc % 2, :], in1=ps_mm[:, slot_u, :],
                op=ALU.mult), r=[f"bufC:sg{fc % 2}"], w=[htag(fc)])
            hmark[fc] = milestone(k, "vector", k.s_v)
        mhn = milestone(k, "vector", k.s_v)
        wait(k, "tensor", k.s_v, mhn)

        def evac_ffn1(oc, si, c0_, wd, ps, th=th, cx=cx):
            ew(k, "vector", lambda e, oc=oc, ps=ps, cx=cx: e.tensor_tensor(
                out=xres[:, oc, cx:cx + 512], in0=xres[:, oc, cx:cx + 512],
                in1=ps, op=ALU.add),
               r=[tg(xres, f"g{oc}.{th}")], w=[tg(xres, f"g{oc}.{th}")])

        emit_proj(k, 16, 64, w2h_slab_ap, lambda dc, c0_, wd: hbuf(dc),
                  evac_ffn1, "vector", pslab, ps_mm, 4, segs=((0, 512),),
                  w_queue="sync", w_sems=(k.s_k, k.s_k2), load_wait=(k.s_v, mhn))



def build(debug_stop=None):
    nc = bass.Bass("TRN2", target_bir_lowering=False)
    es = ExitStack()

    xT = nc.dram_tensor("xT", [NCD, 128, W], F32, kind="ExternalInput").ap()
    wr_t = nc.dram_tensor("wr_t", [16, 128, D], F32, kind="ExternalInput").ap()
    wk_t = nc.dram_tensor("wk_t", [16, 128, D], F32, kind="ExternalInput").ap()
    wv_t = nc.dram_tensor("wv_t", [16, 128, D], F32, kind="ExternalInput").ap()
    wo_t = nc.dram_tensor("wo_t", [16, 128, D], F32, kind="ExternalInput").ap()
    vecs = nc.dram_tensor("vecs", [128, 6, 16], F32, kind="ExternalInput").ap()
    cid = nc.dram_tensor("cid", [128, 3, 128], BF16, kind="ExternalInput").ap()
    outT = nc.dram_tensor("outT", [NCD, 128, T], F32, kind="ExternalOutput").ap()

    l1wq_t = nc.dram_tensor("l1wq_t", [16, 128, D], F32, kind="ExternalInput").ap()
    l1wk_t = nc.dram_tensor("l1wk_t", [16, 128, D], F32, kind="ExternalInput").ap()
    l1wv_t = nc.dram_tensor("l1wv_t", [16, 128, D], F32, kind="ExternalInput").ap()
    l1wo_t = nc.dram_tensor("l1wo_t", [16, 128, D], F32, kind="ExternalInput").ap()
    w1_sl = nc.dram_tensor("w1_sl", [8, 128, D], F32, kind="ExternalInput").ap()
    w3_sl = nc.dram_tensor("w3_sl", [8, 128, D], F32, kind="ExternalInput").ap()
    ropec_d = nc.dram_tensor("ropec_d", [128, 2, 1024], BF16, kind="ExternalInput").ap()
    band_d = nc.dram_tensor("band_d", [128, 1152], BF16, kind="ExternalInput").ap()
    killv_d = nc.dram_tensor("killv_d", [1, 2048], BF16, kind="ExternalInput").ap()
    wgu_slab = nc.dram_tensor("wgu_slab", [2048, 2048], F32, kind="ExternalInput").ap()
    wproj_slab = nc.dram_tensor("wproj_slab", [256, FFH], F32, kind="ExternalInput").ap()
    w2_slab = nc.dram_tensor("w2_slab", [256, FFH], F32, kind="ExternalInput").ap()

    cc_s_in = nc.dram_tensor("cc_s_in", [128, 16], F32)
    cc_s_out = nc.dram_tensor("cc_s_out", [256, 16], F32)
    RANK_SZ = 16 * 262144 + 2 * 1048576
    WP_OFF = 16 * 262144
    RANK2_SZ = 2 * 1048576 + 16 * 262144
    W13_OFF = 2 * 1048576
    cc_w_in = nc.dram_tensor("cc_w_in", [RANK_SZ], BF16)
    cc_w_out = nc.dram_tensor("cc_w_out", [8 * RANK_SZ], BF16)
    cc_w2_in = nc.dram_tensor("cc_w2_in", [RANK2_SZ], BF16)
    cc_w2_out = nc.dram_tensor("cc_w2_out", [8 * RANK2_SZ], BF16)
    cc_kv_in = nc.dram_tensor("cc_kv_in", [32 * 131072], BF16)
    cc_kv_out = nc.dram_tensor("cc_kv_out", [64 * 131072], BF16)
    k_rank_sz = RANK_SZ

    k = K()
    k.nc = nc
    k.em = Em()

    for s in ("s_ld", "s_w", "s_w2", "s_w3", "s_w4", "s_k", "s_k2", "s_k3", "s_k4", "s_pe", "s_v", "s_sc", "s_g", "s_cc", "s_st"):
        setattr(k, s, Sem(es.enter_context(nc.semaphore(s))))

    onesid = es.enter_context(nc.sbuf_tensor("onesid", [128, 3, 128], BF16))[:]
    k.ones = onesid[:, 0, :]
    k.ident = onesid[:, 1, :]
    k.swap = onesid[:, 2, :]
    killv = es.enter_context(nc.sbuf_tensor("killv", [1, 2048], BF16))[:]
    vtile = es.enter_context(nc.sbuf_tensor("vtile", [128, 10, 16], F32))[:]
    stile = es.enter_context(nc.sbuf_tensor("stile", [128, 3, 16], F32))[:]

    k.s_ld.inc(16)
    ew(k, "sync", lambda e, h=k.s_ld.h: e.dma_start(out=onesid, in_=cid).then_inc(h, 16))
    k.s_ld.inc(16)
    ew(k, "sync", lambda e, h=k.s_ld.h: e.dma_start(
        out=vtile[:, 0:6, :], in_=vecs).then_inc(h, 16))
    k.s_ld.inc(16)
    ew(k, "sync", lambda e, h=k.s_ld.h: e.dma_start(out=killv[:], in_=killv_d).then_inc(h, 16))
    m_in = k.s_ld.n
    for e in ("scalar", "vector", "tensor"):
        wait(k, e, k.s_ld, m_in)

    for i in range(3):
        ew(k, "vector", lambda e, i=i: e.tensor_scalar(
            out=vtile[:, 6 + i, :], in0=vtile[:, i, :], scalar1=-1.0, scalar2=1.0,
            op0=ALU.mult, op1=ALU.add), r=[vtile], w=[vtile])
    ew(k, "scalar", lambda e: e.activation(out=vtile[:, 9, :], in_=vtile[:, 3, :],
                                           func=AF.Exp), r=[vtile], w=[vtile])
    ew(k, "scalar", lambda e: e.activation(out=vtile[:, 9, :], in_=vtile[:, 9, :],
                                           func=AF.Exp, scale=-1.0), r=[vtile], w=[vtile])
    m_vec_s = milestone(k, "scalar", k.s_sc)
    milestone(k, "vector", k.s_v)

    k.cc_w_out_ap = cc_w_out.ap()
    k.cc_w2_out_ap = cc_w2_out.ap()
    emit_phase0(k, wgu_slab, wproj_slab, w2_slab, cc_w_in, WP_OFF,
                cc_w2_in, w1_sl, w3_sl, W13_OFF)
    m_p0 = k.s_g.n

    xres = es.enter_context(nc.sbuf_tensor("xres", [128, NCD, W], F32))[:]
    xhat = es.enter_context(nc.sbuf_tensor("xhat", [128, NCD, W], BF16))[:]
    bufA_t = es.enter_context(nc.sbuf_tensor("bufA", [128, NCD * W], BF16))[:]
    bufA = bufA_t.rearrange("p (a b) -> p a b", b=W)
    bufB_t = es.enter_context(nc.sbuf_tensor("bufB", [128, NCD * W], BF16))[:]
    bufB = bufB_t.rearrange("p (a b) -> p a b", b=W)
    bufC_t = es.enter_context(nc.sbuf_tensor("bufC", [128, NCD * W], BF16))[:]
    bufC = bufC_t.rearrange("p (a b) -> p a b", b=W)
    k.stat_scr = bufA_t[:, 0:3072].bitcast(F32).rearrange("p (a b) -> p a b", a=3)
    k.rb = bufA_t[:, 3072:5120].rearrange("p (a b) -> p a b", a=2)
    k.mrb = bufA_t[:, 5120:7168].rearrange("p (a b) -> p a b", a=2)
    wslab = es.enter_context(nc.sbuf_tensor("wslab", [128, 2, 16, 128], BF16))[:]

    k.ps_ln = es.enter_context(nc.psum_tensor("ps_ln", [128, 2, 512], F32))[:]
    ps_mm = es.enter_context(nc.psum_tensor("ps_mm", [128, 4, 512], F32))[:]
    ps_r = es.enter_context(nc.psum_tensor("ps_r", [128, 1, 512], F32))[:]
    k.ps_r = ps_r

    wait(k, "sync", k.s_g, m_p0)
    for c in range(NCD):
        k.s_ld.inc(16)
        ew(k, "sync", lambda e, c=c, h=k.s_ld.h: e.dma_start(
            out=xres[:, c, :], in_=xT[c]).then_inc(h, 16))
    m_in = k.s_ld.n
    for e in ("scalar", "vector", "tensor"):
        wait(k, e, k.s_ld, m_in)

    emit_ln(k, xres, xhat, ((0, 512), (512, 512), (1024, 1)), bufC)
    wait(k, "vector", k.s_sc, m_vec_s)

    def emit_mix(tm_i, om_i, dstbuf):
        for c in range(NCD):
            ew(k, "vector", lambda e, c=c, om_i=om_i, dstbuf=dstbuf: e.tensor_scalar_mul(
                out=dstbuf[:, c, 0:T], in0=xhat[:, c, 0:T],
                scalar1=vtile[:, om_i, c:c + 1]), r=[xhat, vtile], w=[tg(dstbuf, c)])
        for c in range(NCD):
            ew(k, "vector", lambda e, c=c, tm_i=tm_i, dstbuf=dstbuf: e.scalar_tensor_tensor(
                out=dstbuf[:, c, 0:T], in0=xhat[:, c, 1:W], scalar=vtile[:, tm_i, c:c + 1],
                in1=dstbuf[:, c, 0:T], op0=ALU.mult, op1=ALU.add),
               r=[xhat, vtile, tg(dstbuf, c)], w=[tg(dstbuf, c)])
        return milestone(k, "vector", k.s_v)

    def proj_rhs(buf):
        return lambda dc, c0, wd: buf[:, dc, c0:c0 + wd]

    def evac_copy_to(buf, act=None):
        def f(oc, si, c0, wd, ps):
            if act is None:
                ew(k, "vector", lambda e, oc=oc, c0=c0, wd=wd, ps=ps: e.tensor_copy(
                    out=buf[:, oc, c0:c0 + wd], in_=ps), w=[tg(buf, oc)])
            else:
                ew(k, "scalar", lambda e, oc=oc, c0=c0, wd=wd, ps=ps: e.activation(
                    out=buf[:, oc, c0:c0 + wd], in_=ps, func=act), w=[tg(buf, oc)])
        return f

    mk = emit_mix(0, 6, bufB)
    mvx = emit_mix(1, 7, bufA)
    wait(k, "tensor", k.s_v, mk)
    mm_k = emit_proj(k, 16, 16, lambda oc: wk_t[oc], proj_rhs(bufB),
                     evac_copy_to(bufC), "vector", wslab, ps_mm, 4)
    wait(k, "tensor", k.s_v, mvx)
    mm_v = emit_proj(k, 16, 16, lambda oc: wv_t[oc], proj_rhs(bufA),
                     evac_copy_to(bufB), "vector", wslab, ps_mm, 4)
    wait(k, "vector", k.s_pe, mm_v[15])
    mr = emit_mix(2, 8, bufA)
    wait(k, "tensor", k.s_v, mr)
    emit_proj(k, 16, 16, lambda oc: wr_t[oc], proj_rhs(bufA),
              evac_copy_to(xhat, act=AF.Sigmoid), "scalar", wslab, ps_mm, 4)
    m_rdone = k.s_sc.n

    for c in range(NCD):
        ew(k, "vector", lambda e, c=c: e.tensor_tensor(
            out=bufC[:, c, 0:T], in0=bufC[:, c, 0:T], in1=bufB[:, c, 0:T], op=ALU.mult),
           r=[tg(bufB, c), tg(bufC, c)], w=[tg(bufC, c)])
    wait(k, "vector", k.s_pe)
    for c in range(NCD):
        ew(k, "vector", lambda e, c=c: e.tensor_tensor_scan(
            out=bufB[:, c, 1:W], data0=vtile[:, 9, c:c + 1].broadcast_to([128, T]),
            data1=bufC[:, c, 0:T], initial=0.0, op0=ALU.mult, op1=ALU.add),
           r=[tg(bufC, c), vtile], w=[tg(bufB, c)])
    ew(k, "vector", lambda e: e.tensor_copy(
        out=stile[:, 0, :], in_=bufB[:, :, W - 1:W].rearrange("p c o -> p (c o)")),
       r=[bufB], w=[stile])
    m_scanA = milestone(k, "vector", k.s_v)
    wait(k, "gpsimd", k.s_v, m_scanA)
    k.s_g.inc(16)
    ew(k, "gpsimd", lambda e, h=k.s_g.h: e.dma_start(
        out=cc_s_in[:, :], in_=stile[:, 0, :]).then_inc(h, 16))
    wait(k, "gpsimd", k.s_g)
    m_cc = k.s_cc.inc(1)
    ew(k, "gpsimd", lambda e, h=k.s_cc.h: e.collective_compute(
        "AllGather", ALU.bypass, replica_groups=PAIRS,
        ins=[cc_s_in.ap().opt()], outs=[cc_s_out.ap().opt()]).then_inc(h, 1))
    wait(k, "gpsimd", k.s_cc, m_cc)
    m_srecv = k.s_g.inc(16)
    ew(k, "gpsimd", lambda e, h=k.s_g.h: e.dma_start(
        out=stile[:, 1, :], in_=cc_s_out.ap()[0:128, :]).then_inc(h, 16))
    wait(k, "vector", k.s_g, m_srecv)
    ew(k, "vector", lambda e: e.tensor_tensor(
        out=stile[:, 2, :], in0=stile[:, 1, :], in1=vtile[:, 5, :], op=ALU.mult),
       r=[stile, vtile], w=[stile])
    for c in range(NCD):
        ew(k, "vector", lambda e, c=c: e.tensor_tensor_scan(
            out=bufB[:, c, 1:W], data0=vtile[:, 9, c:c + 1].broadcast_to([128, T]),
            data1=bufC[:, c, 0:T], initial=stile[:, 2, c:c + 1], op0=ALU.mult, op1=ALU.add),
           r=[tg(bufC, c), vtile, stile], w=[tg(bufB, c)])
    ew(k, "vector", lambda e: e.tensor_copy(
        out=bufB[:, :, 0:1].rearrange("p c o -> p (c o)"), in_=stile[:, 2, :]),
       r=[stile], w=[bufB])
    wait(k, "vector", k.s_sc, m_rdone)
    wait(k, "vector", k.s_pe)
    for c in range(NCD):
        ew(k, "vector", lambda e, c=c: e.scalar_tensor_tensor(
            out=bufA[:, c, 0:T], in0=bufC[:, c, 0:T], scalar=vtile[:, 4, c:c + 1],
            in1=bufB[:, c, 0:T], op0=ALU.mult, op1=ALU.add),
           r=[tg(bufC, c), tg(bufB, c), vtile], w=[tg(bufA, c)])
    for c in range(NCD):
        ew(k, "vector", lambda e, c=c: e.tensor_tensor(
            out=bufC[:, c, 0:T], in0=bufA[:, c, 0:T], in1=xhat[:, c, 0:T], op=ALU.mult),
           r=[tg(bufA, c), xhat], w=[tg(bufC, c)])
    m_wkv = milestone(k, "vector", k.s_v)
    wait(k, "tensor", k.s_v, m_wkv)

    def evac_addres(oc, si, c0, wd, ps):
        ew(k, "vector", lambda e, oc=oc, c0=c0, wd=wd, ps=ps: e.tensor_tensor(
            out=xres[:, oc, 1 + c0:1 + c0 + wd], in0=xres[:, oc, 1 + c0:1 + c0 + wd],
            in1=ps, op=ALU.add), r=[tg(xres, f"{oc}.{si}")], w=[tg(xres, f"{oc}.{si}")])

    emit_proj(k, 16, 16, lambda oc: wo_t[oc], proj_rhs(bufC),
              evac_addres, "vector", wslab, ps_mm, 4)

    emit_ffn0(k, xres, xhat, bufA_t, bufB_t, bufC, bufC_t, wslab, ps_mm, ps_r,
              WP_OFF, RANK_SZ)

    emit_l1attn(k, xres, xhat, bufA, bufB, bufC, bufC_t, wslab, ps_mm,
                l1wq_t, l1wk_t, l1wv_t, l1wo_t, ropec_d, band_d, killv,
                cc_kv_in, cc_kv_out)

    emit_ffn1(k, xres, xhat, bufA_t, bufB_t, bufC, bufC_t, ps_mm,
              W13_OFF, RANK2_SZ)

    m_done_v = milestone(k, "vector", k.s_v)
    m_done_s = milestone(k, "scalar", k.s_sc)
    wait(k, "sync", k.s_v, m_done_v)
    wait(k, "sync", k.s_sc, m_done_s)
    for c in range(NCD):
        k.s_st.inc(16)
        ew(k, "sync", lambda e, c=c, h=k.s_st.h: e.dma_start(
            out=outT[c], in_=xres[:, c, 1:W]).then_inc(h, 16))
    wait(k, "sync", k.s_st)

    def _replay(e, lst):
        for f, posts in lst:
            r = f(e)
            for s in posts:
                r.then_inc(s, 1)

    with nc.Block() as block:
        @block.sync
        def _(e):
            _replay(e, k.em.q["sync"])

        @block.scalar
        def _(e):
            _replay(e, k.em.q["scalar"])

        @block.vector
        def _(e):
            _replay(e, k.em.q["vector"])

        @block.tensor
        def _(e):
            _replay(e, k.em.q["tensor"])

        @block.gpsimd
        def _(e):
            _replay(e, k.em.q["gpsimd"])

    es.close()
    return nc



def _tile_lhsT(Wm, contract_axis):
    if contract_axis == 1:
        O, Kd = Wm.shape
        Oc, Kc = O // 128, Kd // 128
        t = Wm.reshape(Oc, 128, Kc, 128).transpose(0, 3, 2, 1)
    else:
        Kd, O = Wm.shape
        Kc, Oc = Kd // 128, O // 128
        t = Wm.reshape(Kc, 128, Oc, 128).transpose(2, 1, 0, 3)
    return np.ascontiguousarray(t.reshape(Oc, 128, Kc * 128))


def kernel(**inputs):
    L0x = np.asarray(inputs["L0_attn_hidden_states"])

    wr_t = _tile_lhsT(np.asarray(inputs["L0_attn_Wr"]), 1)
    wk_t = _tile_lhsT(np.asarray(inputs["L0_attn_Wk"]), 1)
    wv_t = _tile_lhsT(np.asarray(inputs["L0_attn_Wv"]), 1)
    wo_t = _tile_lhsT(np.asarray(inputs["L0_attn_Wo"]), 1)

    def vec_tile(v):
        return np.ascontiguousarray(np.asarray(v).reshape(-1).reshape(16, 128).T)

    tmk = vec_tile(inputs["L0_attn_time_mix_k"])
    tmv = vec_tile(inputs["L0_attn_time_mix_v"])
    tmr = vec_tile(inputs["L0_attn_time_mix_r"])
    td = vec_tile(inputs["L0_attn_time_decay"])
    tf = vec_tile(inputs["L0_attn_time_first"])

    w1_t = _tile_lhsT(np.asarray(inputs["L1_ffn_W1"]), 1)
    w3_t = _tile_lhsT(np.asarray(inputs["L1_ffn_W3"]), 1)
    l1wq_t = _tile_lhsT(np.asarray(inputs["L1_attn_wq"]), 0)
    l1wk_t = _tile_lhsT(np.asarray(inputs["L1_attn_wk"]), 0)
    l1wv_t = _tile_lhsT(np.asarray(inputs["L1_attn_wv"]), 0)
    l1wo_t = _tile_lhsT(np.asarray(inputs["L1_attn_wo"]), 0)

    wgu = np.asarray(inputs["L0_ffn_w_gate_up"])
    wproj = np.asarray(inputs["L0_ffn_w_proj"])
    W2 = np.asarray(inputs["L1_ffn_W2"])

    cid = np.zeros((128, 3, 128), dtype=ml_dtypes.bfloat16)
    cid[:, 0, :] = 1.0
    cid[:, 1, :] = np.eye(128)
    sw = np.zeros((128, 128), dtype=np.float32)
    for ii in range(128):
        sw[ii, ii ^ 64] = 1.0
    cid[:, 2, :] = sw

    ii = np.arange(128)[:, None]
    jj = np.arange(1152)[None, :]
    band = np.where((jj >= ii + 1) & (jj <= ii + 1024), 0.0, NEG).astype(ml_dtypes.bfloat16)
    inv_freq = (1.0 / 10000.0 ** (np.arange(0, 128, 2, dtype=np.float32) / 128.0))

    in_maps = []
    for c in range(8):
        b, p = c // 2, c % 2
        t0 = p * T
        xTc = np.zeros((NCD, 128, W), dtype=np.float32)
        xTc[:, :, 1:] = L0x[b, t0:t0 + T, :].T.reshape(NCD, 128, T)
        if p == 1:
            xTc[:, :, 0] = L0x[b, t0 - 1, :].reshape(NCD, 128)
        vv = np.stack([tmk, tmv, tmr, td, tf,
                       np.full((128, 16), 1.0 if p == 1 else 0.0, np.float32)], axis=1)
        pos = np.arange(t0, t0 + T, dtype=np.float32)
        fr = pos[None, :] * inv_freq[:, None]
        cos2 = np.concatenate([np.cos(fr), np.cos(fr)], axis=0)
        sin2 = np.concatenate([-np.sin(fr), np.sin(fr)], axis=0)
        ropec_h = np.stack([cos2, sin2], axis=1).astype(ml_dtypes.bfloat16)
        killv_h = np.zeros((1, 2048), dtype=ml_dtypes.bfloat16)
        if p == 0:
            killv_h[0, 0:1024] = NEG
        in_maps.append({
            "xT": np.ascontiguousarray(xTc),
            "wr_t": wr_t, "wk_t": wk_t, "wv_t": wv_t, "wo_t": wo_t,
            "vecs": np.ascontiguousarray(vv.astype(np.float32)),
            "cid": cid,
            "wgu_slab": np.ascontiguousarray(wgu[c * 2048:(c + 1) * 2048, :]),
            "wproj_slab": np.ascontiguousarray(wproj[c * 256:(c + 1) * 256, :]),
            "w2_slab": np.ascontiguousarray(W2[c * 256:(c + 1) * 256, :]),
            "l1wq_t": l1wq_t, "l1wk_t": l1wk_t, "l1wv_t": l1wv_t, "l1wo_t": l1wo_t,
            "w1_sl": np.ascontiguousarray(w1_t[c * 8:(c + 1) * 8]),
            "w3_sl": np.ascontiguousarray(w3_t[c * 8:(c + 1) * 8]),
            "ropec_d": np.ascontiguousarray(ropec_h),
            "band_d": band, "killv_d": killv_h,
        })

    nc = build()
    res = run_bass_kernel_spmd(nc, in_maps, core_ids=list(range(8)))

    out = np.empty((4, 2048, 2048), dtype=np.float32)
    for c in range(8):
        b, p = c // 2, c % 2
        out[b, p * T:(p + 1) * T, :] = res.results[c]["outT"].reshape(D, T).T
    return out


# version 07
# speedup vs baseline: 1.0527x; 1.0312x over previous
import numpy as np
import ml_dtypes
from contextlib import ExitStack

import concourse.bass as bass
import concourse.mybir as mybir
from concourse.bass_utils import run_bass_kernel_spmd

F32 = mybir.dt.float32
BF16 = mybir.dt.bfloat16
AF = mybir.ActivationFunctionType
ALU = mybir.AluOpType
AX = mybir.AxisListType

ENGINES = ("sync", "scalar", "vector", "tensor", "gpsimd")

D = 2048
NCD = 16
T = 1024
W = T + 1
NH = 16
HDIM = 128
FFH = 8192
NFH = FFH // 128
NEG = -1.0e9
LN_EPS = 1e-5
RMS_EPS = 1.1920929e-7
ATT_SCALE = 0.08838834764831845
PAIRS = [[0, 1], [2, 3], [4, 5], [6, 7]]


class Sem:
    __slots__ = ("h", "n")

    def __init__(self, h):
        self.h = h
        self.n = 0

    def inc(self, k=1):
        self.n += k
        return self.n


class Em:

    def __init__(self):
        self.q = {e: [] for e in ENGINES}
        self.dirty = {e: set() for e in ENGINES}
        self.rdirty = {e: set() for e in ENGINES}

    @staticmethod
    def _keys(aps):
        out = set()
        for a in aps:
            if a is None:
                continue
            s = a if isinstance(a, str) else a.name
            name, _, tag = s.partition(":")
            out.add((name, tag or None))
        return out

    @staticmethod
    def _conflict(ks, dirty):
        for n, t in ks:
            for dn, dt in dirty:
                if n == dn and (t is None or dt is None or t == dt):
                    return True
        return False

    def on(self, e, f, r=(), w=()):
        rk = self._keys(r)
        wk = self._keys(w)
        if (rk and self._conflict(rk, self.dirty[e])) or \
           (wk and (self._conflict(wk, self.rdirty[e]) or self._conflict(wk, self.dirty[e]))):
            self.q[e].append([lambda eng: eng.drain(), []])
            self.dirty[e] = set()
            self.rdirty[e] = set()
        self.q[e].append([f, []])
        self.dirty[e] |= wk
        self.rdirty[e] |= rk

    def drain(self, e):
        if self.dirty[e] or self.rdirty[e]:
            self.q[e].append([lambda eng: eng.drain(), []])
            self.dirty[e] = set()
            self.rdirty[e] = set()


class K:
    pass


def ew(k, eng, f, r=(), w=()):
    k.em.on(eng, f, r, w)


def tg(ap, tag):
    return f"{ap.name}:{tag}"


def wait(k, eng, sem, mark=None):
    v = sem.n if mark is None else mark
    if v > 0:
        k.em.on(eng, lambda e, s=sem.h, v=v: e.wait_ge(s, v))


def milestone(k, eng, sem):
    q = k.em.q[eng]
    if not q or q[-1][1]:
        k.em.q[eng].append([lambda e: e.drain(), []])
        k.em.dirty[eng] = set()
        v = sem.inc(1)
        k.em.on(eng, lambda e, s=sem.h: e.nop().then_inc(s, 1))
        return v
    prev = sem.n
    if prev > 0:
        q.insert(len(q) - 1, [lambda e, s=sem.h, p=prev: e.wait_ge(s, p), []])
    v = sem.inc(1)
    q[-1][1].append(sem.h)
    return v


SEGS_T = ((0, 512), (512, 512))



def emit_ln(k, src, dst, segs, sqbuf):
    stat, rb, mrb, ps = k.stat_scr, k.rb, k.mrb, k.ps_ln
    for si, (c0, wd) in enumerate(segs):
        ps_s = ps[:, 0, 0:wd]
        ps_q = ps[:, 1, 0:wd]
        for c in range(NCD):
            ew(k, "scalar", lambda e, c=c, c0=c0, wd=wd: e.activation(
                out=dst[:, c, c0:c0 + wd], in_=src[:, c, c0:c0 + wd], func=AF.Copy),
               r=[src], w=[dst])
        m_cast = milestone(k, "scalar", k.s_sc)
        wait(k, "vector", k.s_sc, m_cast)
        wait(k, "tensor", k.s_sc, m_cast)
        for c in range(NCD):
            ew(k, "vector", lambda e, c=c, c0=c0, wd=wd: e.tensor_tensor(
                out=sqbuf[:, c, 0:wd], in0=dst[:, c, c0:c0 + wd],
                in1=dst[:, c, c0:c0 + wd], op=ALU.mult), r=[dst], w=[sqbuf])
        m_sq = milestone(k, "vector", k.s_v)
        for c in range(NCD):
            ew(k, "tensor", lambda e, c=c, c0=c0, wd=wd, ps_s=ps_s: e.matmul(
                ps_s, k.ones, dst[:, c, c0:c0 + wd], start=(c == 0), stop=(c == NCD - 1)))
        wait(k, "tensor", k.s_v, m_sq)
        for c in range(NCD):
            ew(k, "tensor", lambda e, c=c, wd=wd, ps_q=ps_q: e.matmul(
                ps_q, k.ones, sqbuf[:, c, 0:wd], start=(c == 0), stop=(c == NCD - 1)))
        m_mm = milestone(k, "tensor", k.s_pe)
        wait(k, "vector", k.s_pe, m_mm)
        mean = stat[:, 0, 0:wd]
        var = stat[:, 1, 0:wd]
        msq = stat[:, 2, 0:wd]
        ew(k, "vector", lambda e, mean=mean, ps_s=ps_s: e.tensor_scalar_mul(
            out=mean, in0=ps_s, scalar1=1.0 / D), w=[stat])
        ew(k, "vector", lambda e, var=var, ps_q=ps_q: e.tensor_scalar(
            out=var, in0=ps_q, scalar1=1.0 / D, scalar2=LN_EPS, op0=ALU.mult, op1=ALU.add),
           w=[stat])
        ew(k, "vector", lambda e, mean=mean, msq=msq: e.tensor_tensor(
            out=msq, in0=mean, in1=mean, op=ALU.mult), r=[stat], w=[stat])
        ew(k, "vector", lambda e, var=var, msq=msq: e.tensor_tensor(
            out=var, in0=var, in1=msq, op=ALU.subtract), r=[stat], w=[stat])
        m_v1 = milestone(k, "vector", k.s_v)
        wait(k, "scalar", k.s_v, m_v1)
        ew(k, "scalar", lambda e, var=var: e.activation(out=var, in_=var, func=AF.Sqrt),
           r=[stat], w=[stat])
        m_s2 = milestone(k, "scalar", k.s_sc)
        wait(k, "vector", k.s_sc, m_s2)
        rbs = rb[:, si % 2, 0:wd]
        mrbs = mrb[:, si % 2, 0:wd]

        def _recip(e, var=var, rbs=rbs, nc_=k.nc):
            with nc_.allow_low_precision(reason="rstd broadcast tile stored bf16"):
                return e.reciprocal(out=rbs, in_=var)
        ew(k, "vector", _recip, r=[stat], w=[rb])
        ew(k, "vector", lambda e, mean=mean, rbs=rbs, mrbs=mrbs: e.tensor_tensor(
            out=mrbs, in0=mean, in1=rbs, op=ALU.mult), r=[stat, rb], w=[mrb])
        for c in range(NCD):
            ew(k, "vector", lambda e, c=c, c0=c0, wd=wd, rbs=rbs: e.tensor_tensor(
                out=dst[:, c, c0:c0 + wd], in0=dst[:, c, c0:c0 + wd], in1=rbs, op=ALU.mult),
               r=[tg(dst, c), rb], w=[tg(dst, c)])
        for c in range(NCD):
            ew(k, "vector", lambda e, c=c, c0=c0, wd=wd, mrbs=mrbs: e.tensor_tensor(
                out=dst[:, c, c0:c0 + wd], in0=dst[:, c, c0:c0 + wd], in1=mrbs,
                op=ALU.subtract), r=[tg(dst, c), mrb], w=[tg(dst, c)])
    milestone(k, "vector", k.s_v)



def emit_proj(k, n_oc, n_kc, w_src, rhs, evac, evac_eng, wslab, ps_pool, n_banks,
              segs=SEGS_T, w_queue="gpsimd", w_sems=None, load_wait=None):
    w_sems = w_sems or (k.s_w, k.s_w2)
    evac_sem = k.s_v if evac_eng == "vector" else k.s_sc
    nseg = len(segs)
    evac_marks = {}
    load_marks = {}
    mm_marks = {}

    def emit_load(oc):
        if oc >= 2:
            wait(k, w_queue, k.s_pe, mm_marks[oc - 2])
        else:
            wait(k, w_queue, k.s_pe)
            if load_wait is not None:
                wait(k, w_queue, load_wait[0], load_wait[1])
        ld = w_sems[oc % 2].inc(16)
        ew(k, w_queue, lambda e, oc=oc, h=w_sems[oc % 2].h: e.dma_start(
            out=wslab[:, oc % 2, :, :].rearrange("p a b -> p (a b)"),
            in_=w_src(oc)).then_inc(h, 16))
        load_marks[oc] = ld

    emit_load(0)
    if n_oc > 1:
        emit_load(1)
    for oc in range(n_oc):
        wait(k, "tensor", w_sems[oc % 2], load_marks[oc])
        if oc >= max(1, n_banks // nseg):
            wait(k, "tensor", evac_sem, evac_marks[oc - max(1, n_banks // nseg)])
        elif oc == 0:
            wait(k, "tensor", k.s_v)
            wait(k, "tensor", k.s_sc)
        for si, (c0, wd) in enumerate(segs):
            pb = (nseg * oc + si) % n_banks
            for dc in range(n_kc):
                ew(k, "tensor", lambda e, oc=oc, dc=dc, c0=c0, wd=wd, pb=pb: e.matmul(
                    ps_pool[:, pb, 0:wd], wslab[:, oc % 2, dc, :], rhs(dc, c0, wd),
                    start=(dc == 0), stop=(dc == n_kc - 1)))
        mm = milestone(k, "tensor", k.s_pe)
        mm_marks[oc] = mm
        if oc + 2 < n_oc:
            emit_load(oc + 2)
        wait(k, evac_eng, k.s_pe, mm)
        for si, (c0, wd) in enumerate(segs):
            pb = (nseg * oc + si) % n_banks
            evac(oc, si, c0, wd, ps_pool[:, pb, 0:wd])
        evac_marks[oc] = milestone(k, evac_eng, evac_sem)
    return mm_marks





def emit_phase0(k, wgu_slab, wproj_slab, w2_slab, cc_w_in, WP_OFF, W2_OFF):
    nc = k.nc
    es0 = ExitStack()
    ld = es0.enter_context(nc.sbuf_tensor("p0_ld", [128, 2, FFH], BF16))[:]
    m1 = es0.enter_context(nc.sbuf_tensor("p0_m1", [128, FFH], BF16))[:]
    m2 = es0.enter_context(nc.sbuf_tensor("p0_m2", [128, FFH], BF16))[:]
    st_ = es0.enter_context(nc.sbuf_tensor("p0_st", [128, 64, 128], BF16))[:]
    qs = es0.enter_context(nc.sbuf_tensor("p0_qs", [128, 3, 128], F32))[:]
    sbf = es0.enter_context(nc.sbuf_tensor("p0_sbf", [128, 128], BF16))[:]
    pp = es0.enter_context(nc.sbuf_tensor("p0_pp", [128, 2, FFH], BF16))[:]
    pt = es0.enter_context(nc.psum_tensor("p0_pt", [128, 2, 1024], BF16))[:]
    in_cc = cc_w_in.ap()
    last_stage = [0]

    def quant(par, ncols):
        ng = ncols // 64
        lv = ld[:, par, 0:ncols].rearrange("p (g e) -> p g e", e=64)
        gb = lambda a: a.rearrange("p (g o) -> p g o", o=1).broadcast_to([128, ng, 64])
        ew(k, "vector", lambda e, lv=lv, ng=ng: e.reduce_sum(
            out=qs[:, 0, 0:ng].rearrange("p (g o) -> p g o", o=1), in_=lv,
            axis=AX.X, apply_absolute_value=True), r=[tg(ld, par)], w=[qs])
        ew(k, "vector", lambda e, ng=ng: e.tensor_scalar(
            out=qs[:, 0, 0:ng], in0=qs[:, 0, 0:ng], scalar1=1.0 / 64, scalar2=1e-8,
            op0=ALU.mult, op1=ALU.max), r=[qs], w=[qs])
        ew(k, "vector", lambda e, ng=ng: e.tensor_copy(
            out=sbf[:, 0:ng], in_=qs[:, 0, 0:ng]), r=[qs], w=[sbf])

        def _rcp(e, ng=ng, nc_=k.nc):
            with nc_.allow_low_precision(reason="group scale reciprocal"):
                e.reciprocal(out=qs[:, 1, 0:ng], in_=sbf[:, 0:ng])
        ew(k, "vector", _rcp, r=[sbf], w=[qs])
        m1v = m1[:, 0:ncols].rearrange("p (g e) -> p g e", e=64)
        m2v = m2[:, 0:ncols].rearrange("p (g e) -> p g e", e=64)
        ew(k, "vector", lambda e, lv=lv, m1v=m1v, ng=ng: e.tensor_tensor(
            out=m1v, in0=lv, in1=gb(qs[:, 1, 0:ng]), op=ALU.mult),
           r=[tg(ld, par), qs], w=[m1])
        ew(k, "vector", lambda e, ncols=ncols: e.tensor_single_scalar(
            out=m2[:, 0:ncols], in_=m1[:, 0:ncols], scalar=0.5, op=ALU.is_gt),
           r=[m1], w=[m2])
        ew(k, "vector", lambda e, ncols=ncols: e.tensor_single_scalar(
            out=m1[:, 0:ncols], in_=m1[:, 0:ncols], scalar=-0.5, op=ALU.is_lt),
           r=[m1], w=[m1])
        ew(k, "vector", lambda e, ncols=ncols: e.tensor_tensor(
            out=m1[:, 0:ncols], in0=m2[:, 0:ncols], in1=m1[:, 0:ncols], op=ALU.subtract),
           r=[m1, m2], w=[m1])
        ew(k, "vector", lambda e, m1v=m1v, ng=ng: e.tensor_tensor(
            out=m1v, in0=m1v, in1=gb(sbf[:, 0:ng]), op=ALU.mult),
           r=[m1, sbf], w=[m1])
        return milestone(k, "vector", k.s_v)

    def transpose_stage(ntiles, off, m_src):
        wait(k, "tensor", k.s_v, m_src)
        wait(k, "vector", k.s_g, last_stage[0])
        ev = {}
        for g0 in range(0, ntiles, 8):
            gcnt = min(8, ntiles - g0)
            bank = (g0 // 8) % 2
            if g0 >= 16:
                wait(k, "tensor", k.s_v, ev[g0 - 16])
            for t in range(gcnt):
                cc = g0 + t
                ew(k, "tensor", lambda e, cc=cc, bank=bank, t=t: e.transpose(
                    pt[:, bank, t * 128:(t + 1) * 128],
                    m1[:, cc * 128:(cc + 1) * 128], k.ident))
            mk = milestone(k, "tensor", k.s_pe)
            wait(k, "vector", k.s_pe, mk)
            ew(k, "vector", lambda e, g0=g0, bank=bank, gcnt=gcnt: e.tensor_copy(
                out=st_[:, g0:g0 + gcnt, :].rearrange("p a b -> p (a b)"),
                in_=pt[:, bank, 0:gcnt * 128]), w=[st_])
            ev[g0] = milestone(k, "vector", k.s_v)
        wait(k, "gpsimd", k.s_v, ev[(ntiles - 1) // 8 * 8])
        sz = ntiles * 128 * 128
        last_stage[0] = k.s_g.inc(16)
        ew(k, "gpsimd", lambda e, off=off, sz=sz, ntiles=ntiles, h=k.s_g.h: e.dma_start(
            out=in_cc[off:off + sz].rearrange("(p m) -> p m", p=128),
            in_=st_[:, 0:ntiles, :].rearrange("p a b -> p (a b)")).then_inc(h, 16))
        return milestone(k, "tensor", k.s_pe)

    qm = {}
    tm = {}
    for l in range(16):
        par = l % 2
        if l >= 2:
            wait(k, "gpsimd", k.s_v, qm[l - 2])
        sem = (k.s_w, k.s_w2)[par]
        lm = sem.inc(16)
        ew(k, "gpsimd", lambda e, l=l, par=par, h=sem.h: e.dma_start(
            out=ld[:, par, 0:2048], in_=wgu_slab[l * 128:(l + 1) * 128, :]).then_inc(h, 16))
        wait(k, "vector", sem, lm)
        if l >= 1:
            wait(k, "vector", k.s_pe, tm[l - 1])
        qm[l] = quant(par, 2048)
        tm[l] = transpose_stage(16, l * 262144, qm[l])

    for l in range(2):
        par = l % 2
        sem = (k.s_w, k.s_w2)[par]
        wait(k, "gpsimd", k.s_v, qm[14 + l] if l else qm[14])
        lm = sem.inc(16)
        ew(k, "gpsimd", lambda e, l=l, par=par, h=sem.h: e.dma_start(
            out=ld[:, par, :], in_=wproj_slab[l * 128:(l + 1) * 128, :]).then_inc(h, 16))
        wait(k, "vector", sem, lm)
        wait(k, "vector", k.s_pe, tm[15 + l] if l else tm[15])
        qm[16 + l] = quant(par, FFH)
        tm[16 + l] = transpose_stage(64, WP_OFF + l * 1048576, qm[16 + l])

    SCL = float(FFH) ** -0.5
    bm = {}
    for l in range(2):
        lm = k.s_w.inc(16)
        if l == 1:
            wait(k, "gpsimd", k.s_v, bm[0])
        ew(k, "gpsimd", lambda e, l=l, h=k.s_w.h: e.dma_start(
            out=pp[:, 0, :], in_=w2_slab[l * 128:(l + 1) * 128, :]).then_inc(h, 16))
        wait(k, "vector", k.s_w, lm)
        par = 0
        for s in range(13):
            h2 = 1 << s
            sv = pp[:, par, :].rearrange("p (nb two h) -> p nb two h", two=2, h=h2)
            dv = pp[:, 1 - par, :].rearrange("p (nb two h) -> p nb two h", two=2, h=h2)
            ew(k, "vector", lambda e, sv=sv, dv=dv: e.tensor_tensor(
                out=dv[:, :, 0, :], in0=sv[:, :, 0, :], in1=sv[:, :, 1, :], op=ALU.add),
               r=[tg(pp, par)], w=[tg(pp, 1 - par)])
            ew(k, "vector", lambda e, sv=sv, dv=dv: e.tensor_tensor(
                out=dv[:, :, 1, :], in0=sv[:, :, 0, :], in1=sv[:, :, 1, :], op=ALU.subtract),
               r=[tg(pp, par)], w=[tg(pp, 1 - par)])
            par = 1 - par
        wait(k, "vector", k.s_pe, tm[17] if l else tm[17])
        ew(k, "vector", lambda e, par=par: e.tensor_scalar_mul(
            out=m1[:, :], in0=pp[:, par, :], scalar1=SCL), r=[tg(pp, par)], w=[m1])
        bm[l] = milestone(k, "vector", k.s_v)
        tm[18 + l] = transpose_stage(64, W2_OFF + l * 1048576, bm[l])

    wait(k, "gpsimd", k.s_g)
    k.m_ccw = k.s_cc.inc(1)
    ew(k, "gpsimd", lambda e, h=k.s_cc.h: e.collective_compute(
        "AllGather", ALU.bypass, replica_groups=[list(range(8))],
        ins=[cc_w_in.ap().opt()], outs=[k.cc_w_out_ap.opt()]).then_inc(h, 1))
    es0.close()



def emit_ffn0(k, xres, xhat, bufA_t, bufB_t, bufC, bufC_t, wslab, ps_mm, ps_r,
              WP_OFF, RANK_SZ):
    occ = k.cc_w_out_ap

    def wgu_slab_ap(hc):
        r, l = divmod(hc, 16)
        off = r * RANK_SZ + l * 262144
        return occ[off:off + 262144].rearrange("(p m) -> p m", p=128)

    def wp_slab_ap(dc):
        r, l = divmod(dc, 2)
        off = r * RANK_SZ + WP_OFF + l * 1048576
        return occ[off:off + 1048576].rearrange("(p m) -> p m", p=128)

    hA = bufA_t[:, 0:16384].rearrange("p (a b) -> p a b", b=512)
    hB = bufB_t[:, 0:16384].rearrange("p (a b) -> p a b", b=512)

    def hbuf(fc):
        return (hA if fc < 32 else hB)[:, fc % 32, :]

    def htag(fc):
        return tg(bufA_t if fc < 32 else bufB_t, fc % 32)

    gslab = bufC_t[:, 0:8192].rearrange("p (a b c) -> p a b c", b=16, c=128)
    sqt = bufC_t[:, 8192:10240].rearrange("p (a b) -> p a b", a=4)
    sg = bufC_t[:, 10240:11264].rearrange("p (a b) -> p a b", a=2)
    tstat = bufC_t[:, 11264:12288].bitcast(F32)
    rsb = bufC_t[:, 12288:13312].rearrange("p (a b) -> p a b", a=2)
    pslab = bufC_t[:, 0:16384].rearrange("p (a b c) -> p a b c", b=64, c=128)
    ksl = (k.s_k, k.s_k2, k.s_k3, k.s_k4)

    wait(k, "scalar", k.s_v)
    wait(k, "vector", k.s_pe)
    emit_ln(k, xres, xhat, ((1, 512), (513, 512)), bufC)

    for th in range(2):
        cx = 1 + th * 512
        wait(k, "sync", k.s_pe)
        wait(k, "vector", k.s_pe)
        wait(k, "scalar", k.s_pe)
        wait(k, "tensor", k.s_v)
        wait(k, "tensor", k.s_sc)
        if th == 0:
            wait(k, "sync", k.s_cc, k.m_ccw)
        load_marks = {}

        def gu_load(fc, which):
            slot = (2 * fc + which) % 4
            sem = ksl[slot]
            lm = sem.inc(16)
            ew(k, "sync", lambda e, fc=fc, which=which, slot=slot, h=sem.h: e.dma_start(
                out=gslab[:, slot, :, :].rearrange("p a b -> p (a b)"),
                in_=wgu_slab_ap(fc + 64 * which)).then_inc(h, 16))
            load_marks[(fc, which)] = lm

        gu_load(0, 0)
        gu_load(0, 1)
        gu_load(1, 0)
        gu_load(1, 1)
        gmark = {}
        umark = {}
        hmark = {}
        smark = {}
        sqmark = {}
        ssmark = {}
        for fc in range(64):
            slot_g = (2 * fc) % 4
            slot_u = (2 * fc + 1) % 4
            wait(k, "tensor", ksl[slot_g], load_marks[(fc, 0)])
            if fc >= 2:
                wait(k, "tensor", k.s_sc, smark[fc - 2])
            for dc in range(NCD):
                ew(k, "tensor", lambda e, dc=dc, slot_g=slot_g, cx=cx: e.matmul(
                    ps_mm[:, slot_g % 4, :], gslab[:, slot_g, dc, :],
                    xhat[:, dc, cx:cx + 512], start=(dc == 0), stop=(dc == NCD - 1)))
            gmark[fc] = milestone(k, "tensor", k.s_pe)
            wait(k, "tensor", ksl[slot_u], load_marks[(fc, 1)])
            if fc >= 2:
                wait(k, "tensor", k.s_v, hmark[fc - 2])
            for dc in range(NCD):
                ew(k, "tensor", lambda e, dc=dc, slot_u=slot_u, cx=cx: e.matmul(
                    ps_mm[:, slot_u % 4, :], gslab[:, slot_u, dc, :],
                    xhat[:, dc, cx:cx + 512], start=(dc == 0), stop=(dc == NCD - 1)))
            umark[fc] = milestone(k, "tensor", k.s_pe)
            if fc >= 2:
                f2 = fc - 2
                wait(k, "tensor", k.s_sc, sqmark[f2])
                ew(k, "tensor", lambda e, f2=f2: e.matmul(
                    ps_r[:, 0, :], k.ones, sqt[:, f2 % 4, :],
                    start=(f2 == 0), stop=False))
                ssmark[f2] = milestone(k, "tensor", k.s_pe)
            if fc + 2 < 64:
                wait(k, "sync", k.s_pe, gmark[fc])
                gu_load(fc + 2, 0)
                wait(k, "sync", k.s_pe, umark[fc])
                gu_load(fc + 2, 1)
            wait(k, "scalar", k.s_pe, gmark[fc])
            if fc >= 2:
                wait(k, "scalar", k.s_v, hmark[fc - 2])
            ew(k, "scalar", lambda e, fc=fc, slot_g=slot_g: e.activation(
                out=sg[:, fc % 2, :], in_=ps_mm[:, slot_g % 4, :], func=AF.Silu),
               w=[f"bufC:sg{fc % 2}"])
            smark[fc] = milestone(k, "scalar", k.s_sc)
            wait(k, "vector", k.s_pe, umark[fc])
            wait(k, "vector", k.s_sc, smark[fc])
            ew(k, "vector", lambda e, fc=fc, slot_u=slot_u: e.tensor_tensor(
                out=hbuf(fc), in0=sg[:, fc % 2, :], in1=ps_mm[:, slot_u % 4, :],
                op=ALU.mult), r=[f"bufC:sg{fc % 2}"], w=[htag(fc)])
            hmark[fc] = milestone(k, "vector", k.s_v)
            if fc >= 4:
                wait(k, "scalar", k.s_pe, ssmark[fc - 4])
            wait(k, "scalar", k.s_v, hmark[fc])
            ew(k, "scalar", lambda e, fc=fc: e.activation(
                out=sqt[:, fc % 4, :], in_=hbuf(fc), func=AF.Square),
               w=[f"bufC:sq{fc % 4}"])
            sqmark[fc] = milestone(k, "scalar", k.s_sc)
        for f2 in (62, 63):
            wait(k, "tensor", k.s_sc, sqmark[f2])
            ew(k, "tensor", lambda e, f2=f2: e.matmul(
                ps_r[:, 0, :], k.ones, sqt[:, f2 % 4, :], start=False, stop=(f2 == 63)))
        mss = milestone(k, "tensor", k.s_pe)
        wait(k, "vector", k.s_pe, mss)
        wait(k, "vector", k.s_sc)
        ew(k, "vector", lambda e: e.tensor_scalar(
            out=tstat[:, 0:512], in0=ps_r[:, 0, :], scalar1=1.0 / FFH, scalar2=RMS_EPS,
            op0=ALU.mult, op1=ALU.add), w=["bufC:ts"])
        mvst = milestone(k, "vector", k.s_v)
        wait(k, "scalar", k.s_v, mvst)
        ew(k, "scalar", lambda e: e.activation(
            out=tstat[:, 0:512], in_=tstat[:, 0:512], func=AF.Sqrt),
           r=["bufC:ts"], w=["bufC:ts"])
        msqr = milestone(k, "scalar", k.s_sc)
        wait(k, "vector", k.s_sc, msqr)

        def _recipr(e, th=th, nc_=k.nc):
            with nc_.allow_low_precision(reason="rms rstd tile stored bf16"):
                return e.reciprocal(out=rsb[:, th, :], in_=tstat[:, 0:512])
        ew(k, "vector", _recipr, r=["bufC:ts"], w=["bufC:rs"])
        for fc in range(64):
            ew(k, "vector", lambda e, fc=fc, th=th: e.tensor_tensor(
                out=hbuf(fc), in0=hbuf(fc), in1=rsb[:, th, :], op=ALU.mult),
               r=[htag(fc), "bufC:rs"], w=[htag(fc)])
        mhn = milestone(k, "vector", k.s_v)
        wait(k, "tensor", k.s_v, mhn)

        def evac_ffn(oc, si, c0_, wd, ps, th=th, cx=cx):
            ew(k, "vector", lambda e, oc=oc, ps=ps, cx=cx: e.tensor_tensor(
                out=xres[:, oc, cx:cx + 512], in0=xres[:, oc, cx:cx + 512],
                in1=ps, op=ALU.add),
               r=[tg(xres, f"f{oc}.{th}")], w=[tg(xres, f"f{oc}.{th}")])

        emit_proj(k, 16, 64, wp_slab_ap, lambda dc, c0_, wd: hbuf(dc),
                  evac_ffn, "vector", pslab, ps_mm, 4, segs=((0, 512),),
                  w_queue="sync", w_sems=(k.s_k, k.s_k2), load_wait=(k.s_v, mhn))





def emit_l1attn(k, xres, xhat, bufA, bufB, bufC, bufC_t, wslab, ps_mm,
                wq_t, wk_t, wv_t, wo_t, ropec_d, band_d, killv, cc_kv_in, cc_kv_out):
    nc = k.nc
    ropec = bufC_t[:, 14016:16064].rearrange("p (a b) -> p a b", a=2)
    band = bufC_t[:, 12864:14016]
    KHD = 131072
    kin = cc_kv_in.ap()
    kout = cc_kv_out.ap()

    def kv_in_K(h):
        return kin[h * KHD:(h + 1) * KHD].rearrange("(p m) -> p m", p=128)

    def kv_in_V(h):
        off = 16 * KHD + h * KHD
        return kin[off:off + KHD].rearrange("(p m) -> p m", p=128)

    def kv_out_K(h):
        return kout[h * KHD:(h + 1) * KHD].rearrange("(p m) -> p m", p=128)

    def kv_out_V(h):
        off = 16 * KHD + h * KHD
        return kout[off:off + KHD].rearrange("(p m) -> p m", p=128)

    ps_s = ps_mm[:, 0:3, :].rearrange("p a b -> p (a b)")
    ps_pt = ps_mm[:, 3, :].bitcast(BF16)
    ps_pt2 = k.ps_ln[:, 0, 0:64].bitcast(BF16)
    ps_ot = [k.ps_ln[:, 1, 0:128], k.ps_r[:, 0, 0:128]]

    wait(k, "scalar", k.s_v)
    wait(k, "scalar", k.s_pe)
    wait(k, "vector", k.s_pe)
    emit_ln(k, xres, xhat, ((1, 512), (513, 512)), bufC)

    def proj_rhs_x(dc, c0, wd):
        return xhat[:, dc, 1 + c0:1 + c0 + wd]

    def evac_copy_to(buf):
        def f(oc, si, c0, wd, ps):
            ew(k, "vector", lambda e, oc=oc, c0=c0, wd=wd, ps=ps: e.tensor_copy(
                out=buf[:, oc, c0:c0 + wd], in_=ps), w=[tg(buf, f"a{oc}")])
        return f

    def emit_rope(buf, h):
        wait(k, "tensor", k.s_v)
        for s2 in range(2):
            ew(k, "tensor", lambda e, h=h, s2=s2, buf=buf: e.matmul(
                ps_mm[:, 2 + s2, :], k.swap, buf[:, h, s2 * 512:(s2 + 1) * 512],
                start=True, stop=True))
        mpe = milestone(k, "tensor", k.s_pe)
        ew(k, "vector", lambda e, h=h, buf=buf: e.tensor_tensor(
            out=bufC[:, 0, 0:T], in0=buf[:, h, 0:T], in1=ropec[:, 0, :], op=ALU.mult),
           r=[tg(buf, f"a{h}")], w=[tg(bufC_t, "rt0")])
        wait(k, "vector", k.s_pe, mpe)
        for s2 in range(2):
            ew(k, "vector", lambda e, s2=s2: e.tensor_tensor(
                out=bufC[:, 1, s2 * 512:(s2 + 1) * 512], in0=ps_mm[:, 2 + s2, :],
                in1=ropec[:, 1, s2 * 512:(s2 + 1) * 512], op=ALU.mult),
               w=[tg(bufC_t, "rt1")])
        ew(k, "vector", lambda e, h=h, buf=buf: e.tensor_tensor(
            out=buf[:, h, 0:T], in0=bufC[:, 0, 0:T], in1=bufC[:, 1, 0:T], op=ALU.add),
           r=[tg(bufC_t, "rt0"), tg(bufC_t, "rt1")], w=[tg(buf, f"a{h}")])
        return milestone(k, "vector", k.s_v)

    wait(k, "sync", k.s_pe)
    m_rc1 = k.s_ld.inc(16)
    ew(k, "sync", lambda e, h=k.s_ld.h: e.dma_start(
        out=ropec, in_=ropec_d).then_inc(h, 16))
    wait(k, "vector", k.s_ld, m_rc1)
    wait(k, "tensor", k.s_v)
    emit_proj(k, 16, 16, lambda oc: wk_t[oc], proj_rhs_x,
              evac_copy_to(bufB), "vector", wslab, ps_mm, 2)
    for h in range(NH):
        mr = emit_rope(bufB, h)
        k.s_g.inc(16)
        wait(k, "gpsimd", k.s_v, mr)
        ew(k, "gpsimd", lambda e, h=h, hh=k.s_g.h: e.dma_start(
            out=kv_in_K(h), in_=bufB[:, h, 0:T]).then_inc(hh, 16))

    wait(k, "vector", k.s_pe)
    emit_proj(k, 16, 16, lambda oc: wv_t[oc], proj_rhs_x,
              evac_copy_to(bufC), "vector", wslab, ps_mm, 2)
    ps_vt = ps_mm[:, 2:4, :].rearrange("p a b -> p (a b)").bitcast(BF16)
    wait(k, "tensor", k.s_v)
    mv_prev = None
    for h in range(NH):
        if mv_prev is not None:
            wait(k, "tensor", k.s_v, mv_prev)
        for t8 in range(8):
            ew(k, "tensor", lambda e, h=h, t8=t8: e.transpose(
                ps_vt[:, t8 * 128:(t8 + 1) * 128],
                bufC[:, h, t8 * 128:(t8 + 1) * 128], k.ident))
        mpe = milestone(k, "tensor", k.s_pe)
        wait(k, "vector", k.s_pe, mpe)
        if h > 0:
            wait(k, "vector", k.s_g)
        ew(k, "vector", lambda e, h=h: e.tensor_copy(
            out=bufC[:, h, 0:T], in_=ps_vt[:, 0:1024]), w=[tg(bufC_t, f"v{h}")])
        mv = milestone(k, "vector", k.s_v)
        mv_prev = mv
        k.s_g.inc(16)
        wait(k, "gpsimd", k.s_v, mv)
        ew(k, "gpsimd", lambda e, h=h, hh=k.s_g.h: e.dma_start(
            out=kv_in_V(h), in_=bufC[:, h, 0:T]).then_inc(hh, 16))

    wait(k, "gpsimd", k.s_g)
    m_kv = k.s_cc.inc(1)
    ew(k, "gpsimd", lambda e, hh=k.s_cc.h: e.collective_compute(
        "AllGather", ALU.bypass, replica_groups=PAIRS,
        ins=[cc_kv_in.ap().opt()], outs=[cc_kv_out.ap().opt()]).then_inc(hh, 1))

    emit_proj(k, 16, 16, lambda oc: wq_t[oc], proj_rhs_x,
              evac_copy_to(bufA), "vector", wslab, ps_mm, 2)
    wait(k, "sync", k.s_g)
    m_rc2 = k.s_ld.inc(32)
    ew(k, "sync", lambda e, h=k.s_ld.h: e.dma_start(
        out=ropec, in_=ropec_d).then_inc(h, 16))
    ew(k, "sync", lambda e, h=k.s_ld.h: e.dma_start(
        out=band[:, :], in_=band_d).then_inc(h, 16))
    wait(k, "vector", k.s_ld, m_rc2)
    wait(k, "vector", k.s_g)
    rope_q_marks = {}
    for h in range(NH):
        rope_q_marks[h] = emit_rope(bufA, h)

    kbuf = bufC_t[:, 0:4096].rearrange("p (s m) -> p s m", s=2)
    vbuf = bufC_t[:, 4096:8192].rearrange("p (s a c) -> p s a c", s=2, a=16)
    Pt = bufC_t[:, 8192:10496].rearrange("p (s m) -> p s m", s=2)
    ptile = bufC_t[:, 10496:12800].rearrange("p (s m) -> p s m", s=2)
    smal = bufC_t[:, 12800:12864].bitcast(F32)

    wait(k, "vector", k.s_pe)
    wait(k, "sync", k.s_cc, m_kv)
    wait(k, "sync", k.s_v)
    wait(k, "sync", k.s_g)

    NIT = NH * 8
    kvsem = (k.s_k3, k.s_k4)
    kvload = {}

    def kv_load(h):
        sl = h % 2
        sem = kvsem[sl]
        ew(k, "sync", lambda e, h=h, sl=sl, hh=sem.h: e.dma_start(
            out=kbuf[:, sl, 0:1024], in_=kv_out_K(h)).then_inc(hh, 16))
        ew(k, "sync", lambda e, h=h, sl=sl, hh=sem.h: e.dma_start(
            out=kbuf[:, sl, 1024:2048], in_=bufB[:, h, 0:T]).then_inc(hh, 16))
        ew(k, "sync", lambda e, h=h, sl=sl, hh=sem.h: e.dma_start(
            out=vbuf[:, sl, 0:8, :].rearrange("p a c -> p (a c)"),
            in_=kv_out_V(h)).then_inc(hh, 16))
        ew(k, "sync", lambda e, h=h, sl=sl, hh=sem.h: e.dma_start(
            out=vbuf[:, sl, 8:16, :].rearrange("p a c -> p (a c)"),
            in_=kv_in_V(h)).then_inc(hh, 16))
        kvload[h] = sem.inc(64)

    kv_load(0)
    kv_load(1)

    ps_marks = {}
    mx_marks = {}
    exp_marks = {}
    pn_marks = {}
    pt_marks = {}
    pte_marks = {}
    av_marks = {}
    oe_marks = {}

    for i in range(NIT + 2):
        h, qt = divmod(i, 8)
        if i < NIT:
            sl = h % 2
            if qt == 0:
                wait(k, "tensor", kvsem[sl], kvload[h])
                wait(k, "tensor", k.s_v, rope_q_marks[h])
            if i >= 1:
                wait(k, "tensor", k.s_sc, exp_marks[i - 1])
            ws = qt * 128
            ew(k, "tensor", lambda e, h=h, qt=qt, sl=sl, ws=ws: e.matmul(
                ps_s[:, 0:512], bufA[:, h, qt * 128:(qt + 1) * 128],
                kbuf[:, sl, ws:ws + 512], start=True, stop=False))
            ew(k, "tensor", lambda e, h=h, qt=qt, sl=sl, ws=ws: e.matmul(
                ps_s[:, 512:1024], bufA[:, h, qt * 128:(qt + 1) * 128],
                kbuf[:, sl, ws + 512:ws + 1024], start=True, stop=(qt >= 4)))
            ew(k, "tensor", lambda e, h=h, qt=qt, sl=sl, ws=ws: e.matmul(
                ps_s[:, 1024:1152], bufA[:, h, qt * 128:(qt + 1) * 128],
                kbuf[:, sl, ws + 1024:ws + 1152], start=True, stop=False))
            ew(k, "tensor", lambda e: e.matmul(
                ps_s[:, 0:512], k.ident, band[:, 0:512], start=False, stop=False))
            ew(k, "tensor", lambda e: e.matmul(
                ps_s[:, 1024:1152], k.ident, band[:, 1024:1152], start=False, stop=True))
            ew(k, "tensor", lambda e, ws=ws: e.matmul(
                ps_s[:, 0:512], k.ones[0:1, :], killv[0:1, ws:ws + 512],
                start=False, stop=True))
            if qt < 4:
                ew(k, "tensor", lambda e, ws=ws: e.matmul(
                    ps_s[:, 512:1024], k.ones[0:1, :], killv[0:1, ws + 512:ws + 1024],
                    start=False, stop=True))
            ps_marks[i] = milestone(k, "tensor", k.s_pe)
            if qt == 2 and 2 <= h + 1 < NH:
                wait(k, "sync", k.s_pe, av_marks[(h - 1) * 8 + 7])
                kv_load(h + 1)
        j = i - 1
        if 0 <= j < NIT:
            wait(k, "tensor", k.s_sc, pn_marks[j])
            if j >= 1:
                wait(k, "tensor", k.s_v, pte_marks[j - 1])
            for t9 in range(8):
                ew(k, "tensor", lambda e, j=j, t9=t9: e.transpose(
                    ps_pt[:, t9 * 128:(t9 + 1) * 128],
                    Pt[:, j % 2, t9 * 128:(t9 + 1) * 128], k.ident))
            ew(k, "tensor", lambda e, j=j: e.transpose(
                ps_pt2[:, 0:128], Pt[:, j % 2, 1024:1152], k.ident))
            pt_marks[j] = milestone(k, "tensor", k.s_pe)
        m = i - 2
        if m >= 0:
            hm, qtm = divmod(m, 8)
            slm = hm % 2
            wait(k, "tensor", k.s_v, pte_marks[m])
            if m >= 2:
                wait(k, "tensor", k.s_v, oe_marks[m - 2])
            for t9 in range(9):
                ew(k, "tensor", lambda e, m=m, t9=t9, slm=slm, qtm=qtm: e.matmul(
                    ps_ot[m % 2], vbuf[:, slm, qtm + t9, :],
                    ptile[:, m % 2, t9 * 128:(t9 + 1) * 128],
                    start=(t9 == 0), stop=(t9 == 8)))
            av_marks[m] = milestone(k, "tensor", k.s_pe)
        if i < NIT:
            c0s = (i % 2) * 8
            wait(k, "vector", k.s_pe, ps_marks[i])
            ew(k, "vector", lambda e, c0s=c0s: e.reduce_max(
                out=smal[:, c0s:c0s + 1], in_=ps_s[:, 0:1152], axis=AX.X),
               w=[tg(bufC_t, f"sm{i % 2}")])
            ew(k, "vector", lambda e, c0s=c0s: e.tensor_scalar_mul(
                out=smal[:, c0s + 1:c0s + 2], in0=smal[:, c0s:c0s + 1],
                scalar1=-ATT_SCALE),
               r=[tg(bufC_t, f"sm{i % 2}")], w=[tg(bufC_t, f"sm{i % 2}")])
            mx_marks[i] = milestone(k, "vector", k.s_v)
            wait(k, "scalar", k.s_v, mx_marks[i])
            ew(k, "scalar", lambda e, i=i, c0s=c0s: e.activation(
                out=Pt[:, i % 2, :], in_=ps_s[:, 0:1152], func=AF.Exp,
                scale=ATT_SCALE, bias=smal[:, c0s + 1:c0s + 2],
                accum_out=smal[:, c0s + 2:c0s + 3]),
               r=[tg(bufC_t, f"sm{i % 2}")],
               w=[tg(bufC_t, f"p{i % 2}"), tg(bufC_t, f"sm{i % 2}")])
            exp_marks[i] = milestone(k, "scalar", k.s_sc)
            wait(k, "vector", k.s_sc, exp_marks[i])

            def _rrin(e, c0s=c0s, nc_=k.nc):
                with nc_.allow_low_precision(reason="softmax denom reciprocal"):
                    return e.reciprocal(out=smal[:, c0s + 3:c0s + 4],
                                        in_=smal[:, c0s + 2:c0s + 3])
            ew(k, "vector", _rrin, r=[tg(bufC_t, f"sm{i % 2}")],
               w=[tg(bufC_t, f"sm{i % 2}")])
            mrin = milestone(k, "vector", k.s_v)
            wait(k, "scalar", k.s_v, mrin)
            ew(k, "scalar", lambda e, i=i, c0s=c0s: e.activation(
                out=Pt[:, i % 2, :], in_=Pt[:, i % 2, :], func=AF.Copy,
                scale=smal[:, c0s + 3:c0s + 4]),
               r=[tg(bufC_t, f"p{i % 2}"), tg(bufC_t, f"sm{i % 2}")],
               w=[tg(bufC_t, f"p{i % 2}")])
            pn_marks[i] = milestone(k, "scalar", k.s_sc)
        if 0 <= j < NIT:
            wait(k, "vector", k.s_pe, pt_marks[j])
            ew(k, "vector", lambda e, j=j: e.tensor_copy(
                out=ptile[:, j % 2, 0:1024], in_=ps_pt[:, :]),
               w=[tg(bufC_t, f"pt{j % 2}")])
            ew(k, "vector", lambda e, j=j: e.tensor_copy(
                out=ptile[:, j % 2, 1024:1152], in_=ps_pt2[:, 0:128]),
               w=[tg(bufC_t, f"pt{j % 2}")])
            pte_marks[j] = milestone(k, "vector", k.s_v)
        if m >= 0:
            hm, qtm = divmod(m, 8)
            wait(k, "vector", k.s_pe, av_marks[m])
            ew(k, "vector", lambda e, m=m, hm=hm, qtm=qtm: e.tensor_copy(
                out=xhat[:, hm, qtm * 128:(qtm + 1) * 128], in_=ps_ot[m % 2]),
               w=[tg(xhat, f"at{hm}.{qtm}")])
            oe_marks[m] = milestone(k, "vector", k.s_v)

    wait(k, "tensor", k.s_v)

    def evac_addres1(oc, si, c0, wd, ps):
        ew(k, "vector", lambda e, oc=oc, c0=c0, wd=wd, ps=ps: e.tensor_tensor(
            out=xres[:, oc, 1 + c0:1 + c0 + wd], in0=xres[:, oc, 1 + c0:1 + c0 + wd],
            in1=ps, op=ALU.add), r=[tg(xres, f"o{oc}.{si}")], w=[tg(xres, f"o{oc}.{si}")])

    emit_proj(k, 16, 16, lambda oc: wo_t[oc],
              lambda dc, c0, wd: xhat[:, dc, c0:c0 + wd],
              evac_addres1, "vector", wslab, ps_mm, 2)




def emit_ffn1(k, xres, xhat, bufA_t, bufB_t, bufC, bufC_t, ps_mm,
              w1_t, w3_t, W2_OFF, RANK_SZ):
    occ = k.cc_w_out_ap

    def w2h_slab_ap(dc):
        r, l = divmod(dc, 2)
        off = r * RANK_SZ + W2_OFF + l * 1048576
        return occ[off:off + 1048576].rearrange("(p m) -> p m", p=128)

    hA = bufA_t[:, 0:16384].rearrange("p (a b) -> p a b", b=512)
    hB = bufB_t[:, 0:16384].rearrange("p (a b) -> p a b", b=512)

    def hbuf(fc):
        return (hA if fc < 32 else hB)[:, fc % 32, :]

    def htag(fc):
        return tg(bufA_t if fc < 32 else bufB_t, fc % 32)

    gslab = bufC_t[:, 0:8192].rearrange("p (a b c) -> p a b c", a=4, c=128)
    sg = bufC_t[:, 10240:11264].rearrange("p (a b) -> p a b", a=2)
    pslab = bufC_t[:, 0:16384].rearrange("p (a b c) -> p a b c", b=64, c=128)
    wsl = (k.s_w, k.s_w2, k.s_w3, k.s_w4)

    wait(k, "scalar", k.s_v)
    wait(k, "scalar", k.s_pe)
    wait(k, "vector", k.s_pe)
    emit_ln(k, xres, xhat, ((1, 512), (513, 512)), bufC)

    for th in range(2):
        cx = 1 + th * 512
        wait(k, "gpsimd", k.s_pe)
        wait(k, "vector", k.s_pe)
        wait(k, "scalar", k.s_pe)
        wait(k, "tensor", k.s_v)
        wait(k, "tensor", k.s_sc)
        load_marks = {}

        def w_load(fc, which):
            slot = (2 * fc + which) % 4
            sem = wsl[slot]
            lm = sem.inc(16)
            src_ = (w1_t if which == 0 else w3_t)
            ew(k, "gpsimd", lambda e, fc=fc, src_=src_, slot=slot, h=sem.h: e.dma_start(
                out=gslab[:, slot, :, :].rearrange("p a b -> p (a b)"),
                in_=src_[fc]).then_inc(h, 16))
            load_marks[(fc, which)] = lm

        w_load(0, 0)
        w_load(0, 1)
        w_load(1, 0)
        w_load(1, 1)
        gmark = {}
        umark = {}
        hmark = {}
        smark = {}
        for fc in range(64):
            slot_g = (2 * fc) % 4
            slot_u = (2 * fc + 1) % 4
            wait(k, "tensor", wsl[slot_g], load_marks[(fc, 0)])
            if fc >= 2:
                wait(k, "tensor", k.s_sc, smark[fc - 2])
            for dc in range(NCD):
                ew(k, "tensor", lambda e, dc=dc, slot_g=slot_g, cx=cx: e.matmul(
                    ps_mm[:, slot_g, :], gslab[:, slot_g, dc, :],
                    xhat[:, dc, cx:cx + 512], start=(dc == 0), stop=(dc == NCD - 1)))
            gmark[fc] = milestone(k, "tensor", k.s_pe)
            wait(k, "tensor", wsl[slot_u], load_marks[(fc, 1)])
            if fc >= 2:
                wait(k, "tensor", k.s_v, hmark[fc - 2])
            for dc in range(NCD):
                ew(k, "tensor", lambda e, dc=dc, slot_u=slot_u, cx=cx: e.matmul(
                    ps_mm[:, slot_u, :], gslab[:, slot_u, dc, :],
                    xhat[:, dc, cx:cx + 512], start=(dc == 0), stop=(dc == NCD - 1)))
            umark[fc] = milestone(k, "tensor", k.s_pe)
            if fc + 2 < 64:
                wait(k, "gpsimd", k.s_pe, gmark[fc])
                w_load(fc + 2, 0)
                wait(k, "gpsimd", k.s_pe, umark[fc])
                w_load(fc + 2, 1)
            wait(k, "scalar", k.s_pe, gmark[fc])
            if fc >= 2:
                wait(k, "scalar", k.s_v, hmark[fc - 2])
            ew(k, "scalar", lambda e, fc=fc, slot_g=slot_g: e.activation(
                out=sg[:, fc % 2, :], in_=ps_mm[:, slot_g, :], func=AF.Silu),
               w=[f"bufC:sg{fc % 2}"])
            smark[fc] = milestone(k, "scalar", k.s_sc)
            wait(k, "vector", k.s_pe, umark[fc])
            wait(k, "vector", k.s_sc, smark[fc])
            ew(k, "vector", lambda e, fc=fc, slot_u=slot_u: e.tensor_tensor(
                out=hbuf(fc), in0=sg[:, fc % 2, :], in1=ps_mm[:, slot_u, :],
                op=ALU.mult), r=[f"bufC:sg{fc % 2}"], w=[htag(fc)])
            hmark[fc] = milestone(k, "vector", k.s_v)
        mhn = milestone(k, "vector", k.s_v)
        wait(k, "tensor", k.s_v, mhn)

        def evac_ffn1(oc, si, c0_, wd, ps, th=th, cx=cx):
            ew(k, "vector", lambda e, oc=oc, ps=ps, cx=cx: e.tensor_tensor(
                out=xres[:, oc, cx:cx + 512], in0=xres[:, oc, cx:cx + 512],
                in1=ps, op=ALU.add),
               r=[tg(xres, f"g{oc}.{th}")], w=[tg(xres, f"g{oc}.{th}")])

        emit_proj(k, 16, 64, w2h_slab_ap, lambda dc, c0_, wd: hbuf(dc),
                  evac_ffn1, "vector", pslab, ps_mm, 4, segs=((0, 512),),
                  w_queue="sync", w_sems=(k.s_k, k.s_k2), load_wait=(k.s_v, mhn))



def build(debug_stop=None):
    nc = bass.Bass("TRN2", target_bir_lowering=False)
    es = ExitStack()

    xT = nc.dram_tensor("xT", [NCD, 128, W], F32, kind="ExternalInput").ap()
    wr_t = nc.dram_tensor("wr_t", [16, 128, D], F32, kind="ExternalInput").ap()
    wk_t = nc.dram_tensor("wk_t", [16, 128, D], F32, kind="ExternalInput").ap()
    wv_t = nc.dram_tensor("wv_t", [16, 128, D], F32, kind="ExternalInput").ap()
    wo_t = nc.dram_tensor("wo_t", [16, 128, D], F32, kind="ExternalInput").ap()
    vecs = nc.dram_tensor("vecs", [128, 6, 16], F32, kind="ExternalInput").ap()
    cid = nc.dram_tensor("cid", [128, 3, 128], BF16, kind="ExternalInput").ap()
    outT = nc.dram_tensor("outT", [NCD, 128, T], F32, kind="ExternalOutput").ap()

    l1wq_t = nc.dram_tensor("l1wq_t", [16, 128, D], F32, kind="ExternalInput").ap()
    l1wk_t = nc.dram_tensor("l1wk_t", [16, 128, D], F32, kind="ExternalInput").ap()
    l1wv_t = nc.dram_tensor("l1wv_t", [16, 128, D], F32, kind="ExternalInput").ap()
    l1wo_t = nc.dram_tensor("l1wo_t", [16, 128, D], F32, kind="ExternalInput").ap()
    w1_t = nc.dram_tensor("w1_t", [64, 128, D], F32, kind="ExternalInput").ap()
    w3_t = nc.dram_tensor("w3_t", [64, 128, D], F32, kind="ExternalInput").ap()
    ropec_d = nc.dram_tensor("ropec_d", [128, 2, 1024], BF16, kind="ExternalInput").ap()
    band_d = nc.dram_tensor("band_d", [128, 1152], BF16, kind="ExternalInput").ap()
    killv_d = nc.dram_tensor("killv_d", [1, 2048], BF16, kind="ExternalInput").ap()
    wgu_slab = nc.dram_tensor("wgu_slab", [2048, 2048], F32, kind="ExternalInput").ap()
    wproj_slab = nc.dram_tensor("wproj_slab", [256, FFH], F32, kind="ExternalInput").ap()
    w2_slab = nc.dram_tensor("w2_slab", [256, FFH], F32, kind="ExternalInput").ap()

    cc_s_in = nc.dram_tensor("cc_s_in", [128, 16], F32)
    cc_s_out = nc.dram_tensor("cc_s_out", [256, 16], F32)
    RANK_SZ = 16 * 262144 + 4 * 1048576
    WP_OFF = 16 * 262144
    W2_OFF = WP_OFF + 2 * 1048576
    cc_w_in = nc.dram_tensor("cc_w_in", [RANK_SZ], BF16)
    cc_w_out = nc.dram_tensor("cc_w_out", [8 * RANK_SZ], BF16)
    cc_kv_in = nc.dram_tensor("cc_kv_in", [32 * 131072], BF16)
    cc_kv_out = nc.dram_tensor("cc_kv_out", [64 * 131072], BF16)
    k_rank_sz = RANK_SZ

    k = K()
    k.nc = nc
    k.em = Em()

    for s in ("s_ld", "s_w", "s_w2", "s_w3", "s_w4", "s_k", "s_k2", "s_k3", "s_k4", "s_pe", "s_v", "s_sc", "s_g", "s_cc", "s_st"):
        setattr(k, s, Sem(es.enter_context(nc.semaphore(s))))

    onesid = es.enter_context(nc.sbuf_tensor("onesid", [128, 3, 128], BF16))[:]
    k.ones = onesid[:, 0, :]
    k.ident = onesid[:, 1, :]
    k.swap = onesid[:, 2, :]
    killv = es.enter_context(nc.sbuf_tensor("killv", [1, 2048], BF16))[:]
    vtile = es.enter_context(nc.sbuf_tensor("vtile", [128, 10, 16], F32))[:]
    stile = es.enter_context(nc.sbuf_tensor("stile", [128, 3, 16], F32))[:]

    k.s_ld.inc(16)
    ew(k, "sync", lambda e, h=k.s_ld.h: e.dma_start(out=onesid, in_=cid).then_inc(h, 16))
    k.s_ld.inc(16)
    ew(k, "sync", lambda e, h=k.s_ld.h: e.dma_start(
        out=vtile[:, 0:6, :], in_=vecs).then_inc(h, 16))
    k.s_ld.inc(16)
    ew(k, "sync", lambda e, h=k.s_ld.h: e.dma_start(out=killv[:], in_=killv_d).then_inc(h, 16))
    m_in = k.s_ld.n
    for e in ("scalar", "vector", "tensor"):
        wait(k, e, k.s_ld, m_in)

    for i in range(3):
        ew(k, "vector", lambda e, i=i: e.tensor_scalar(
            out=vtile[:, 6 + i, :], in0=vtile[:, i, :], scalar1=-1.0, scalar2=1.0,
            op0=ALU.mult, op1=ALU.add), r=[vtile], w=[vtile])
    ew(k, "scalar", lambda e: e.activation(out=vtile[:, 9, :], in_=vtile[:, 3, :],
                                           func=AF.Exp), r=[vtile], w=[vtile])
    ew(k, "scalar", lambda e: e.activation(out=vtile[:, 9, :], in_=vtile[:, 9, :],
                                           func=AF.Exp, scale=-1.0), r=[vtile], w=[vtile])
    m_vec_s = milestone(k, "scalar", k.s_sc)
    milestone(k, "vector", k.s_v)

    k.cc_w_out_ap = cc_w_out.ap()
    emit_phase0(k, wgu_slab, wproj_slab, w2_slab, cc_w_in, WP_OFF, W2_OFF)
    m_p0 = k.s_g.n

    xres = es.enter_context(nc.sbuf_tensor("xres", [128, NCD, W], F32))[:]
    xhat = es.enter_context(nc.sbuf_tensor("xhat", [128, NCD, W], BF16))[:]
    bufA_t = es.enter_context(nc.sbuf_tensor("bufA", [128, NCD * W], BF16))[:]
    bufA = bufA_t.rearrange("p (a b) -> p a b", b=W)
    bufB_t = es.enter_context(nc.sbuf_tensor("bufB", [128, NCD * W], BF16))[:]
    bufB = bufB_t.rearrange("p (a b) -> p a b", b=W)
    bufC_t = es.enter_context(nc.sbuf_tensor("bufC", [128, NCD * W], BF16))[:]
    bufC = bufC_t.rearrange("p (a b) -> p a b", b=W)
    k.stat_scr = bufA_t[:, 0:3072].bitcast(F32).rearrange("p (a b) -> p a b", a=3)
    k.rb = bufA_t[:, 3072:5120].rearrange("p (a b) -> p a b", a=2)
    k.mrb = bufA_t[:, 5120:7168].rearrange("p (a b) -> p a b", a=2)
    wslab = es.enter_context(nc.sbuf_tensor("wslab", [128, 2, 16, 128], BF16))[:]

    k.ps_ln = es.enter_context(nc.psum_tensor("ps_ln", [128, 2, 512], F32))[:]
    ps_mm = es.enter_context(nc.psum_tensor("ps_mm", [128, 4, 512], F32))[:]
    ps_r = es.enter_context(nc.psum_tensor("ps_r", [128, 1, 512], F32))[:]
    k.ps_r = ps_r

    wait(k, "sync", k.s_g, m_p0)
    for c in range(NCD):
        k.s_ld.inc(16)
        ew(k, "sync", lambda e, c=c, h=k.s_ld.h: e.dma_start(
            out=xres[:, c, :], in_=xT[c]).then_inc(h, 16))
    m_in = k.s_ld.n
    for e in ("scalar", "vector", "tensor"):
        wait(k, e, k.s_ld, m_in)

    emit_ln(k, xres, xhat, ((0, 512), (512, 512), (1024, 1)), bufC)
    wait(k, "vector", k.s_sc, m_vec_s)

    def emit_mix(tm_i, om_i, dstbuf):
        for c in range(NCD):
            ew(k, "vector", lambda e, c=c, om_i=om_i, dstbuf=dstbuf: e.tensor_scalar_mul(
                out=dstbuf[:, c, 0:T], in0=xhat[:, c, 0:T],
                scalar1=vtile[:, om_i, c:c + 1]), r=[xhat, vtile], w=[tg(dstbuf, c)])
        for c in range(NCD):
            ew(k, "vector", lambda e, c=c, tm_i=tm_i, dstbuf=dstbuf: e.scalar_tensor_tensor(
                out=dstbuf[:, c, 0:T], in0=xhat[:, c, 1:W], scalar=vtile[:, tm_i, c:c + 1],
                in1=dstbuf[:, c, 0:T], op0=ALU.mult, op1=ALU.add),
               r=[xhat, vtile, tg(dstbuf, c)], w=[tg(dstbuf, c)])
        return milestone(k, "vector", k.s_v)

    def proj_rhs(buf):
        return lambda dc, c0, wd: buf[:, dc, c0:c0 + wd]

    def evac_copy_to(buf, act=None):
        def f(oc, si, c0, wd, ps):
            if act is None:
                ew(k, "vector", lambda e, oc=oc, c0=c0, wd=wd, ps=ps: e.tensor_copy(
                    out=buf[:, oc, c0:c0 + wd], in_=ps), w=[tg(buf, oc)])
            else:
                ew(k, "scalar", lambda e, oc=oc, c0=c0, wd=wd, ps=ps: e.activation(
                    out=buf[:, oc, c0:c0 + wd], in_=ps, func=act), w=[tg(buf, oc)])
        return f

    mk = emit_mix(0, 6, bufB)
    mvx = emit_mix(1, 7, bufA)
    wait(k, "tensor", k.s_v, mk)
    mm_k = emit_proj(k, 16, 16, lambda oc: wk_t[oc], proj_rhs(bufB),
                     evac_copy_to(bufC), "vector", wslab, ps_mm, 4)
    wait(k, "tensor", k.s_v, mvx)
    mm_v = emit_proj(k, 16, 16, lambda oc: wv_t[oc], proj_rhs(bufA),
                     evac_copy_to(bufB), "vector", wslab, ps_mm, 4)
    wait(k, "vector", k.s_pe, mm_v[15])
    mr = emit_mix(2, 8, bufA)
    wait(k, "tensor", k.s_v, mr)
    emit_proj(k, 16, 16, lambda oc: wr_t[oc], proj_rhs(bufA),
              evac_copy_to(xhat, act=AF.Sigmoid), "scalar", wslab, ps_mm, 4)
    m_rdone = k.s_sc.n

    for c in range(NCD):
        ew(k, "vector", lambda e, c=c: e.tensor_tensor(
            out=bufC[:, c, 0:T], in0=bufC[:, c, 0:T], in1=bufB[:, c, 0:T], op=ALU.mult),
           r=[tg(bufB, c), tg(bufC, c)], w=[tg(bufC, c)])
    wait(k, "vector", k.s_pe)
    for c in range(NCD):
        ew(k, "vector", lambda e, c=c: e.tensor_tensor_scan(
            out=bufB[:, c, 1:W], data0=vtile[:, 9, c:c + 1].broadcast_to([128, T]),
            data1=bufC[:, c, 0:T], initial=0.0, op0=ALU.mult, op1=ALU.add),
           r=[tg(bufC, c), vtile], w=[tg(bufB, c)])
    ew(k, "vector", lambda e: e.tensor_copy(
        out=stile[:, 0, :], in_=bufB[:, :, W - 1:W].rearrange("p c o -> p (c o)")),
       r=[bufB], w=[stile])
    m_scanA = milestone(k, "vector", k.s_v)
    wait(k, "gpsimd", k.s_v, m_scanA)
    k.s_g.inc(16)
    ew(k, "gpsimd", lambda e, h=k.s_g.h: e.dma_start(
        out=cc_s_in[:, :], in_=stile[:, 0, :]).then_inc(h, 16))
    wait(k, "gpsimd", k.s_g)
    m_cc = k.s_cc.inc(1)
    ew(k, "gpsimd", lambda e, h=k.s_cc.h: e.collective_compute(
        "AllGather", ALU.bypass, replica_groups=PAIRS,
        ins=[cc_s_in.ap().opt()], outs=[cc_s_out.ap().opt()]).then_inc(h, 1))
    wait(k, "gpsimd", k.s_cc, m_cc)
    m_srecv = k.s_g.inc(16)
    ew(k, "gpsimd", lambda e, h=k.s_g.h: e.dma_start(
        out=stile[:, 1, :], in_=cc_s_out.ap()[0:128, :]).then_inc(h, 16))
    wait(k, "vector", k.s_g, m_srecv)
    ew(k, "vector", lambda e: e.tensor_tensor(
        out=stile[:, 2, :], in0=stile[:, 1, :], in1=vtile[:, 5, :], op=ALU.mult),
       r=[stile, vtile], w=[stile])
    for c in range(NCD):
        ew(k, "vector", lambda e, c=c: e.tensor_tensor_scan(
            out=bufB[:, c, 1:W], data0=vtile[:, 9, c:c + 1].broadcast_to([128, T]),
            data1=bufC[:, c, 0:T], initial=stile[:, 2, c:c + 1], op0=ALU.mult, op1=ALU.add),
           r=[tg(bufC, c), vtile, stile], w=[tg(bufB, c)])
    ew(k, "vector", lambda e: e.tensor_copy(
        out=bufB[:, :, 0:1].rearrange("p c o -> p (c o)"), in_=stile[:, 2, :]),
       r=[stile], w=[bufB])
    wait(k, "vector", k.s_sc, m_rdone)
    wait(k, "vector", k.s_pe)
    for c in range(NCD):
        ew(k, "vector", lambda e, c=c: e.scalar_tensor_tensor(
            out=bufA[:, c, 0:T], in0=bufC[:, c, 0:T], scalar=vtile[:, 4, c:c + 1],
            in1=bufB[:, c, 0:T], op0=ALU.mult, op1=ALU.add),
           r=[tg(bufC, c), tg(bufB, c), vtile], w=[tg(bufA, c)])
    for c in range(NCD):
        ew(k, "vector", lambda e, c=c: e.tensor_tensor(
            out=bufC[:, c, 0:T], in0=bufA[:, c, 0:T], in1=xhat[:, c, 0:T], op=ALU.mult),
           r=[tg(bufA, c), xhat], w=[tg(bufC, c)])
    m_wkv = milestone(k, "vector", k.s_v)
    wait(k, "tensor", k.s_v, m_wkv)

    def evac_addres(oc, si, c0, wd, ps):
        ew(k, "vector", lambda e, oc=oc, c0=c0, wd=wd, ps=ps: e.tensor_tensor(
            out=xres[:, oc, 1 + c0:1 + c0 + wd], in0=xres[:, oc, 1 + c0:1 + c0 + wd],
            in1=ps, op=ALU.add), r=[tg(xres, f"{oc}.{si}")], w=[tg(xres, f"{oc}.{si}")])

    emit_proj(k, 16, 16, lambda oc: wo_t[oc], proj_rhs(bufC),
              evac_addres, "vector", wslab, ps_mm, 4)

    emit_ffn0(k, xres, xhat, bufA_t, bufB_t, bufC, bufC_t, wslab, ps_mm, ps_r,
              WP_OFF, RANK_SZ)

    emit_l1attn(k, xres, xhat, bufA, bufB, bufC, bufC_t, wslab, ps_mm,
                l1wq_t, l1wk_t, l1wv_t, l1wo_t, ropec_d, band_d, killv,
                cc_kv_in, cc_kv_out)

    emit_ffn1(k, xres, xhat, bufA_t, bufB_t, bufC, bufC_t, ps_mm,
              w1_t, w3_t, W2_OFF, RANK_SZ)

    m_done_v = milestone(k, "vector", k.s_v)
    m_done_s = milestone(k, "scalar", k.s_sc)
    wait(k, "sync", k.s_v, m_done_v)
    wait(k, "sync", k.s_sc, m_done_s)
    for c in range(NCD):
        k.s_st.inc(16)
        ew(k, "sync", lambda e, c=c, h=k.s_st.h: e.dma_start(
            out=outT[c], in_=xres[:, c, 1:W]).then_inc(h, 16))
    wait(k, "sync", k.s_st)

    def _replay(e, lst):
        for f, posts in lst:
            r = f(e)
            for s in posts:
                r.then_inc(s, 1)

    with nc.Block() as block:
        @block.sync
        def _(e):
            _replay(e, k.em.q["sync"])

        @block.scalar
        def _(e):
            _replay(e, k.em.q["scalar"])

        @block.vector
        def _(e):
            _replay(e, k.em.q["vector"])

        @block.tensor
        def _(e):
            _replay(e, k.em.q["tensor"])

        @block.gpsimd
        def _(e):
            _replay(e, k.em.q["gpsimd"])

    es.close()
    return nc



def _tile_lhsT(Wm, contract_axis):
    if contract_axis == 1:
        O, Kd = Wm.shape
        Oc, Kc = O // 128, Kd // 128
        t = Wm.reshape(Oc, 128, Kc, 128).transpose(0, 3, 2, 1)
    else:
        Kd, O = Wm.shape
        Kc, Oc = Kd // 128, O // 128
        t = Wm.reshape(Kc, 128, Oc, 128).transpose(2, 1, 0, 3)
    return np.ascontiguousarray(t.reshape(Oc, 128, Kc * 128))


def kernel(**inputs):
    L0x = np.asarray(inputs["L0_attn_hidden_states"])

    wr_t = _tile_lhsT(np.asarray(inputs["L0_attn_Wr"]), 1)
    wk_t = _tile_lhsT(np.asarray(inputs["L0_attn_Wk"]), 1)
    wv_t = _tile_lhsT(np.asarray(inputs["L0_attn_Wv"]), 1)
    wo_t = _tile_lhsT(np.asarray(inputs["L0_attn_Wo"]), 1)

    def vec_tile(v):
        return np.ascontiguousarray(np.asarray(v).reshape(-1).reshape(16, 128).T)

    tmk = vec_tile(inputs["L0_attn_time_mix_k"])
    tmv = vec_tile(inputs["L0_attn_time_mix_v"])
    tmr = vec_tile(inputs["L0_attn_time_mix_r"])
    td = vec_tile(inputs["L0_attn_time_decay"])
    tf = vec_tile(inputs["L0_attn_time_first"])

    w1_t = _tile_lhsT(np.asarray(inputs["L1_ffn_W1"]), 1)
    w3_t = _tile_lhsT(np.asarray(inputs["L1_ffn_W3"]), 1)
    l1wq_t = _tile_lhsT(np.asarray(inputs["L1_attn_wq"]), 0)
    l1wk_t = _tile_lhsT(np.asarray(inputs["L1_attn_wk"]), 0)
    l1wv_t = _tile_lhsT(np.asarray(inputs["L1_attn_wv"]), 0)
    l1wo_t = _tile_lhsT(np.asarray(inputs["L1_attn_wo"]), 0)

    wgu = np.asarray(inputs["L0_ffn_w_gate_up"])
    wproj = np.asarray(inputs["L0_ffn_w_proj"])
    W2 = np.asarray(inputs["L1_ffn_W2"])

    cid = np.zeros((128, 3, 128), dtype=ml_dtypes.bfloat16)
    cid[:, 0, :] = 1.0
    cid[:, 1, :] = np.eye(128)
    sw = np.zeros((128, 128), dtype=np.float32)
    for ii in range(128):
        sw[ii, ii ^ 64] = 1.0
    cid[:, 2, :] = sw

    ii = np.arange(128)[:, None]
    jj = np.arange(1152)[None, :]
    band = np.where((jj >= ii + 1) & (jj <= ii + 1024), 0.0, NEG).astype(ml_dtypes.bfloat16)
    inv_freq = (1.0 / 10000.0 ** (np.arange(0, 128, 2, dtype=np.float32) / 128.0))

    in_maps = []
    for c in range(8):
        b, p = c // 2, c % 2
        t0 = p * T
        xTc = np.zeros((NCD, 128, W), dtype=np.float32)
        xTc[:, :, 1:] = L0x[b, t0:t0 + T, :].T.reshape(NCD, 128, T)
        if p == 1:
            xTc[:, :, 0] = L0x[b, t0 - 1, :].reshape(NCD, 128)
        vv = np.stack([tmk, tmv, tmr, td, tf,
                       np.full((128, 16), 1.0 if p == 1 else 0.0, np.float32)], axis=1)
        pos = np.arange(t0, t0 + T, dtype=np.float32)
        fr = pos[None, :] * inv_freq[:, None]
        cos2 = np.concatenate([np.cos(fr), np.cos(fr)], axis=0)
        sin2 = np.concatenate([-np.sin(fr), np.sin(fr)], axis=0)
        ropec_h = np.stack([cos2, sin2], axis=1).astype(ml_dtypes.bfloat16)
        killv_h = np.zeros((1, 2048), dtype=ml_dtypes.bfloat16)
        if p == 0:
            killv_h[0, 0:1024] = NEG
        in_maps.append({
            "xT": np.ascontiguousarray(xTc),
            "wr_t": wr_t, "wk_t": wk_t, "wv_t": wv_t, "wo_t": wo_t,
            "vecs": np.ascontiguousarray(vv.astype(np.float32)),
            "cid": cid,
            "wgu_slab": np.ascontiguousarray(wgu[c * 2048:(c + 1) * 2048, :]),
            "wproj_slab": np.ascontiguousarray(wproj[c * 256:(c + 1) * 256, :]),
            "w2_slab": np.ascontiguousarray(W2[c * 256:(c + 1) * 256, :]),
            "l1wq_t": l1wq_t, "l1wk_t": l1wk_t, "l1wv_t": l1wv_t, "l1wo_t": l1wo_t,
            "w1_t": w1_t, "w3_t": w3_t,
            "ropec_d": np.ascontiguousarray(ropec_h),
            "band_d": band, "killv_d": killv_h,
        })

    nc = build()
    res = run_bass_kernel_spmd(nc, in_maps, core_ids=list(range(8)))

    out = np.empty((4, 2048, 2048), dtype=np.float32)
    for c in range(8):
        b, p = c // 2, c % 2
        out[b, p * T:(p + 1) * T, :] = res.results[c]["outT"].reshape(D, T).T
    return out


# version 08 (final)
# speedup vs baseline: 1.0658x; 1.0124x over previous
import numpy as np
import ml_dtypes
from contextlib import ExitStack

import concourse.bass as bass
import concourse.mybir as mybir
from concourse.bass_utils import run_bass_kernel_spmd

F32 = mybir.dt.float32
BF16 = mybir.dt.bfloat16
AF = mybir.ActivationFunctionType
ALU = mybir.AluOpType
AX = mybir.AxisListType

ENGINES = ("sync", "scalar", "vector", "tensor", "gpsimd")

D = 2048
NCD = 16
T = 1024
W = T + 1
NH = 16
HDIM = 128
FFH = 8192
NFH = FFH // 128
NEG = -1.0e9
LN_EPS = 1e-5
RMS_EPS = 1.1920929e-7
ATT_SCALE = 0.08838834764831845
PAIRS = [[0, 1], [2, 3], [4, 5], [6, 7]]


class Sem:
    __slots__ = ("h", "n")

    def __init__(self, h):
        self.h = h
        self.n = 0

    def inc(self, k=1):
        self.n += k
        return self.n


class Em:

    def __init__(self):
        self.q = {e: [] for e in ENGINES}
        self.dirty = {e: set() for e in ENGINES}
        self.rdirty = {e: set() for e in ENGINES}

    @staticmethod
    def _keys(aps):
        out = set()
        for a in aps:
            if a is None:
                continue
            s = a if isinstance(a, str) else a.name
            name, _, tag = s.partition(":")
            out.add((name, tag or None))
        return out

    @staticmethod
    def _conflict(ks, dirty):
        for n, t in ks:
            for dn, dt in dirty:
                if n == dn and (t is None or dt is None or t == dt):
                    return True
        return False

    def on(self, e, f, r=(), w=()):
        rk = self._keys(r)
        wk = self._keys(w)
        if (rk and self._conflict(rk, self.dirty[e])) or \
           (wk and (self._conflict(wk, self.rdirty[e]) or self._conflict(wk, self.dirty[e]))):
            self.q[e].append([lambda eng: eng.drain(), []])
            self.dirty[e] = set()
            self.rdirty[e] = set()
        self.q[e].append([f, []])
        self.dirty[e] |= wk
        self.rdirty[e] |= rk

    def drain(self, e):
        if self.dirty[e] or self.rdirty[e]:
            self.q[e].append([lambda eng: eng.drain(), []])
            self.dirty[e] = set()
            self.rdirty[e] = set()


class K:
    pass


def ew(k, eng, f, r=(), w=()):
    k.em.on(eng, f, r, w)


def tg(ap, tag):
    return f"{ap.name}:{tag}"


def wait(k, eng, sem, mark=None):
    v = sem.n if mark is None else mark
    if v > 0:
        k.em.on(eng, lambda e, s=sem.h, v=v: e.wait_ge(s, v))


def milestone(k, eng, sem):
    q = k.em.q[eng]
    if not q or q[-1][1]:
        k.em.q[eng].append([lambda e: e.drain(), []])
        k.em.dirty[eng] = set()
        v = sem.inc(1)
        k.em.on(eng, lambda e, s=sem.h: e.nop().then_inc(s, 1))
        return v
    prev = sem.n
    if prev > 0:
        q.insert(len(q) - 1, [lambda e, s=sem.h, p=prev: e.wait_ge(s, p), []])
    v = sem.inc(1)
    q[-1][1].append(sem.h)
    return v


SEGS_T = ((0, 512), (512, 512))



def emit_ln(k, src, dst, segs, sqbuf):
    stat, rb, mrb, ps = k.stat_scr, k.rb, k.mrb, k.ps_ln
    for si, (c0, wd) in enumerate(segs):
        ps_s = ps[:, 0, 0:wd]
        ps_q = ps[:, 1, 0:wd]
        for c in range(NCD):
            ew(k, "scalar", lambda e, c=c, c0=c0, wd=wd: e.activation(
                out=dst[:, c, c0:c0 + wd], in_=src[:, c, c0:c0 + wd], func=AF.Copy),
               r=[src], w=[dst])
        m_cast = milestone(k, "scalar", k.s_sc)
        wait(k, "vector", k.s_sc, m_cast)
        wait(k, "tensor", k.s_sc, m_cast)
        for c in range(NCD):
            ew(k, "vector", lambda e, c=c, c0=c0, wd=wd: e.tensor_tensor(
                out=sqbuf[:, c, 0:wd], in0=dst[:, c, c0:c0 + wd],
                in1=dst[:, c, c0:c0 + wd], op=ALU.mult), r=[dst], w=[sqbuf])
        m_sq = milestone(k, "vector", k.s_v)
        for c in range(NCD):
            ew(k, "tensor", lambda e, c=c, c0=c0, wd=wd, ps_s=ps_s: e.matmul(
                ps_s, k.ones, dst[:, c, c0:c0 + wd], start=(c == 0), stop=(c == NCD - 1)))
        wait(k, "tensor", k.s_v, m_sq)
        for c in range(NCD):
            ew(k, "tensor", lambda e, c=c, wd=wd, ps_q=ps_q: e.matmul(
                ps_q, k.ones, sqbuf[:, c, 0:wd], start=(c == 0), stop=(c == NCD - 1)))
        m_mm = milestone(k, "tensor", k.s_pe)
        wait(k, "vector", k.s_pe, m_mm)
        mean = stat[:, 0, 0:wd]
        var = stat[:, 1, 0:wd]
        msq = stat[:, 2, 0:wd]
        ew(k, "vector", lambda e, mean=mean, ps_s=ps_s: e.tensor_scalar_mul(
            out=mean, in0=ps_s, scalar1=1.0 / D), w=[stat])
        ew(k, "vector", lambda e, var=var, ps_q=ps_q: e.tensor_scalar(
            out=var, in0=ps_q, scalar1=1.0 / D, scalar2=LN_EPS, op0=ALU.mult, op1=ALU.add),
           w=[stat])
        ew(k, "vector", lambda e, mean=mean, msq=msq: e.tensor_tensor(
            out=msq, in0=mean, in1=mean, op=ALU.mult), r=[stat], w=[stat])
        ew(k, "vector", lambda e, var=var, msq=msq: e.tensor_tensor(
            out=var, in0=var, in1=msq, op=ALU.subtract), r=[stat], w=[stat])
        m_v1 = milestone(k, "vector", k.s_v)
        wait(k, "scalar", k.s_v, m_v1)
        ew(k, "scalar", lambda e, var=var: e.activation(out=var, in_=var, func=AF.Sqrt),
           r=[stat], w=[stat])
        m_s2 = milestone(k, "scalar", k.s_sc)
        wait(k, "vector", k.s_sc, m_s2)
        rbs = rb[:, si % 2, 0:wd]
        mrbs = mrb[:, si % 2, 0:wd]

        def _recip(e, var=var, rbs=rbs, nc_=k.nc):
            with nc_.allow_low_precision(reason="rstd broadcast tile stored bf16"):
                return e.reciprocal(out=rbs, in_=var)
        ew(k, "vector", _recip, r=[stat], w=[rb])
        ew(k, "vector", lambda e, mean=mean, rbs=rbs, mrbs=mrbs: e.tensor_tensor(
            out=mrbs, in0=mean, in1=rbs, op=ALU.mult), r=[stat, rb], w=[mrb])
        for c in range(NCD):
            ew(k, "vector", lambda e, c=c, c0=c0, wd=wd, rbs=rbs: e.tensor_tensor(
                out=dst[:, c, c0:c0 + wd], in0=dst[:, c, c0:c0 + wd], in1=rbs, op=ALU.mult),
               r=[tg(dst, c), rb], w=[tg(dst, c)])
        for c in range(NCD):
            ew(k, "vector", lambda e, c=c, c0=c0, wd=wd, mrbs=mrbs: e.tensor_tensor(
                out=dst[:, c, c0:c0 + wd], in0=dst[:, c, c0:c0 + wd], in1=mrbs,
                op=ALU.subtract), r=[tg(dst, c), mrb], w=[tg(dst, c)])
    milestone(k, "vector", k.s_v)



def emit_proj(k, n_oc, n_kc, w_src, rhs, evac, evac_eng, wslab, ps_pool, n_banks,
              segs=SEGS_T, w_queue="gpsimd", w_sems=None, load_wait=None):
    w_sems = w_sems or (k.s_w, k.s_w2)
    evac_sem = k.s_v if evac_eng == "vector" else k.s_sc
    nseg = len(segs)
    evac_marks = {}
    load_marks = {}
    mm_marks = {}

    def emit_load(oc):
        if oc >= 2:
            wait(k, w_queue, k.s_pe, mm_marks[oc - 2])
        else:
            wait(k, w_queue, k.s_pe)
            if load_wait is not None:
                wait(k, w_queue, load_wait[0], load_wait[1])
        ld = w_sems[oc % 2].inc(16)
        ew(k, w_queue, lambda e, oc=oc, h=w_sems[oc % 2].h: e.dma_start(
            out=wslab[:, oc % 2, :, :].rearrange("p a b -> p (a b)"),
            in_=w_src(oc)).then_inc(h, 16))
        load_marks[oc] = ld

    emit_load(0)
    if n_oc > 1:
        emit_load(1)
    for oc in range(n_oc):
        wait(k, "tensor", w_sems[oc % 2], load_marks[oc])
        if oc >= max(1, n_banks // nseg):
            wait(k, "tensor", evac_sem, evac_marks[oc - max(1, n_banks // nseg)])
        elif oc == 0:
            wait(k, "tensor", k.s_v)
            wait(k, "tensor", k.s_sc)
        for si, (c0, wd) in enumerate(segs):
            pb = (nseg * oc + si) % n_banks
            for dc in range(n_kc):
                ew(k, "tensor", lambda e, oc=oc, dc=dc, c0=c0, wd=wd, pb=pb: e.matmul(
                    ps_pool[:, pb, 0:wd], wslab[:, oc % 2, dc, :], rhs(dc, c0, wd),
                    start=(dc == 0), stop=(dc == n_kc - 1)))
        mm = milestone(k, "tensor", k.s_pe)
        mm_marks[oc] = mm
        if oc + 2 < n_oc:
            emit_load(oc + 2)
        wait(k, evac_eng, k.s_pe, mm)
        for si, (c0, wd) in enumerate(segs):
            pb = (nseg * oc + si) % n_banks
            evac(oc, si, c0, wd, ps_pool[:, pb, 0:wd])
        evac_marks[oc] = milestone(k, evac_eng, evac_sem)
    return mm_marks





def emit_phase0(k, wgu_slab, wproj_slab, w2_slab, cc_w_in, WP_OFF, W2_OFF):
    nc = k.nc
    es0 = ExitStack()
    ld = es0.enter_context(nc.sbuf_tensor("p0_ld", [128, 2, FFH], BF16))[:]
    m1 = es0.enter_context(nc.sbuf_tensor("p0_m1", [128, FFH], BF16))[:]
    m2 = es0.enter_context(nc.sbuf_tensor("p0_m2", [128, FFH], BF16))[:]
    st_ = es0.enter_context(nc.sbuf_tensor("p0_st", [128, 64, 128], BF16))[:]
    qs = es0.enter_context(nc.sbuf_tensor("p0_qs", [128, 3, 128], F32))[:]
    sbf = es0.enter_context(nc.sbuf_tensor("p0_sbf", [128, 128], BF16))[:]
    pp = es0.enter_context(nc.sbuf_tensor("p0_pp", [128, 2, FFH], BF16))[:]
    pt = es0.enter_context(nc.psum_tensor("p0_pt", [128, 2, 1024], BF16))[:]
    in_cc = cc_w_in.ap()
    last_stage = [0]

    def quant(par, ncols):
        ng = ncols // 64
        lv = ld[:, par, 0:ncols].rearrange("p (g e) -> p g e", e=64)
        gb = lambda a: a.rearrange("p (g o) -> p g o", o=1).broadcast_to([128, ng, 64])
        ew(k, "vector", lambda e, lv=lv, ng=ng: e.reduce_sum(
            out=qs[:, 0, 0:ng].rearrange("p (g o) -> p g o", o=1), in_=lv,
            axis=AX.X, apply_absolute_value=True), r=[tg(ld, par)], w=[qs])
        ew(k, "vector", lambda e, ng=ng: e.tensor_scalar(
            out=qs[:, 0, 0:ng], in0=qs[:, 0, 0:ng], scalar1=1.0 / 64, scalar2=1e-8,
            op0=ALU.mult, op1=ALU.max), r=[qs], w=[qs])
        ew(k, "vector", lambda e, ng=ng: e.tensor_copy(
            out=sbf[:, 0:ng], in_=qs[:, 0, 0:ng]), r=[qs], w=[sbf])

        def _rcp(e, ng=ng, nc_=k.nc):
            with nc_.allow_low_precision(reason="group scale reciprocal"):
                e.reciprocal(out=qs[:, 1, 0:ng], in_=sbf[:, 0:ng])
        ew(k, "vector", _rcp, r=[sbf], w=[qs])
        m1v = m1[:, 0:ncols].rearrange("p (g e) -> p g e", e=64)
        m2v = m2[:, 0:ncols].rearrange("p (g e) -> p g e", e=64)
        ew(k, "vector", lambda e, lv=lv, m1v=m1v, ng=ng: e.tensor_tensor(
            out=m1v, in0=lv, in1=gb(qs[:, 1, 0:ng]), op=ALU.mult),
           r=[tg(ld, par), qs], w=[m1])
        ew(k, "vector", lambda e, ncols=ncols: e.tensor_single_scalar(
            out=m2[:, 0:ncols], in_=m1[:, 0:ncols], scalar=0.5, op=ALU.is_gt),
           r=[m1], w=[m2])
        ew(k, "vector", lambda e, ncols=ncols: e.tensor_single_scalar(
            out=m1[:, 0:ncols], in_=m1[:, 0:ncols], scalar=-0.5, op=ALU.is_lt),
           r=[m1], w=[m1])
        ew(k, "vector", lambda e, ncols=ncols: e.tensor_tensor(
            out=m1[:, 0:ncols], in0=m2[:, 0:ncols], in1=m1[:, 0:ncols], op=ALU.subtract),
           r=[m1, m2], w=[m1])
        ew(k, "vector", lambda e, m1v=m1v, ng=ng: e.tensor_tensor(
            out=m1v, in0=m1v, in1=gb(sbf[:, 0:ng]), op=ALU.mult),
           r=[m1, sbf], w=[m1])
        return milestone(k, "vector", k.s_v)

    def transpose_stage(ntiles, off, m_src):
        wait(k, "tensor", k.s_v, m_src)
        wait(k, "vector", k.s_g, last_stage[0])
        ev = {}
        for g0 in range(0, ntiles, 8):
            gcnt = min(8, ntiles - g0)
            bank = (g0 // 8) % 2
            if g0 >= 16:
                wait(k, "tensor", k.s_v, ev[g0 - 16])
            for t in range(gcnt):
                cc = g0 + t
                ew(k, "tensor", lambda e, cc=cc, bank=bank, t=t: e.transpose(
                    pt[:, bank, t * 128:(t + 1) * 128],
                    m1[:, cc * 128:(cc + 1) * 128], k.ident))
            mk = milestone(k, "tensor", k.s_pe)
            wait(k, "vector", k.s_pe, mk)
            ew(k, "vector", lambda e, g0=g0, bank=bank, gcnt=gcnt: e.tensor_copy(
                out=st_[:, g0:g0 + gcnt, :].rearrange("p a b -> p (a b)"),
                in_=pt[:, bank, 0:gcnt * 128]), w=[st_])
            ev[g0] = milestone(k, "vector", k.s_v)
        wait(k, "gpsimd", k.s_v, ev[(ntiles - 1) // 8 * 8])
        sz = ntiles * 128 * 128
        last_stage[0] = k.s_g.inc(16)
        ew(k, "gpsimd", lambda e, off=off, sz=sz, ntiles=ntiles, h=k.s_g.h: e.dma_start(
            out=in_cc[off:off + sz].rearrange("(p m) -> p m", p=128),
            in_=st_[:, 0:ntiles, :].rearrange("p a b -> p (a b)")).then_inc(h, 16))
        return milestone(k, "tensor", k.s_pe)

    qm = {}
    tm = {}
    for l in range(16):
        par = l % 2
        if l >= 2:
            wait(k, "gpsimd", k.s_v, qm[l - 2])
        sem = (k.s_w, k.s_w2)[par]
        lm = sem.inc(16)
        ew(k, "gpsimd", lambda e, l=l, par=par, h=sem.h: e.dma_start(
            out=ld[:, par, 0:2048], in_=wgu_slab[l * 128:(l + 1) * 128, :]).then_inc(h, 16))
        wait(k, "vector", sem, lm)
        if l >= 1:
            wait(k, "vector", k.s_pe, tm[l - 1])
        qm[l] = quant(par, 2048)
        tm[l] = transpose_stage(16, l * 262144, qm[l])

    for l in range(2):
        par = l % 2
        sem = (k.s_w, k.s_w2)[par]
        wait(k, "gpsimd", k.s_v, qm[14 + l] if l else qm[14])
        lm = sem.inc(16)
        ew(k, "gpsimd", lambda e, l=l, par=par, h=sem.h: e.dma_start(
            out=ld[:, par, :], in_=wproj_slab[l * 128:(l + 1) * 128, :]).then_inc(h, 16))
        wait(k, "vector", sem, lm)
        wait(k, "vector", k.s_pe, tm[15 + l] if l else tm[15])
        qm[16 + l] = quant(par, FFH)
        tm[16 + l] = transpose_stage(64, WP_OFF + l * 1048576, qm[16 + l])

    SCL = float(FFH) ** -0.5
    bm = {}
    for l in range(2):
        lm = k.s_w.inc(16)
        if l == 1:
            wait(k, "gpsimd", k.s_v, bm[0])
        ew(k, "gpsimd", lambda e, l=l, h=k.s_w.h: e.dma_start(
            out=pp[:, 0, :], in_=w2_slab[l * 128:(l + 1) * 128, :]).then_inc(h, 16))
        wait(k, "vector", k.s_w, lm)
        par = 0
        for s in range(13):
            h2 = 1 << s
            sv = pp[:, par, :].rearrange("p (nb two h) -> p nb two h", two=2, h=h2)
            dv = pp[:, 1 - par, :].rearrange("p (nb two h) -> p nb two h", two=2, h=h2)
            ew(k, "vector", lambda e, sv=sv, dv=dv: e.tensor_tensor(
                out=dv[:, :, 0, :], in0=sv[:, :, 0, :], in1=sv[:, :, 1, :], op=ALU.add),
               r=[tg(pp, par)], w=[tg(pp, 1 - par)])
            ew(k, "vector", lambda e, sv=sv, dv=dv: e.tensor_tensor(
                out=dv[:, :, 1, :], in0=sv[:, :, 0, :], in1=sv[:, :, 1, :], op=ALU.subtract),
               r=[tg(pp, par)], w=[tg(pp, 1 - par)])
            par = 1 - par
        wait(k, "vector", k.s_pe, tm[17] if l else tm[17])
        ew(k, "vector", lambda e, par=par: e.tensor_scalar_mul(
            out=m1[:, :], in0=pp[:, par, :], scalar1=SCL), r=[tg(pp, par)], w=[m1])
        bm[l] = milestone(k, "vector", k.s_v)
        tm[18 + l] = transpose_stage(64, W2_OFF + l * 1048576, bm[l])

    wait(k, "gpsimd", k.s_g)
    k.m_ccw = k.s_cc.inc(1)
    ew(k, "gpsimd", lambda e, h=k.s_cc.h: e.collective_compute(
        "AllGather", ALU.bypass, replica_groups=[list(range(8))],
        ins=[cc_w_in.ap().opt()], outs=[k.cc_w_out_ap.opt()]).then_inc(h, 1))
    es0.close()



def emit_ffn0(k, xres, xhat, bufA_t, bufB_t, bufC, bufC_t, wslab, ps_mm, ps_r,
              WP_OFF, RANK_SZ):
    occ = k.cc_w_out_ap

    def wgu_slab_ap(hc):
        r, l = divmod(hc, 16)
        off = r * RANK_SZ + l * 262144
        return occ[off:off + 262144].rearrange("(p m) -> p m", p=128)

    def wp_slab_ap(dc):
        r, l = divmod(dc, 2)
        off = r * RANK_SZ + WP_OFF + l * 1048576
        return occ[off:off + 1048576].rearrange("(p m) -> p m", p=128)

    hA = bufA_t[:, 0:16384].rearrange("p (a b) -> p a b", b=512)
    hB = bufB_t[:, 0:16384].rearrange("p (a b) -> p a b", b=512)

    def hbuf(fc):
        return (hA if fc < 32 else hB)[:, fc % 32, :]

    def htag(fc):
        return tg(bufA_t if fc < 32 else bufB_t, fc % 32)

    gslab = bufC_t[:, 0:8192].rearrange("p (a b c) -> p a b c", b=16, c=128)
    sqt = bufC_t[:, 8192:10240].rearrange("p (a b) -> p a b", a=4)
    sg = bufC_t[:, 10240:11264].rearrange("p (a b) -> p a b", a=2)
    tstat = bufC_t[:, 11264:12288].bitcast(F32)
    rsb = bufC_t[:, 12288:13312].rearrange("p (a b) -> p a b", a=2)
    pslab = bufC_t[:, 0:16384].rearrange("p (a b c) -> p a b c", b=64, c=128)
    ksl = (k.s_k, k.s_k2, k.s_k3, k.s_k4)

    wait(k, "scalar", k.s_v)
    wait(k, "vector", k.s_pe)
    emit_ln(k, xres, xhat, ((1, 512), (513, 512)), bufC)

    for th in range(2):
        cx = 1 + th * 512
        wait(k, "sync", k.s_pe)
        wait(k, "vector", k.s_pe)
        wait(k, "scalar", k.s_pe)
        wait(k, "tensor", k.s_v)
        wait(k, "tensor", k.s_sc)
        if th == 0:
            wait(k, "sync", k.s_cc, k.m_ccw)
        load_marks = {}

        def gu_load(fc, which):
            slot = (2 * fc + which) % 4
            sem = ksl[slot]
            lm = sem.inc(16)
            ew(k, "sync", lambda e, fc=fc, which=which, slot=slot, h=sem.h: e.dma_start(
                out=gslab[:, slot, :, :].rearrange("p a b -> p (a b)"),
                in_=wgu_slab_ap(fc + 64 * which)).then_inc(h, 16))
            load_marks[(fc, which)] = lm

        gu_load(0, 0)
        gu_load(0, 1)
        gu_load(1, 0)
        gu_load(1, 1)
        gmark = {}
        umark = {}
        hmark = {}
        smark = {}
        sqmark = {}
        ssmark = {}
        for fc in range(64):
            slot_g = (2 * fc) % 4
            slot_u = (2 * fc + 1) % 4
            wait(k, "tensor", ksl[slot_g], load_marks[(fc, 0)])
            if fc >= 2:
                wait(k, "tensor", k.s_sc, smark[fc - 2])
            for dc in range(NCD):
                ew(k, "tensor", lambda e, dc=dc, slot_g=slot_g, cx=cx: e.matmul(
                    ps_mm[:, slot_g % 4, :], gslab[:, slot_g, dc, :],
                    xhat[:, dc, cx:cx + 512], start=(dc == 0), stop=(dc == NCD - 1)))
            gmark[fc] = milestone(k, "tensor", k.s_pe)
            wait(k, "tensor", ksl[slot_u], load_marks[(fc, 1)])
            if fc >= 2:
                wait(k, "tensor", k.s_v, hmark[fc - 2])
            for dc in range(NCD):
                ew(k, "tensor", lambda e, dc=dc, slot_u=slot_u, cx=cx: e.matmul(
                    ps_mm[:, slot_u % 4, :], gslab[:, slot_u, dc, :],
                    xhat[:, dc, cx:cx + 512], start=(dc == 0), stop=(dc == NCD - 1)))
            umark[fc] = milestone(k, "tensor", k.s_pe)
            if fc >= 2:
                f2 = fc - 2
                wait(k, "tensor", k.s_sc, sqmark[f2])
                ew(k, "tensor", lambda e, f2=f2: e.matmul(
                    ps_r[:, 0, :], k.ones, sqt[:, f2 % 4, :],
                    start=(f2 == 0), stop=False))
                ssmark[f2] = milestone(k, "tensor", k.s_pe)
            if fc + 2 < 64:
                wait(k, "sync", k.s_pe, gmark[fc])
                gu_load(fc + 2, 0)
                wait(k, "sync", k.s_pe, umark[fc])
                gu_load(fc + 2, 1)
            wait(k, "scalar", k.s_pe, gmark[fc])
            if fc >= 2:
                wait(k, "scalar", k.s_v, hmark[fc - 2])
            ew(k, "scalar", lambda e, fc=fc, slot_g=slot_g: e.activation(
                out=sg[:, fc % 2, :], in_=ps_mm[:, slot_g % 4, :], func=AF.Silu),
               w=[f"bufC:sg{fc % 2}"])
            smark[fc] = milestone(k, "scalar", k.s_sc)
            wait(k, "vector", k.s_pe, umark[fc])
            wait(k, "vector", k.s_sc, smark[fc])
            ew(k, "vector", lambda e, fc=fc, slot_u=slot_u: e.tensor_tensor(
                out=hbuf(fc), in0=sg[:, fc % 2, :], in1=ps_mm[:, slot_u % 4, :],
                op=ALU.mult), r=[f"bufC:sg{fc % 2}"], w=[htag(fc)])
            hmark[fc] = milestone(k, "vector", k.s_v)
            if fc >= 4:
                wait(k, "scalar", k.s_pe, ssmark[fc - 4])
            wait(k, "scalar", k.s_v, hmark[fc])
            ew(k, "scalar", lambda e, fc=fc: e.activation(
                out=sqt[:, fc % 4, :], in_=hbuf(fc), func=AF.Square),
               w=[f"bufC:sq{fc % 4}"])
            sqmark[fc] = milestone(k, "scalar", k.s_sc)
        for f2 in (62, 63):
            wait(k, "tensor", k.s_sc, sqmark[f2])
            ew(k, "tensor", lambda e, f2=f2: e.matmul(
                ps_r[:, 0, :], k.ones, sqt[:, f2 % 4, :], start=False, stop=(f2 == 63)))
        mss = milestone(k, "tensor", k.s_pe)
        wait(k, "vector", k.s_pe, mss)
        wait(k, "vector", k.s_sc)
        ew(k, "vector", lambda e: e.tensor_scalar(
            out=tstat[:, 0:512], in0=ps_r[:, 0, :], scalar1=1.0 / FFH, scalar2=RMS_EPS,
            op0=ALU.mult, op1=ALU.add), w=["bufC:ts"])
        mvst = milestone(k, "vector", k.s_v)
        wait(k, "scalar", k.s_v, mvst)
        ew(k, "scalar", lambda e: e.activation(
            out=tstat[:, 0:512], in_=tstat[:, 0:512], func=AF.Sqrt),
           r=["bufC:ts"], w=["bufC:ts"])
        msqr = milestone(k, "scalar", k.s_sc)
        wait(k, "vector", k.s_sc, msqr)

        def _recipr(e, th=th, nc_=k.nc):
            with nc_.allow_low_precision(reason="rms rstd tile stored bf16"):
                return e.reciprocal(out=rsb[:, th, :], in_=tstat[:, 0:512])
        ew(k, "vector", _recipr, r=["bufC:ts"], w=["bufC:rs"])
        for fc in range(64):
            ew(k, "vector", lambda e, fc=fc, th=th: e.tensor_tensor(
                out=hbuf(fc), in0=hbuf(fc), in1=rsb[:, th, :], op=ALU.mult),
               r=[htag(fc), "bufC:rs"], w=[htag(fc)])
        mhn = milestone(k, "vector", k.s_v)
        wait(k, "tensor", k.s_v, mhn)

        def evac_ffn(oc, si, c0_, wd, ps, th=th, cx=cx):
            ew(k, "vector", lambda e, oc=oc, ps=ps, cx=cx: e.tensor_tensor(
                out=xres[:, oc, cx:cx + 512], in0=xres[:, oc, cx:cx + 512],
                in1=ps, op=ALU.add),
               r=[tg(xres, f"f{oc}.{th}")], w=[tg(xres, f"f{oc}.{th}")])

        emit_proj(k, 16, 64, wp_slab_ap, lambda dc, c0_, wd: hbuf(dc),
                  evac_ffn, "vector", pslab, ps_mm, 4, segs=((0, 512),),
                  w_queue="sync", w_sems=(k.s_k, k.s_k2), load_wait=(k.s_v, mhn))





def emit_l1attn(k, xres, xhat, bufA, bufB, bufC, bufC_t, wslab, ps_mm,
                wq_t, wk_t, wv_t, wo_t, ropec_d, band_d, killv, cc_kv_in, cc_kv_out):
    nc = k.nc
    ropec = bufC_t[:, 14016:16064].rearrange("p (a b) -> p a b", a=2)
    band = bufC_t[:, 12864:14016]
    KHD = 131072
    kin = cc_kv_in.ap()
    kout = cc_kv_out.ap()

    def kv_in_K(h):
        return kin[h * KHD:(h + 1) * KHD].rearrange("(p m) -> p m", p=128)

    def kv_in_V(h):
        off = 16 * KHD + h * KHD
        return kin[off:off + KHD].rearrange("(p m) -> p m", p=128)

    def kv_out_K(h):
        return kout[h * KHD:(h + 1) * KHD].rearrange("(p m) -> p m", p=128)

    def kv_out_V(h):
        off = 16 * KHD + h * KHD
        return kout[off:off + KHD].rearrange("(p m) -> p m", p=128)

    ps_s = ps_mm[:, 0:3, :].rearrange("p a b -> p (a b)")
    ps_pt = ps_mm[:, 3, :].bitcast(BF16)
    ps_pt2 = k.ps_ln[:, 0, 0:64].bitcast(BF16)
    ps_ot = [k.ps_ln[:, 1, 0:128], k.ps_r[:, 0, 0:128]]

    wait(k, "scalar", k.s_v)
    wait(k, "scalar", k.s_pe)
    wait(k, "vector", k.s_pe)
    emit_ln(k, xres, xhat, ((1, 512), (513, 512)), bufC)

    def proj_rhs_x(dc, c0, wd):
        return xhat[:, dc, 1 + c0:1 + c0 + wd]

    def evac_copy_to(buf):
        def f(oc, si, c0, wd, ps):
            ew(k, "vector", lambda e, oc=oc, c0=c0, wd=wd, ps=ps: e.tensor_copy(
                out=buf[:, oc, c0:c0 + wd], in_=ps), w=[tg(buf, f"a{oc}")])
        return f

    def emit_rope(buf, h):
        wait(k, "tensor", k.s_v)
        for s2 in range(2):
            ew(k, "tensor", lambda e, h=h, s2=s2, buf=buf: e.matmul(
                ps_mm[:, 2 + s2, :], k.swap, buf[:, h, s2 * 512:(s2 + 1) * 512],
                start=True, stop=True))
        mpe = milestone(k, "tensor", k.s_pe)
        ew(k, "vector", lambda e, h=h, buf=buf: e.tensor_tensor(
            out=bufC[:, 0, 0:T], in0=buf[:, h, 0:T], in1=ropec[:, 0, :], op=ALU.mult),
           r=[tg(buf, f"a{h}")], w=[tg(bufC_t, "rt0")])
        wait(k, "vector", k.s_pe, mpe)
        for s2 in range(2):
            ew(k, "vector", lambda e, s2=s2: e.tensor_tensor(
                out=bufC[:, 1, s2 * 512:(s2 + 1) * 512], in0=ps_mm[:, 2 + s2, :],
                in1=ropec[:, 1, s2 * 512:(s2 + 1) * 512], op=ALU.mult),
               w=[tg(bufC_t, "rt1")])
        ew(k, "vector", lambda e, h=h, buf=buf: e.tensor_tensor(
            out=buf[:, h, 0:T], in0=bufC[:, 0, 0:T], in1=bufC[:, 1, 0:T], op=ALU.add),
           r=[tg(bufC_t, "rt0"), tg(bufC_t, "rt1")], w=[tg(buf, f"a{h}")])
        return milestone(k, "vector", k.s_v)

    wait(k, "sync", k.s_pe)
    m_rc1 = k.s_ld.inc(16)
    ew(k, "sync", lambda e, h=k.s_ld.h: e.dma_start(
        out=ropec, in_=ropec_d).then_inc(h, 16))
    wait(k, "vector", k.s_ld, m_rc1)
    wait(k, "tensor", k.s_v)
    emit_proj(k, 16, 16, lambda oc: wk_t[oc], proj_rhs_x,
              evac_copy_to(bufB), "vector", wslab, ps_mm, 4)
    for h in range(NH):
        mr = emit_rope(bufB, h)
        k.s_g.inc(16)
        wait(k, "gpsimd", k.s_v, mr)
        ew(k, "gpsimd", lambda e, h=h, hh=k.s_g.h: e.dma_start(
            out=kv_in_K(h), in_=bufB[:, h, 0:T]).then_inc(hh, 16))

    wait(k, "vector", k.s_pe)
    emit_proj(k, 16, 16, lambda oc: wv_t[oc], proj_rhs_x,
              evac_copy_to(bufC), "vector", wslab, ps_mm, 4)
    ps_vt = ps_mm[:, 2:4, :].rearrange("p a b -> p (a b)").bitcast(BF16)
    wait(k, "tensor", k.s_v)
    mv_prev = None
    for h in range(NH):
        if mv_prev is not None:
            wait(k, "tensor", k.s_v, mv_prev)
        for t8 in range(8):
            ew(k, "tensor", lambda e, h=h, t8=t8: e.transpose(
                ps_vt[:, t8 * 128:(t8 + 1) * 128],
                bufC[:, h, t8 * 128:(t8 + 1) * 128], k.ident))
        mpe = milestone(k, "tensor", k.s_pe)
        wait(k, "vector", k.s_pe, mpe)
        if h > 0:
            wait(k, "vector", k.s_g)
        ew(k, "vector", lambda e, h=h: e.tensor_copy(
            out=bufC[:, h, 0:T], in_=ps_vt[:, 0:1024]), w=[tg(bufC_t, f"v{h}")])
        mv = milestone(k, "vector", k.s_v)
        mv_prev = mv
        k.s_g.inc(16)
        wait(k, "gpsimd", k.s_v, mv)
        ew(k, "gpsimd", lambda e, h=h, hh=k.s_g.h: e.dma_start(
            out=kv_in_V(h), in_=bufC[:, h, 0:T]).then_inc(hh, 16))

    wait(k, "gpsimd", k.s_g)
    m_kv = k.s_cc.inc(1)
    ew(k, "gpsimd", lambda e, hh=k.s_cc.h: e.collective_compute(
        "AllGather", ALU.bypass, replica_groups=PAIRS,
        ins=[cc_kv_in.ap().opt()], outs=[cc_kv_out.ap().opt()]).then_inc(hh, 1))

    emit_proj(k, 16, 16, lambda oc: wq_t[oc], proj_rhs_x,
              evac_copy_to(bufA), "vector", wslab, ps_mm, 4)
    wait(k, "sync", k.s_g)
    m_rc2 = k.s_ld.inc(32)
    ew(k, "sync", lambda e, h=k.s_ld.h: e.dma_start(
        out=ropec, in_=ropec_d).then_inc(h, 16))
    ew(k, "sync", lambda e, h=k.s_ld.h: e.dma_start(
        out=band[:, :], in_=band_d).then_inc(h, 16))
    wait(k, "vector", k.s_ld, m_rc2)
    wait(k, "vector", k.s_g)
    rope_q_marks = {}
    for h in range(NH):
        rope_q_marks[h] = emit_rope(bufA, h)

    kbuf = bufC_t[:, 0:4096].rearrange("p (s m) -> p s m", s=2)
    vbuf = bufC_t[:, 4096:8192].rearrange("p (s a c) -> p s a c", s=2, a=16)
    Pt = bufC_t[:, 8192:10496].rearrange("p (s m) -> p s m", s=2)
    ptile = bufC_t[:, 10496:12800].rearrange("p (s m) -> p s m", s=2)
    smal = bufC_t[:, 12800:12864].bitcast(F32)

    wait(k, "vector", k.s_pe)
    wait(k, "sync", k.s_cc, m_kv)
    wait(k, "sync", k.s_v)
    wait(k, "sync", k.s_g)

    NIT = NH * 8
    kvsem = (k.s_k3, k.s_k4)
    kvload = {}

    def kv_load(h):
        sl = h % 2
        sem = kvsem[sl]
        ew(k, "sync", lambda e, h=h, sl=sl, hh=sem.h: e.dma_start(
            out=kbuf[:, sl, 0:1024], in_=kv_out_K(h)).then_inc(hh, 16))
        ew(k, "sync", lambda e, h=h, sl=sl, hh=sem.h: e.dma_start(
            out=kbuf[:, sl, 1024:2048], in_=bufB[:, h, 0:T]).then_inc(hh, 16))
        ew(k, "sync", lambda e, h=h, sl=sl, hh=sem.h: e.dma_start(
            out=vbuf[:, sl, 0:8, :].rearrange("p a c -> p (a c)"),
            in_=kv_out_V(h)).then_inc(hh, 16))
        ew(k, "sync", lambda e, h=h, sl=sl, hh=sem.h: e.dma_start(
            out=vbuf[:, sl, 8:16, :].rearrange("p a c -> p (a c)"),
            in_=kv_in_V(h)).then_inc(hh, 16))
        kvload[h] = sem.inc(64)

    kv_load(0)
    kv_load(1)

    ps_marks = {}
    mx_marks = {}
    exp_marks = {}
    pn_marks = {}
    pt_marks = {}
    pte_marks = {}
    av_marks = {}
    oe_marks = {}

    for i in range(NIT + 2):
        h, qt = divmod(i, 8)
        if i < NIT:
            sl = h % 2
            if qt == 0:
                wait(k, "tensor", kvsem[sl], kvload[h])
                wait(k, "tensor", k.s_v, rope_q_marks[h])
            if i >= 1:
                wait(k, "tensor", k.s_sc, exp_marks[i - 1])
            ws = qt * 128
            ew(k, "tensor", lambda e, h=h, qt=qt, sl=sl, ws=ws: e.matmul(
                ps_s[:, 0:512], bufA[:, h, qt * 128:(qt + 1) * 128],
                kbuf[:, sl, ws:ws + 512], start=True, stop=False))
            ew(k, "tensor", lambda e, h=h, qt=qt, sl=sl, ws=ws: e.matmul(
                ps_s[:, 512:1024], bufA[:, h, qt * 128:(qt + 1) * 128],
                kbuf[:, sl, ws + 512:ws + 1024], start=True, stop=(qt >= 4)))
            ew(k, "tensor", lambda e, h=h, qt=qt, sl=sl, ws=ws: e.matmul(
                ps_s[:, 1024:1152], bufA[:, h, qt * 128:(qt + 1) * 128],
                kbuf[:, sl, ws + 1024:ws + 1152], start=True, stop=False))
            ew(k, "tensor", lambda e: e.matmul(
                ps_s[:, 0:512], k.ident, band[:, 0:512], start=False, stop=False))
            ew(k, "tensor", lambda e: e.matmul(
                ps_s[:, 1024:1152], k.ident, band[:, 1024:1152], start=False, stop=True))
            ew(k, "tensor", lambda e, ws=ws: e.matmul(
                ps_s[:, 0:512], k.ones[0:1, :], killv[0:1, ws:ws + 512],
                start=False, stop=True))
            if qt < 4:
                ew(k, "tensor", lambda e, ws=ws: e.matmul(
                    ps_s[:, 512:1024], k.ones[0:1, :], killv[0:1, ws + 512:ws + 1024],
                    start=False, stop=True))
            ps_marks[i] = milestone(k, "tensor", k.s_pe)
            if qt == 2 and 2 <= h + 1 < NH:
                wait(k, "sync", k.s_pe, av_marks[(h - 1) * 8 + 7])
                kv_load(h + 1)
        j = i - 1
        if 0 <= j < NIT:
            wait(k, "tensor", k.s_sc, pn_marks[j])
            if j >= 1:
                wait(k, "tensor", k.s_v, pte_marks[j - 1])
            for t9 in range(8):
                ew(k, "tensor", lambda e, j=j, t9=t9: e.transpose(
                    ps_pt[:, t9 * 128:(t9 + 1) * 128],
                    Pt[:, j % 2, t9 * 128:(t9 + 1) * 128], k.ident))
            ew(k, "tensor", lambda e, j=j: e.transpose(
                ps_pt2[:, 0:128], Pt[:, j % 2, 1024:1152], k.ident))
            pt_marks[j] = milestone(k, "tensor", k.s_pe)
        m = i - 2
        if m >= 0:
            hm, qtm = divmod(m, 8)
            slm = hm % 2
            wait(k, "tensor", k.s_v, pte_marks[m])
            if m >= 2:
                wait(k, "tensor", k.s_v, oe_marks[m - 2])
            for t9 in range(9):
                ew(k, "tensor", lambda e, m=m, t9=t9, slm=slm, qtm=qtm: e.matmul(
                    ps_ot[m % 2], vbuf[:, slm, qtm + t9, :],
                    ptile[:, m % 2, t9 * 128:(t9 + 1) * 128],
                    start=(t9 == 0), stop=(t9 == 8)))
            av_marks[m] = milestone(k, "tensor", k.s_pe)
        if i < NIT:
            c0s = (i % 2) * 8
            wait(k, "vector", k.s_pe, ps_marks[i])
            ew(k, "vector", lambda e, c0s=c0s: e.reduce_max(
                out=smal[:, c0s:c0s + 1], in_=ps_s[:, 0:1152], axis=AX.X),
               w=[tg(bufC_t, f"sm{i % 2}")])
            ew(k, "vector", lambda e, c0s=c0s: e.tensor_scalar_mul(
                out=smal[:, c0s + 1:c0s + 2], in0=smal[:, c0s:c0s + 1],
                scalar1=-ATT_SCALE),
               r=[tg(bufC_t, f"sm{i % 2}")], w=[tg(bufC_t, f"sm{i % 2}")])
            mx_marks[i] = milestone(k, "vector", k.s_v)
            wait(k, "scalar", k.s_v, mx_marks[i])
            ew(k, "scalar", lambda e, i=i, c0s=c0s: e.activation(
                out=Pt[:, i % 2, :], in_=ps_s[:, 0:1152], func=AF.Exp,
                scale=ATT_SCALE, bias=smal[:, c0s + 1:c0s + 2],
                accum_out=smal[:, c0s + 2:c0s + 3]),
               r=[tg(bufC_t, f"sm{i % 2}")],
               w=[tg(bufC_t, f"p{i % 2}"), tg(bufC_t, f"sm{i % 2}")])
            exp_marks[i] = milestone(k, "scalar", k.s_sc)
            wait(k, "vector", k.s_sc, exp_marks[i])

            def _rrin(e, c0s=c0s, nc_=k.nc):
                with nc_.allow_low_precision(reason="softmax denom reciprocal"):
                    return e.reciprocal(out=smal[:, c0s + 3:c0s + 4],
                                        in_=smal[:, c0s + 2:c0s + 3])
            ew(k, "vector", _rrin, r=[tg(bufC_t, f"sm{i % 2}")],
               w=[tg(bufC_t, f"sm{i % 2}")])
            mrin = milestone(k, "vector", k.s_v)
            wait(k, "scalar", k.s_v, mrin)
            ew(k, "scalar", lambda e, i=i, c0s=c0s: e.activation(
                out=Pt[:, i % 2, :], in_=Pt[:, i % 2, :], func=AF.Copy,
                scale=smal[:, c0s + 3:c0s + 4]),
               r=[tg(bufC_t, f"p{i % 2}"), tg(bufC_t, f"sm{i % 2}")],
               w=[tg(bufC_t, f"p{i % 2}")])
            pn_marks[i] = milestone(k, "scalar", k.s_sc)
        if 0 <= j < NIT:
            wait(k, "vector", k.s_pe, pt_marks[j])
            ew(k, "vector", lambda e, j=j: e.tensor_copy(
                out=ptile[:, j % 2, 0:1024], in_=ps_pt[:, :]),
               w=[tg(bufC_t, f"pt{j % 2}")])
            ew(k, "vector", lambda e, j=j: e.tensor_copy(
                out=ptile[:, j % 2, 1024:1152], in_=ps_pt2[:, 0:128]),
               w=[tg(bufC_t, f"pt{j % 2}")])
            pte_marks[j] = milestone(k, "vector", k.s_v)
        if m >= 0:
            hm, qtm = divmod(m, 8)
            wait(k, "vector", k.s_pe, av_marks[m])
            ew(k, "vector", lambda e, m=m, hm=hm, qtm=qtm: e.tensor_copy(
                out=xhat[:, hm, qtm * 128:(qtm + 1) * 128], in_=ps_ot[m % 2]),
               w=[tg(xhat, f"at{hm}.{qtm}")])
            oe_marks[m] = milestone(k, "vector", k.s_v)

    wait(k, "tensor", k.s_v)

    def evac_addres1(oc, si, c0, wd, ps):
        ew(k, "vector", lambda e, oc=oc, c0=c0, wd=wd, ps=ps: e.tensor_tensor(
            out=xres[:, oc, 1 + c0:1 + c0 + wd], in0=xres[:, oc, 1 + c0:1 + c0 + wd],
            in1=ps, op=ALU.add), r=[tg(xres, f"o{oc}.{si}")], w=[tg(xres, f"o{oc}.{si}")])

    emit_proj(k, 16, 16, lambda oc: wo_t[oc],
              lambda dc, c0, wd: xhat[:, dc, c0:c0 + wd],
              evac_addres1, "vector", wslab, ps_mm, 4)




def emit_ffn1(k, xres, xhat, bufA_t, bufB_t, bufC, bufC_t, ps_mm,
              w1_t, w3_t, W2_OFF, RANK_SZ):
    occ = k.cc_w_out_ap

    def w2h_slab_ap(dc):
        r, l = divmod(dc, 2)
        off = r * RANK_SZ + W2_OFF + l * 1048576
        return occ[off:off + 1048576].rearrange("(p m) -> p m", p=128)

    hA = bufA_t[:, 0:16384].rearrange("p (a b) -> p a b", b=512)
    hB = bufB_t[:, 0:16384].rearrange("p (a b) -> p a b", b=512)

    def hbuf(fc):
        return (hA if fc < 32 else hB)[:, fc % 32, :]

    def htag(fc):
        return tg(bufA_t if fc < 32 else bufB_t, fc % 32)

    gslab = bufC_t[:, 0:8192].rearrange("p (a b c) -> p a b c", a=4, c=128)
    sg = bufC_t[:, 10240:11264].rearrange("p (a b) -> p a b", a=2)
    pslab = bufC_t[:, 0:16384].rearrange("p (a b c) -> p a b c", b=64, c=128)
    wsl = (k.s_w, k.s_w2, k.s_w3, k.s_w4)

    wait(k, "scalar", k.s_v)
    wait(k, "scalar", k.s_pe)
    wait(k, "vector", k.s_pe)
    emit_ln(k, xres, xhat, ((1, 512), (513, 512)), bufC)

    for th in range(2):
        cx = 1 + th * 512
        wait(k, "gpsimd", k.s_pe)
        wait(k, "vector", k.s_pe)
        wait(k, "scalar", k.s_pe)
        wait(k, "tensor", k.s_v)
        wait(k, "tensor", k.s_sc)
        load_marks = {}

        def w_load(fc, which):
            slot = (2 * fc + which) % 4
            sem = wsl[slot]
            lm = sem.inc(16)
            src_ = (w1_t if which == 0 else w3_t)
            ew(k, "gpsimd", lambda e, fc=fc, src_=src_, slot=slot, h=sem.h: e.dma_start(
                out=gslab[:, slot, :, :].rearrange("p a b -> p (a b)"),
                in_=src_[fc]).then_inc(h, 16))
            load_marks[(fc, which)] = lm

        w_load(0, 0)
        w_load(0, 1)
        w_load(1, 0)
        w_load(1, 1)
        gmark = {}
        umark = {}
        hmark = {}
        smark = {}
        for fc in range(64):
            slot_g = (2 * fc) % 4
            slot_u = (2 * fc + 1) % 4
            wait(k, "tensor", wsl[slot_g], load_marks[(fc, 0)])
            if fc >= 2:
                wait(k, "tensor", k.s_sc, smark[fc - 2])
            for dc in range(NCD):
                ew(k, "tensor", lambda e, dc=dc, slot_g=slot_g, cx=cx: e.matmul(
                    ps_mm[:, slot_g, :], gslab[:, slot_g, dc, :],
                    xhat[:, dc, cx:cx + 512], start=(dc == 0), stop=(dc == NCD - 1)))
            gmark[fc] = milestone(k, "tensor", k.s_pe)
            wait(k, "tensor", wsl[slot_u], load_marks[(fc, 1)])
            if fc >= 2:
                wait(k, "tensor", k.s_v, hmark[fc - 2])
            for dc in range(NCD):
                ew(k, "tensor", lambda e, dc=dc, slot_u=slot_u, cx=cx: e.matmul(
                    ps_mm[:, slot_u, :], gslab[:, slot_u, dc, :],
                    xhat[:, dc, cx:cx + 512], start=(dc == 0), stop=(dc == NCD - 1)))
            umark[fc] = milestone(k, "tensor", k.s_pe)
            if fc + 2 < 64:
                wait(k, "gpsimd", k.s_pe, gmark[fc])
                w_load(fc + 2, 0)
                wait(k, "gpsimd", k.s_pe, umark[fc])
                w_load(fc + 2, 1)
            wait(k, "scalar", k.s_pe, gmark[fc])
            if fc >= 2:
                wait(k, "scalar", k.s_v, hmark[fc - 2])
            ew(k, "scalar", lambda e, fc=fc, slot_g=slot_g: e.activation(
                out=sg[:, fc % 2, :], in_=ps_mm[:, slot_g, :], func=AF.Silu),
               w=[f"bufC:sg{fc % 2}"])
            smark[fc] = milestone(k, "scalar", k.s_sc)
            wait(k, "vector", k.s_pe, umark[fc])
            wait(k, "vector", k.s_sc, smark[fc])
            ew(k, "vector", lambda e, fc=fc, slot_u=slot_u: e.tensor_tensor(
                out=hbuf(fc), in0=sg[:, fc % 2, :], in1=ps_mm[:, slot_u, :],
                op=ALU.mult), r=[f"bufC:sg{fc % 2}"], w=[htag(fc)])
            hmark[fc] = milestone(k, "vector", k.s_v)
        mhn = milestone(k, "vector", k.s_v)
        wait(k, "tensor", k.s_v, mhn)

        def evac_ffn1(oc, si, c0_, wd, ps, th=th, cx=cx):
            ew(k, "vector", lambda e, oc=oc, ps=ps, cx=cx: e.tensor_tensor(
                out=xres[:, oc, cx:cx + 512], in0=xres[:, oc, cx:cx + 512],
                in1=ps, op=ALU.add),
               r=[tg(xres, f"g{oc}.{th}")], w=[tg(xres, f"g{oc}.{th}")])

        emit_proj(k, 16, 64, w2h_slab_ap, lambda dc, c0_, wd: hbuf(dc),
                  evac_ffn1, "vector", pslab, ps_mm, 4, segs=((0, 512),),
                  w_queue="sync", w_sems=(k.s_k, k.s_k2), load_wait=(k.s_v, mhn))



def build(debug_stop=None):
    nc = bass.Bass("TRN2", target_bir_lowering=False)
    es = ExitStack()

    xT = nc.dram_tensor("xT", [NCD, 128, W], F32, kind="ExternalInput").ap()
    wr_t = nc.dram_tensor("wr_t", [16, 128, D], F32, kind="ExternalInput").ap()
    wk_t = nc.dram_tensor("wk_t", [16, 128, D], F32, kind="ExternalInput").ap()
    wv_t = nc.dram_tensor("wv_t", [16, 128, D], F32, kind="ExternalInput").ap()
    wo_t = nc.dram_tensor("wo_t", [16, 128, D], F32, kind="ExternalInput").ap()
    vecs = nc.dram_tensor("vecs", [128, 6, 16], F32, kind="ExternalInput").ap()
    cid = nc.dram_tensor("cid", [128, 3, 128], BF16, kind="ExternalInput").ap()
    outT = nc.dram_tensor("outT", [NCD, 128, T], F32, kind="ExternalOutput").ap()

    l1wq_t = nc.dram_tensor("l1wq_t", [16, 128, D], F32, kind="ExternalInput").ap()
    l1wk_t = nc.dram_tensor("l1wk_t", [16, 128, D], F32, kind="ExternalInput").ap()
    l1wv_t = nc.dram_tensor("l1wv_t", [16, 128, D], F32, kind="ExternalInput").ap()
    l1wo_t = nc.dram_tensor("l1wo_t", [16, 128, D], F32, kind="ExternalInput").ap()
    w1_t = nc.dram_tensor("w1_t", [64, 128, D], F32, kind="ExternalInput").ap()
    w3_t = nc.dram_tensor("w3_t", [64, 128, D], F32, kind="ExternalInput").ap()
    ropec_d = nc.dram_tensor("ropec_d", [128, 2, 1024], BF16, kind="ExternalInput").ap()
    band_d = nc.dram_tensor("band_d", [128, 1152], BF16, kind="ExternalInput").ap()
    killv_d = nc.dram_tensor("killv_d", [1, 2048], BF16, kind="ExternalInput").ap()
    wgu_slab = nc.dram_tensor("wgu_slab", [2048, 2048], F32, kind="ExternalInput").ap()
    wproj_slab = nc.dram_tensor("wproj_slab", [256, FFH], F32, kind="ExternalInput").ap()
    w2_slab = nc.dram_tensor("w2_slab", [256, FFH], F32, kind="ExternalInput").ap()

    cc_s_in = nc.dram_tensor("cc_s_in", [128, 16], F32)
    cc_s_out = nc.dram_tensor("cc_s_out", [256, 16], F32)
    RANK_SZ = 16 * 262144 + 4 * 1048576
    WP_OFF = 16 * 262144
    W2_OFF = WP_OFF + 2 * 1048576
    cc_w_in = nc.dram_tensor("cc_w_in", [RANK_SZ], BF16)
    cc_w_out = nc.dram_tensor("cc_w_out", [8 * RANK_SZ], BF16)
    cc_kv_in = nc.dram_tensor("cc_kv_in", [32 * 131072], BF16)
    cc_kv_out = nc.dram_tensor("cc_kv_out", [64 * 131072], BF16)
    k_rank_sz = RANK_SZ

    k = K()
    k.nc = nc
    k.em = Em()

    for s in ("s_ld", "s_w", "s_w2", "s_w3", "s_w4", "s_k", "s_k2", "s_k3", "s_k4", "s_pe", "s_v", "s_sc", "s_g", "s_cc", "s_st"):
        setattr(k, s, Sem(es.enter_context(nc.semaphore(s))))

    onesid = es.enter_context(nc.sbuf_tensor("onesid", [128, 3, 128], BF16))[:]
    k.ones = onesid[:, 0, :]
    k.ident = onesid[:, 1, :]
    k.swap = onesid[:, 2, :]
    killv = es.enter_context(nc.sbuf_tensor("killv", [1, 2048], BF16))[:]
    vtile = es.enter_context(nc.sbuf_tensor("vtile", [128, 10, 16], F32))[:]
    stile = es.enter_context(nc.sbuf_tensor("stile", [128, 3, 16], F32))[:]

    k.s_ld.inc(16)
    ew(k, "sync", lambda e, h=k.s_ld.h: e.dma_start(out=onesid, in_=cid).then_inc(h, 16))
    k.s_ld.inc(16)
    ew(k, "sync", lambda e, h=k.s_ld.h: e.dma_start(
        out=vtile[:, 0:6, :], in_=vecs).then_inc(h, 16))
    k.s_ld.inc(16)
    ew(k, "sync", lambda e, h=k.s_ld.h: e.dma_start(out=killv[:], in_=killv_d).then_inc(h, 16))
    m_in = k.s_ld.n
    for e in ("scalar", "vector", "tensor"):
        wait(k, e, k.s_ld, m_in)

    for i in range(3):
        ew(k, "vector", lambda e, i=i: e.tensor_scalar(
            out=vtile[:, 6 + i, :], in0=vtile[:, i, :], scalar1=-1.0, scalar2=1.0,
            op0=ALU.mult, op1=ALU.add), r=[vtile], w=[vtile])
    ew(k, "scalar", lambda e: e.activation(out=vtile[:, 9, :], in_=vtile[:, 3, :],
                                           func=AF.Exp), r=[vtile], w=[vtile])
    ew(k, "scalar", lambda e: e.activation(out=vtile[:, 9, :], in_=vtile[:, 9, :],
                                           func=AF.Exp, scale=-1.0), r=[vtile], w=[vtile])
    m_vec_s = milestone(k, "scalar", k.s_sc)
    milestone(k, "vector", k.s_v)

    k.cc_w_out_ap = cc_w_out.ap()
    emit_phase0(k, wgu_slab, wproj_slab, w2_slab, cc_w_in, WP_OFF, W2_OFF)
    m_p0 = k.s_g.n

    xres = es.enter_context(nc.sbuf_tensor("xres", [128, NCD, W], F32))[:]
    xhat = es.enter_context(nc.sbuf_tensor("xhat", [128, NCD, W], BF16))[:]
    bufA_t = es.enter_context(nc.sbuf_tensor("bufA", [128, NCD * W], BF16))[:]
    bufA = bufA_t.rearrange("p (a b) -> p a b", b=W)
    bufB_t = es.enter_context(nc.sbuf_tensor("bufB", [128, NCD * W], BF16))[:]
    bufB = bufB_t.rearrange("p (a b) -> p a b", b=W)
    bufC_t = es.enter_context(nc.sbuf_tensor("bufC", [128, NCD * W], BF16))[:]
    bufC = bufC_t.rearrange("p (a b) -> p a b", b=W)
    k.stat_scr = bufA_t[:, 0:3072].bitcast(F32).rearrange("p (a b) -> p a b", a=3)
    k.rb = bufA_t[:, 3072:5120].rearrange("p (a b) -> p a b", a=2)
    k.mrb = bufA_t[:, 5120:7168].rearrange("p (a b) -> p a b", a=2)
    wslab = es.enter_context(nc.sbuf_tensor("wslab", [128, 2, 16, 128], BF16))[:]

    k.ps_ln = es.enter_context(nc.psum_tensor("ps_ln", [128, 2, 512], F32))[:]
    ps_mm = es.enter_context(nc.psum_tensor("ps_mm", [128, 4, 512], F32))[:]
    ps_r = es.enter_context(nc.psum_tensor("ps_r", [128, 1, 512], F32))[:]
    k.ps_r = ps_r

    wait(k, "sync", k.s_g, m_p0)
    for c in range(NCD):
        k.s_ld.inc(16)
        ew(k, "sync", lambda e, c=c, h=k.s_ld.h: e.dma_start(
            out=xres[:, c, :], in_=xT[c]).then_inc(h, 16))
    m_in = k.s_ld.n
    for e in ("scalar", "vector", "tensor"):
        wait(k, e, k.s_ld, m_in)

    emit_ln(k, xres, xhat, ((0, 512), (512, 512), (1024, 1)), bufC)
    wait(k, "vector", k.s_sc, m_vec_s)

    def emit_mix(tm_i, om_i, dstbuf):
        for c in range(NCD):
            ew(k, "vector", lambda e, c=c, om_i=om_i, dstbuf=dstbuf: e.tensor_scalar_mul(
                out=dstbuf[:, c, 0:T], in0=xhat[:, c, 0:T],
                scalar1=vtile[:, om_i, c:c + 1]), r=[xhat, vtile], w=[tg(dstbuf, c)])
        for c in range(NCD):
            ew(k, "vector", lambda e, c=c, tm_i=tm_i, dstbuf=dstbuf: e.scalar_tensor_tensor(
                out=dstbuf[:, c, 0:T], in0=xhat[:, c, 1:W], scalar=vtile[:, tm_i, c:c + 1],
                in1=dstbuf[:, c, 0:T], op0=ALU.mult, op1=ALU.add),
               r=[xhat, vtile, tg(dstbuf, c)], w=[tg(dstbuf, c)])
        return milestone(k, "vector", k.s_v)

    def proj_rhs(buf):
        return lambda dc, c0, wd: buf[:, dc, c0:c0 + wd]

    def evac_copy_to(buf, act=None):
        def f(oc, si, c0, wd, ps):
            if act is None:
                ew(k, "vector", lambda e, oc=oc, c0=c0, wd=wd, ps=ps: e.tensor_copy(
                    out=buf[:, oc, c0:c0 + wd], in_=ps), w=[tg(buf, oc)])
            else:
                ew(k, "scalar", lambda e, oc=oc, c0=c0, wd=wd, ps=ps: e.activation(
                    out=buf[:, oc, c0:c0 + wd], in_=ps, func=act), w=[tg(buf, oc)])
        return f

    mk = emit_mix(0, 6, bufB)
    mvx = emit_mix(1, 7, bufA)
    wait(k, "tensor", k.s_v, mk)
    mm_k = emit_proj(k, 16, 16, lambda oc: wk_t[oc], proj_rhs(bufB),
                     evac_copy_to(bufC), "vector", wslab, ps_mm, 4)
    wait(k, "tensor", k.s_v, mvx)
    mm_v = emit_proj(k, 16, 16, lambda oc: wv_t[oc], proj_rhs(bufA),
                     evac_copy_to(bufB), "vector", wslab, ps_mm, 4)
    wait(k, "vector", k.s_pe, mm_v[15])
    mr = emit_mix(2, 8, bufA)
    wait(k, "tensor", k.s_v, mr)
    emit_proj(k, 16, 16, lambda oc: wr_t[oc], proj_rhs(bufA),
              evac_copy_to(xhat, act=AF.Sigmoid), "scalar", wslab, ps_mm, 4)
    m_rdone = k.s_sc.n

    for c in range(NCD):
        ew(k, "vector", lambda e, c=c: e.tensor_tensor(
            out=bufC[:, c, 0:T], in0=bufC[:, c, 0:T], in1=bufB[:, c, 0:T], op=ALU.mult),
           r=[tg(bufB, c), tg(bufC, c)], w=[tg(bufC, c)])
    wait(k, "vector", k.s_pe)
    for c in range(NCD):
        ew(k, "vector", lambda e, c=c: e.tensor_tensor_scan(
            out=bufB[:, c, 1:W], data0=vtile[:, 9, c:c + 1].broadcast_to([128, T]),
            data1=bufC[:, c, 0:T], initial=0.0, op0=ALU.mult, op1=ALU.add),
           r=[tg(bufC, c), vtile], w=[tg(bufB, c)])
    ew(k, "vector", lambda e: e.tensor_copy(
        out=stile[:, 0, :], in_=bufB[:, :, W - 1:W].rearrange("p c o -> p (c o)")),
       r=[bufB], w=[stile])
    m_scanA = milestone(k, "vector", k.s_v)
    wait(k, "gpsimd", k.s_v, m_scanA)
    k.s_g.inc(16)
    ew(k, "gpsimd", lambda e, h=k.s_g.h: e.dma_start(
        out=cc_s_in[:, :], in_=stile[:, 0, :]).then_inc(h, 16))
    wait(k, "gpsimd", k.s_g)
    m_cc = k.s_cc.inc(1)
    ew(k, "gpsimd", lambda e, h=k.s_cc.h: e.collective_compute(
        "AllGather", ALU.bypass, replica_groups=PAIRS,
        ins=[cc_s_in.ap().opt()], outs=[cc_s_out.ap().opt()]).then_inc(h, 1))
    wait(k, "gpsimd", k.s_cc, m_cc)
    m_srecv = k.s_g.inc(16)
    ew(k, "gpsimd", lambda e, h=k.s_g.h: e.dma_start(
        out=stile[:, 1, :], in_=cc_s_out.ap()[0:128, :]).then_inc(h, 16))
    wait(k, "vector", k.s_g, m_srecv)
    ew(k, "vector", lambda e: e.tensor_tensor(
        out=stile[:, 2, :], in0=stile[:, 1, :], in1=vtile[:, 5, :], op=ALU.mult),
       r=[stile, vtile], w=[stile])
    for c in range(NCD):
        ew(k, "vector", lambda e, c=c: e.tensor_tensor_scan(
            out=bufB[:, c, 1:W], data0=vtile[:, 9, c:c + 1].broadcast_to([128, T]),
            data1=bufC[:, c, 0:T], initial=stile[:, 2, c:c + 1], op0=ALU.mult, op1=ALU.add),
           r=[tg(bufC, c), vtile, stile], w=[tg(bufB, c)])
    ew(k, "vector", lambda e: e.tensor_copy(
        out=bufB[:, :, 0:1].rearrange("p c o -> p (c o)"), in_=stile[:, 2, :]),
       r=[stile], w=[bufB])
    wait(k, "vector", k.s_sc, m_rdone)
    wait(k, "vector", k.s_pe)
    for c in range(NCD):
        ew(k, "vector", lambda e, c=c: e.scalar_tensor_tensor(
            out=bufA[:, c, 0:T], in0=bufC[:, c, 0:T], scalar=vtile[:, 4, c:c + 1],
            in1=bufB[:, c, 0:T], op0=ALU.mult, op1=ALU.add),
           r=[tg(bufC, c), tg(bufB, c), vtile], w=[tg(bufA, c)])
    for c in range(NCD):
        ew(k, "vector", lambda e, c=c: e.tensor_tensor(
            out=bufC[:, c, 0:T], in0=bufA[:, c, 0:T], in1=xhat[:, c, 0:T], op=ALU.mult),
           r=[tg(bufA, c), xhat], w=[tg(bufC, c)])
    m_wkv = milestone(k, "vector", k.s_v)
    wait(k, "tensor", k.s_v, m_wkv)

    def evac_addres(oc, si, c0, wd, ps):
        ew(k, "vector", lambda e, oc=oc, c0=c0, wd=wd, ps=ps: e.tensor_tensor(
            out=xres[:, oc, 1 + c0:1 + c0 + wd], in0=xres[:, oc, 1 + c0:1 + c0 + wd],
            in1=ps, op=ALU.add), r=[tg(xres, f"{oc}.{si}")], w=[tg(xres, f"{oc}.{si}")])

    emit_proj(k, 16, 16, lambda oc: wo_t[oc], proj_rhs(bufC),
              evac_addres, "vector", wslab, ps_mm, 4)

    emit_ffn0(k, xres, xhat, bufA_t, bufB_t, bufC, bufC_t, wslab, ps_mm, ps_r,
              WP_OFF, RANK_SZ)

    emit_l1attn(k, xres, xhat, bufA, bufB, bufC, bufC_t, wslab, ps_mm,
                l1wq_t, l1wk_t, l1wv_t, l1wo_t, ropec_d, band_d, killv,
                cc_kv_in, cc_kv_out)

    emit_ffn1(k, xres, xhat, bufA_t, bufB_t, bufC, bufC_t, ps_mm,
              w1_t, w3_t, W2_OFF, RANK_SZ)

    m_done_v = milestone(k, "vector", k.s_v)
    m_done_s = milestone(k, "scalar", k.s_sc)
    wait(k, "sync", k.s_v, m_done_v)
    wait(k, "sync", k.s_sc, m_done_s)
    for c in range(NCD):
        k.s_st.inc(16)
        ew(k, "sync", lambda e, c=c, h=k.s_st.h: e.dma_start(
            out=outT[c], in_=xres[:, c, 1:W]).then_inc(h, 16))
    wait(k, "sync", k.s_st)

    def _replay(e, lst):
        for f, posts in lst:
            r = f(e)
            for s in posts:
                r.then_inc(s, 1)

    with nc.Block() as block:
        @block.sync
        def _(e):
            _replay(e, k.em.q["sync"])

        @block.scalar
        def _(e):
            _replay(e, k.em.q["scalar"])

        @block.vector
        def _(e):
            _replay(e, k.em.q["vector"])

        @block.tensor
        def _(e):
            _replay(e, k.em.q["tensor"])

        @block.gpsimd
        def _(e):
            _replay(e, k.em.q["gpsimd"])

    es.close()
    return nc



def _tile_lhsT(Wm, contract_axis):
    if contract_axis == 1:
        O, Kd = Wm.shape
        Oc, Kc = O // 128, Kd // 128
        t = Wm.reshape(Oc, 128, Kc, 128).transpose(0, 3, 2, 1)
    else:
        Kd, O = Wm.shape
        Kc, Oc = Kd // 128, O // 128
        t = Wm.reshape(Kc, 128, Oc, 128).transpose(2, 1, 0, 3)
    return np.ascontiguousarray(t.reshape(Oc, 128, Kc * 128))


def kernel(**inputs):
    L0x = np.asarray(inputs["L0_attn_hidden_states"])

    wr_t = _tile_lhsT(np.asarray(inputs["L0_attn_Wr"]), 1)
    wk_t = _tile_lhsT(np.asarray(inputs["L0_attn_Wk"]), 1)
    wv_t = _tile_lhsT(np.asarray(inputs["L0_attn_Wv"]), 1)
    wo_t = _tile_lhsT(np.asarray(inputs["L0_attn_Wo"]), 1)

    def vec_tile(v):
        return np.ascontiguousarray(np.asarray(v).reshape(-1).reshape(16, 128).T)

    tmk = vec_tile(inputs["L0_attn_time_mix_k"])
    tmv = vec_tile(inputs["L0_attn_time_mix_v"])
    tmr = vec_tile(inputs["L0_attn_time_mix_r"])
    td = vec_tile(inputs["L0_attn_time_decay"])
    tf = vec_tile(inputs["L0_attn_time_first"])

    w1_t = _tile_lhsT(np.asarray(inputs["L1_ffn_W1"]), 1)
    w3_t = _tile_lhsT(np.asarray(inputs["L1_ffn_W3"]), 1)
    l1wq_t = _tile_lhsT(np.asarray(inputs["L1_attn_wq"]), 0)
    l1wk_t = _tile_lhsT(np.asarray(inputs["L1_attn_wk"]), 0)
    l1wv_t = _tile_lhsT(np.asarray(inputs["L1_attn_wv"]), 0)
    l1wo_t = _tile_lhsT(np.asarray(inputs["L1_attn_wo"]), 0)

    wgu = np.asarray(inputs["L0_ffn_w_gate_up"])
    wproj = np.asarray(inputs["L0_ffn_w_proj"])
    W2 = np.asarray(inputs["L1_ffn_W2"])

    cid = np.zeros((128, 3, 128), dtype=ml_dtypes.bfloat16)
    cid[:, 0, :] = 1.0
    cid[:, 1, :] = np.eye(128)
    sw = np.zeros((128, 128), dtype=np.float32)
    for ii in range(128):
        sw[ii, ii ^ 64] = 1.0
    cid[:, 2, :] = sw

    ii = np.arange(128)[:, None]
    jj = np.arange(1152)[None, :]
    band = np.where((jj >= ii + 1) & (jj <= ii + 1024), 0.0, NEG).astype(ml_dtypes.bfloat16)
    inv_freq = (1.0 / 10000.0 ** (np.arange(0, 128, 2, dtype=np.float32) / 128.0))

    in_maps = []
    for c in range(8):
        b, p = c // 2, c % 2
        t0 = p * T
        xTc = np.zeros((NCD, 128, W), dtype=np.float32)
        xTc[:, :, 1:] = L0x[b, t0:t0 + T, :].T.reshape(NCD, 128, T)
        if p == 1:
            xTc[:, :, 0] = L0x[b, t0 - 1, :].reshape(NCD, 128)
        vv = np.stack([tmk, tmv, tmr, td, tf,
                       np.full((128, 16), 1.0 if p == 1 else 0.0, np.float32)], axis=1)
        pos = np.arange(t0, t0 + T, dtype=np.float32)
        fr = pos[None, :] * inv_freq[:, None]
        cos2 = np.concatenate([np.cos(fr), np.cos(fr)], axis=0)
        sin2 = np.concatenate([-np.sin(fr), np.sin(fr)], axis=0)
        ropec_h = np.stack([cos2, sin2], axis=1).astype(ml_dtypes.bfloat16)
        killv_h = np.zeros((1, 2048), dtype=ml_dtypes.bfloat16)
        if p == 0:
            killv_h[0, 0:1024] = NEG
        in_maps.append({
            "xT": np.ascontiguousarray(xTc),
            "wr_t": wr_t, "wk_t": wk_t, "wv_t": wv_t, "wo_t": wo_t,
            "vecs": np.ascontiguousarray(vv.astype(np.float32)),
            "cid": cid,
            "wgu_slab": np.ascontiguousarray(wgu[c * 2048:(c + 1) * 2048, :]),
            "wproj_slab": np.ascontiguousarray(wproj[c * 256:(c + 1) * 256, :]),
            "w2_slab": np.ascontiguousarray(W2[c * 256:(c + 1) * 256, :]),
            "l1wq_t": l1wq_t, "l1wk_t": l1wk_t, "l1wv_t": l1wv_t, "l1wo_t": l1wo_t,
            "w1_t": w1_t, "w3_t": w3_t,
            "ropec_d": np.ascontiguousarray(ropec_h),
            "band_d": band, "killv_d": killv_h,
        })

    nc = build()
    res = run_bass_kernel_spmd(nc, in_maps, core_ids=list(range(8)))

    out = np.empty((4, 2048, 2048), dtype=np.float32)
    for c in range(8):
        b, p = c // 2, c % 2
        out[b, p * T:(p + 1) * T, :] = res.results[c]["outT"].reshape(D, T).T
    return out
